# version 10
# baseline (speedup 1.0000x reference)
import contextlib
import numpy as np
import ml_dtypes
import concourse.bass as bass
import concourse.mybir as mybir
from concourse.bass_utils import run_bass_kernel_spmd

F32 = mybir.dt.float32
BF16 = mybir.dt.bfloat16
AF = mybir.ActivationFunctionType
ALU = mybir.AluOpType
AX = mybir.AxisListType

NCORES = 8
S_TOT = 16384
TPC = S_TOT // NCORES
D = 1024
H_FFN = 4096
DI = 2048
NST = 16
DTR = 64
DCONV = 4
NEXP = 4
TC = 512
NTC = TPC // TC
EPS = 1e-5
C_RG = 8.0

ENGS = ("sync", "tensor", "scalar", "vector", "gpsimd")

def _mk_sems(s):
    for nm in ("qc", "qx", "qw0", "qw1", "qw2", "qst", "cpe", "cs", "cv", "cc",
               "qz", "qs0", "qs1"):
        s.sem(nm)




class Sched:

    CPOOL_CAP = 24

    def __init__(self, nc):
        self.nc = nc
        self.ops = {e: [] for e in ENGS}
        self.es = contextlib.ExitStack()
        self.es_root = self.es
        self._sems = {}
        self.counts = {}
        self._consts = {}
        self._cpool = self.es_root.enter_context(
            self.nc.sbuf_tensor("sched_cpool", [128, self.CPOOL_CAP], F32))
        self.sem("cinit")
        self._uid = 0
        self._nbar = 0

    def scope(self):
        sched = self

        class _Scope:
            def __enter__(self_):
                self_.saved = sched.es
                sched.es = contextlib.ExitStack()
                sched.es.__enter__()
                return sched

            def __exit__(self_, *exc):
                sched.es.__exit__(*exc)
                sched.es = self_.saved
                return False
        return _Scope()

    def barrier(self):
        if "cbar" not in self._sems:
            self.sem("cbar")
        for nm in ("qc", "qx", "qw0", "qw1", "qw2", "qz"):
            if nm in self.counts:
                self.wait("sync", nm, self.counts[nm])
        for nm in ("qst", "cc", "qs0", "qs1", "qz"):
            if nm in self.counts:
                self.wait("gpsimd", nm, self.counts[nm])
        h = self._sems["cbar"]
        for eng in ENGS:
            if eng in ("tensor", "vector", "scalar", "gpsimd"):
                self.ops[eng].append(lambda e: e.drain())
            self.ops[eng].append(lambda e, h=h: e.sem_inc(h, 1))
        self._nbar += 1
        tgt = 5 * self._nbar
        self.counts["cbar"] = tgt
        for eng in ENGS:
            self.ops[eng].append(lambda e, h=h, t=tgt: e.wait_ge(h, t))

    def const(self, val):
        val = float(val)
        if val in self._consts:
            return self._consts[val]
        idx = len(self._consts)
        assert idx < self.CPOOL_CAP
        ap = self._cpool[:, idx:idx + 1]
        self._consts[val] = ap
        return ap

    def sem(self, name):
        self._sems[name] = self.es_root.enter_context(self.nc.semaphore(name))
        self.counts[name] = 0
        return name

    def sbuf(self, name, shape, dtype=F32):
        self._uid += 1
        return self.es.enter_context(
            self.nc.sbuf_tensor("%s__%d" % (name, self._uid), shape, dtype))

    def psum(self, name, shape, dtype=F32):
        self._uid += 1
        return self.es.enter_context(
            self.nc.psum_tensor("%s__%d" % (name, self._uid), shape, dtype))

    def wait(self, eng, sname, cnt=None):
        if cnt is None:
            cnt = self.counts[sname]
        if cnt <= 0:
            return
        h = self._sems[sname]
        self.ops[eng].append(lambda e, h=h, c=cnt: e.wait_ge(h, c))

    SELF_SEM = {"vector": "cv", "scalar": "cs"}

    def do(self, eng, fn, inc=None, amt=1):
        own = self.SELF_SEM.get(eng)
        if own is not None:
            if inc is None:
                inc = own
            if inc == own and self.counts[own] > 0:
                self.wait(eng, own, self.counts[own])
        if inc is None:
            self.ops[eng].append(fn)
            return None
        h = self._sems[inc]
        self.ops[eng].append(lambda e, fn=fn, h=h, a=amt: fn(e).then_inc(h, a))
        self.counts[inc] += amt
        return self.counts[inc]

    def dma(self, eng, out, in_, sem, waits=()):
        for w in waits:
            if isinstance(w, tuple):
                self.wait(eng, w[0], w[1])
            else:
                self.wait(eng, w)
        return self.do(eng, lambda e, o=out, i=in_: e.dma_start(out=o, in_=i),
                       inc=sem, amt=16)

    def n(self, sname):
        return self.counts[sname]

    def finalize(self):
        if self._consts:
            h = self._sems["cinit"]
            pre = []
            items = list(self._consts.items())
            for i, (val, ap) in enumerate(items):
                if i == len(items) - 1:
                    pre.append(lambda e, a=ap, v=val, h=h:
                               e.memset(a, v).then_inc(h, 1))
                else:
                    pre.append(lambda e, a=ap, v=val: e.memset(a, v))
            self.ops["vector"] = pre + self.ops["vector"]
            for eng in ENGS:
                if eng == "vector" or not self.ops[eng]:
                    continue
                self.ops[eng] = [lambda e, h=h: e.wait_ge(h, 1)] + self.ops[eng]
        block = self.es_root.enter_context(self.nc.Block())
        for eng in ENGS:
            if not self.ops[eng]:
                continue
            section = getattr(block, eng)
            ops = self.ops[eng]

            def body(e, ops=ops):
                for f in ops:
                    f(e)
            section(body)


def fm(ap2d):
    return ap2d.rearrange("(kt p) t -> p kt t", p=128)


def fl(t):
    return t.ap()[:, :, :].rearrange("q a b -> q (a b)")


def emit_ln(s, nf, X_c, sqxn, rows, ones_col, ones_row,
            ps_stat, ps_stat2, ps_bm, ps_br, deps, out=None):
    nc = s.nc
    if out is None:
        out = sqxn
    (x_sem, x_cnt) = deps["x_ready"]
    s.wait("scalar", x_sem, x_cnt)
    for w in deps.get("sq_war", ()):
        s.wait("scalar", *w)
    for f in range(nf):
        cs_sq = s.do("scalar", lambda e, f=f: e.square(sqxn[:, f, :], X_c[:, f, :]),
                     inc="cs")
    s.wait("tensor", x_sem, x_cnt)
    s.wait("tensor", "qc", s.n("qc") if "qc" in s.counts else 0)
    for w in deps.get("stat_war", ()):
        s.wait("tensor", *w)
    for f in range(nf):
        s.do("tensor", lambda e, f=f: e.matmul(
            ps_stat[:1, :], ones_col[:, :1], X_c[:, f, :],
            start=(f == 0), stop=(f == nf - 1)),
            inc=("cpe" if f == nf - 1 else None))
    cpe_sum = s.n("cpe")
    s.wait("tensor", "cs", cs_sq)
    for f in range(nf):
        s.do("tensor", lambda e, f=f: e.matmul(
            ps_stat2[:1, :], ones_col[:, :1], sqxn[:, f, :],
            start=(f == 0), stop=(f == nf - 1)),
            inc=("cpe" if f == nf - 1 else None))
    cpe_sumsq = s.n("cpe")
    mrow, vrow, trow, rrow = rows["m"], rows["v"], rows["t"], rows["r"]
    s.wait("vector", "cpe", cpe_sum)
    cv0 = s.do("vector", lambda e: e.tensor_scalar(
        mrow[:1, :], ps_stat[:1, :], 1.0 / (nf * 128), None, ALU.mult), inc="cv")
    s.wait("vector", "cpe", cpe_sumsq)
    s.do("vector", lambda e: e.tensor_scalar(
        trow[:1, :], ps_stat2[:1, :], 1.0 / (nf * 128), None, ALU.mult), inc="cv")
    s.do("vector", lambda e: e.tensor_mul(vrow[:1, :], mrow[:1, :], mrow[:1, :]), inc="cv")
    cv_var = s.do("vector", lambda e: e.tensor_sub(vrow[:1, :], trow[:1, :], vrow[:1, :]),
                  inc="cv")
    s.wait("scalar", "cv", cv_var)
    c_eps = s.const(EPS)[:1, :]
    c_nh = s.const(-0.5)[:1, :]
    s.do("scalar", lambda e: e.activation(trow[:1, :], vrow[:1, :], AF.Ln, bias=c_eps),
         inc="cs")
    cs_rstd = s.do("scalar", lambda e: e.activation(rrow[:1, :], trow[:1, :], AF.Exp,
                                                    scale=c_nh), inc="cs")
    s.wait("tensor", "cv", cv0)
    for w in deps.get("bcast_war", ()):
        s.wait("tensor", *w)
    s.do("tensor", lambda e: e.matmul(ps_bm[:, :], ones_row[:1, :], mrow[:1, :],
                                      start=True, stop=True))
    s.wait("tensor", "cs", cs_rstd)
    cpe_b = s.do("tensor", lambda e: e.matmul(ps_br[:, :], ones_row[:1, :], rrow[:1, :],
                                              start=True, stop=True), inc="cpe")
    s.wait("vector", "cpe", cpe_b)
    for w in deps.get("out_war", ()):
        s.wait("vector", *w)
    for f in range(nf):
        s.do("vector", lambda e, f=f: e.tensor_sub(out[:, f, :], X_c[:, f, :],
                                                   ps_bm[:, :]), inc="cv")
        s.do("vector", lambda e, f=f: e.tensor_mul(out[:, f, :], out[:, f, :],
                                                   ps_br[:, :]), inc="cv")
    return s.n("cv")


def emit_ffn0(s, io):
    x_in = io["x_in"]; y_out = io["y_out"]
    bw1 = io["bw1"]; bw2 = io["bw2"]
    a1c_d = io["a1c_d"]; a2c_d = io["a2c_d"]
    b1c_d = io["b1c_d"]; b2c_d = io["b2c_d"]
    oc_d = io["oc_d"]; or_d = io["or_d"]
    NF = D // 128
    NM1 = H_FFN // 128
    NM2 = D // 128
    with s.scope():
        X = s.sbuf("X", [128, NF, TPC])
        sq = s.sbuf("sq", [128, NF, TC])
        xnb = s.sbuf("xnb", [128, NF, TPC], BF16)
        h = s.sbuf("h", [128, NM1, TC], BF16)
        w1b = [s.sbuf("w1b0", [128, NF, 128], BF16),
               s.sbuf("w1b1", [128, NF, 128], BF16)]
        w2b = [s.sbuf("w2b0", [128, NM1, 128], BF16),
               s.sbuf("w2b1", [128, NM1, 128], BF16)]
        tb = [s.sbuf("tb0", [128, TC]), s.sbuf("tb1", [128, TC])]
        b1c = s.sbuf("sb_b1c", [128, NM1])
        b2h = s.sbuf("sb_b2h", [128, NM2])
        a1c = s.sbuf("sb_a1c", [128, NM1])
        a2c = s.sbuf("sb_a2c", [128, NM2])
        oc = s.sbuf("oc", [128, 1])
        orow = s.sbuf("orow", [1, 128])
        rows = {k: s.sbuf("row_" + k, [1, TC]) for k in "mvtr"}
        ps_stat = s.psum("ps_stat", [1, TC])
        ps_stat2 = s.psum("ps_stat2", [1, TC])
        ps_bm = s.psum("ps_bm", [128, TC])
        ps_br = s.psum("ps_br", [128, TC])
        ps_mm = [s.psum("ps_mm0", [128, TC]), s.psum("ps_mm1", [128, TC])]

        s.dma("sync", b1c[:, :], b1c_d[:, :], "qc")
        s.dma("sync", b2h[:, :], b2c_d[:, :], "qc")
        s.dma("sync", a1c[:, :], a1c_d[:, :], "qc")
        s.dma("sync", a2c[:, :], a2c_d[:, :], "qc")
        s.dma("sync", oc[:, :], oc_d[:, :], "qc")
        s.dma("sync", orow[:1, :], or_d[:1, :], "qc")
        qld_const = s.n("qc")
        s.wait("vector", "qc", qld_const)
        cv_b2 = s.do("vector", lambda e: e.tensor_scalar(b2h[:, :], b2h[:, :], 0.5,
                                                         None, ALU.mult))
        x_cnt = s.dma("sync", X[:, :, :], fm(x_in)[:, :, :], "qx")
        cv_xns = []
        for tc_i in range(NTC):
            xsl = slice(tc_i * TC, (tc_i + 1) * TC)
            deps = {"x_ready": ("qx", x_cnt),
                    "sq_war": [("cpe", s.n("cpe"))],
                    "stat_war": [("cv", s.n("cv"))],
                    "bcast_war": [("cv", s.n("cv"))]}
            cv_xns.append(emit_ln(s, NF, X[:, :, xsl], sq, rows, oc, orow,
                                  ps_stat, ps_stat2, ps_bm, ps_br, deps,
                                  out=xnb[:, :, xsl]))
        last_w1 = {0: 0, 1: 0}
        last_w2 = {0: 0, 1: 0}
        last_evac1 = {0: 0, 1: 0}
        last_evac2 = {0: 0, 1: 0}
        last_h_read = 0
        for tc_i in range(NTC):
            xsl = slice(tc_i * TC, (tc_i + 1) * TC)
            for mc in range(NM1):
                p = mc % 2
                w_cnt = s.dma("sync", fl(w1b[p]),
                              bw1[mc, :, :], "qw%d" % p,
                              waits=[("cpe", max(last_w1[p], last_w2[p]))])
                s.wait("tensor", "qw%d" % p, w_cnt)
                if mc == 0:
                    s.wait("tensor", "cv", cv_xns[tc_i])
                s.wait("tensor", "cs", last_evac1[p])
                s.wait("tensor", "cv", last_evac2[p])
                for kt in range(NF):
                    s.do("tensor", lambda e, p=p, kt=kt, xsl=xsl: e.matmul(
                        ps_mm[p][:, :], w1b[p][:, kt, :], xnb[:, kt, xsl],
                        start=(kt == 0), stop=(kt == NF - 1)),
                        inc=("cpe" if kt == NF - 1 else None))
                cpe_h = s.n("cpe")
                last_w1[p] = cpe_h
                s.wait("scalar", "cpe", cpe_h)
                if tc_i > 0 and mc == 0:
                    s.wait("scalar", "cpe", last_h_read)
                last_evac1[p] = s.do("scalar", lambda e, p=p, mc=mc: e.activation(
                    h[:, mc, :], ps_mm[p][:, :], AF.Gelu,
                    bias=b1c[:, mc:mc + 1], scale=a1c[:, mc:mc + 1]))
            cs_h_all = s.n("cs")
            for mc2 in range(NM2):
                p = mc2 % 2
                w_cnt = s.dma("sync", fl(w2b[p]),
                              bw2[mc2, :, :], "qw%d" % p,
                              waits=[("cpe", max(last_w2[p], last_w1[p]))])
                s.wait("tensor", "qw%d" % p, w_cnt)
                s.wait("tensor", "cs", cs_h_all)
                s.wait("tensor", "cv", last_evac2[p])
                for kt in range(NM1):
                    s.do("tensor", lambda e, p=p, kt=kt: e.matmul(
                        ps_mm[p][:, :], w2b[p][:, kt, :], h[:, kt, :],
                        start=(kt == 0), stop=(kt == NM1 - 1)),
                        inc=("cpe" if kt == NM1 - 1 else None))
                cpe_m2 = s.n("cpe")
                last_w2[p] = cpe_m2
                if mc2 == NM2 - 1:
                    last_h_read = cpe_m2
                s.wait("vector", "cpe", cpe_m2)
                if tc_i == 0 and mc2 == 0:
                    s.wait("vector", "cv", cv_b2)
                s.do("vector", lambda e, p=p, mc2=mc2: e.tensor_scalar(
                    tb[p][:, :], ps_mm[p][:, :], a2c[:, mc2:mc2 + 1],
                    b2h[:, mc2:mc2 + 1], ALU.mult, ALU.add))
                last_evac2[p] = s.do("vector",
                    lambda e, p=p, mc2=mc2, xsl=xsl: e.tensor_add(
                        X[:, mc2, xsl], X[:, mc2, xsl], tb[p][:, :]))
            s.wait("gpsimd", "cv", s.n("cv"))
            s.dma("gpsimd", fm(y_out)[:, :, xsl], X[:, :, xsl], "qst")


def build_ffn0():
    nc = bass.Bass("TRN2", target_bir_lowering=False)
    x_in = nc.dram_tensor("x", [D, TPC], F32, kind="ExternalInput").ap()
    w1t = nc.dram_tensor("w1t", [D, H_FFN], F32, kind="ExternalInput").ap()
    w2t = nc.dram_tensor("w2t", [H_FFN, D], F32, kind="ExternalInput").ap()
    b1c_d = nc.dram_tensor("b1c", [128, H_FFN // 128], F32, kind="ExternalInput").ap()
    b2c_d = nc.dram_tensor("b2c", [128, D // 128], F32, kind="ExternalInput").ap()
    oc_d = nc.dram_tensor("ones_col", [128, 1], F32, kind="ExternalInput").ap()
    or_d = nc.dram_tensor("ones_row", [1, 128], F32, kind="ExternalInput").ap()
    y_out = nc.dram_tensor("y", [D, TPC], F32, kind="ExternalOutput").ap()

    NF = D // 128
    NM1 = H_FFN // 128
    NM2 = D // 128

    s = Sched(nc)
    with s.es:
        _mk_sems(s)
        emit_ffn0(s, dict(x_in=x_in, w1t=w1t, w2t=w2t, b1c_d=b1c_d, b2c_d=b2c_d, oc_d=oc_d, or_d=or_d, y_out=y_out))
        s.wait("gpsimd", "qst", s.n("qst"))
        s.finalize()
    return nc




def emit_attn0a(s, io):
    x_in = io["x_in"]
    brow_d = io["brow_d"]; lam_d = io["lam_d"]
    oc_d = io["oc_d"]; or_d = io["or_d"]
    x1_out = io["x1_out"]; a_out = io["a_out"]; g_out = io["g_out"]
    wts = io["wts"]
    NF = D // 128
    with s.scope():
        X = s.sbuf("X", [128, NF, TPC])
        xnb = s.sbuf("xnb", [128, NF, TPC], BF16)
        x1v = s.sbuf("x1v", [128, NF, TC], BF16)
        x2a = s.sbuf("x2a", [128, NF, TC], BF16)
        x2b = s.sbuf("x2b", [128, NF, TC], BF16)
        rv = s.sbuf("rv", [128, NF, TC])
        iv = s.sbuf("iv", [128, NF, TC], BF16)
        avb = s.sbuf("avb", [128, NF, TC])
        gvb = s.sbuf("gvb", [128, NF, TC], BF16)
        hsc = s.sbuf("hsc", [128, NF, TC], BF16)
        wb = [s.sbuf("wb%d" % i, [128, NF, 128], BF16) for i in range(3)]
        brow = s.sbuf("sb_brow", [1, 5 * D], BF16)
        lamc = s.sbuf("sb_lamc", [128, 8])
        lt = s.sbuf("lt", [128, 8])
        lr = s.sbuf("lr", [128, 8])
        lam2 = s.sbuf("lam2", [128, 8])
        carry1 = s.sbuf("carry1", [128, 8])
        rsum = s.sbuf("rsum", [128, 8])
        rtmp = s.sbuf("rtmp", [128, 8])
        aend = s.sbuf("aend", [128, 16])
        oc = s.sbuf("oc", [128, 1])
        orow = s.sbuf("orow", [1, 128])
        otc = s.sbuf("otc", [1, TC], BF16)
        rows = {k: s.sbuf("row_" + k, [1, TC]) for k in "mvtr"}
        ps_stat = s.psum("ps_stat", [1, TC])
        ps_stat2 = s.psum("ps_stat2", [1, TC])
        ps_bm = s.psum("ps_bm", [128, TC])
        ps_br = s.psum("ps_br", [128, TC])
        ps_mm = [s.psum("ps_mm0", [128, TC]), s.psum("ps_mm1", [128, TC])]

        c1 = s.const(1.0)
        cn1 = s.const(-1.0)
        ch = s.const(0.5)
        s.dma("sync", brow[:1, :], brow_d[:1, :], "qc")
        s.dma("sync", lamc[:, :], lam_d[:, :], "qc")
        s.dma("sync", oc[:, :], oc_d[:, :], "qc")
        s.dma("sync", orow[:1, :], or_d[:1, :], "qc")
        qconst = s.n("qc")
        cv_ones = s.do("vector", lambda e: e.memset(otc[:1, :], 1.0))
        s.wait("scalar", "qc", qconst)
        s.do("scalar", lambda e: e.activation(lt[:, :], lamc[:, :], AF.Abs))
        s.do("scalar", lambda e: e.activation(lt[:, :], lt[:, :], AF.Exp, scale=cn1))
        cs_l = s.do("scalar", lambda e: e.activation(lt[:, :], lt[:, :], AF.Ln, bias=c1))
        cs_r = s.do("scalar", lambda e: e.activation(lr[:, :], lamc[:, :], AF.Relu,
                                                     scale=cn1))
        s.wait("vector", "cs", cs_r)
        s.do("vector", lambda e: e.tensor_add(lt[:, :], lt[:, :], lr[:, :]))
        cv_lam = s.do("vector", lambda e: e.tensor_scalar(
            lamc[:, :], lt[:, :], -C_RG, None, ALU.mult))
        cv_lam = s.do("vector", lambda e: e.tensor_scalar(
            lam2[:, :], lamc[:, :], 2.0, None, ALU.mult))
        s.do("vector", lambda e: e.memset(carry1[:, :], 0.0))
        s.do("vector", lambda e: e.memset(rsum[:, :], 0.0))
        x_cnt = s.dma("sync", X[:, :, :], fm(x_in)[:, :, :], "qx")
        cv_xns = []
        for tc_i in range(NTC):
            xsl = slice(tc_i * TC, (tc_i + 1) * TC)
            deps = {"x_ready": ("qx", x_cnt),
                    "sq_war": [("cpe", s.n("cpe"))],
                    "stat_war": [("cv", s.n("cv"))],
                    "bcast_war": [("cv", s.n("cv"))]}
            cv_xns.append(emit_ln(s, NF, X[:, :, xsl], rv, rows, oc, orow,
                                  ps_stat, ps_stat2, ps_bm, ps_br, deps,
                                  out=xnb[:, :, xsl]))
        st = {"wpar": 0, "mmpar": 0, "last_w": {0: 0, 1: 0, 2: 0},
              "last_evac": {0: 0, 1: 0}, "qst": 0}

        def lin(wt, boff, rhs, rhs_ready, out, func, scale=None):
            evacs = []
            for mc in range(NF):
                wp = st["wpar"]; st["wpar"] = (st["wpar"] + 1) % 3
                w_cnt = s.dma("sync", fl(wb[wp]),
                              wt[mc, :, :], "qw%d" % wp,
                              waits=[("cpe", st["last_w"][wp])])
                pp = st["mmpar"]; st["mmpar"] ^= 1
                s.wait("tensor", "qw%d" % wp, w_cnt)
                if mc == 0 and rhs_ready is not None:
                    s.wait("tensor", rhs_ready[0], rhs_ready[1])
                    s.wait("tensor", "qc", s.n("qc"))
                s.wait("tensor", "cs", st["last_evac"][pp])
                for kt in range(NF):
                    s.do("tensor", lambda e, wp=wp, pp=pp, kt=kt, rhs=rhs: e.matmul(
                        ps_mm[pp][:, :], wb[wp][:, kt, :], rhs[:, kt, :],
                        start=(kt == 0), stop=False))
                cpe_mm = s.do("tensor", lambda e, pp=pp, boff=boff, mc=mc: e.matmul(
                    ps_mm[pp][:, :], brow[:1, boff + mc * 128:boff + (mc + 1) * 128],
                    otc[:1, :], start=False, stop=True), inc="cpe")
                st["last_w"][wp] = cpe_mm
                s.wait("scalar", "cpe", cpe_mm)
                if scale is None:
                    cs_e = s.do("scalar", lambda e, pp=pp, mc=mc, out=out: e.activation(
                        out[:, mc, :], ps_mm[pp][:, :], func))
                else:
                    cs_e = s.do("scalar", lambda e, pp=pp, mc=mc, out=out: e.activation(
                        out[:, mc, :], ps_mm[pp][:, :], func, scale=scale))
                st["last_evac"][pp] = cs_e
                evacs.append(cs_e)
            return evacs[-1]

        flat = lambda t: t.ap()[:, :, :].rearrange("p a b -> p (a b)")
        for tc_i in range(NTC):
            xsl = slice(tc_i * TC, (tc_i + 1) * TC)
            xnc = xnb[:, :, xsl]
            cv_rdy = max(cv_xns[tc_i], cv_ones or 0, cv_lam or 0)
            s.wait("scalar", "qst", st["qst"])
            s.wait("scalar", "cv", s.n("cv"))
            cs_x2a = lin(wts["wp2t"], 1 * D, xnc, ("cv", cv_rdy), x2a, AF.Copy)
            lin(wts["wp1t"], 0 * D, xnc, ("cv", cv_rdy), x1v, AF.Gelu)
            cs_x2b = lin(wts["cvt"], 2 * D, x2a, ("cs", cs_x2a), x2b, AF.Relu)
            cs_rv = lin(wts["wat"], 3 * D, x2b, ("cs", cs_x2b), rv, AF.Sigmoid)
            for mc in range(NF):
                s.do("scalar", lambda e, mc=mc: e.activation(
                    avb[:, mc, :], rv[:, mc, :], AF.Exp,
                    scale=lamc[:, mc:mc + 1]))
            cs_a = s.n("cs")
            for mc in range(NF):
                s.do("scalar", lambda e, mc=mc: e.activation(
                    rv[:, mc, :], rv[:, mc, :], AF.Exp,
                    scale=lam2[:, mc:mc + 1]))
            cs_g = s.do("scalar", lambda e: e.activation(flat(rv), flat(rv), AF.Sqrt,
                                                         scale=cn1, bias=c1))
            cs_i = lin(wts["wxt"], 4 * D, x2b, ("cs", cs_x2b), iv, AF.Sigmoid)
            s.wait("vector", "cs", max(cs_g, cs_i))
            s.do("vector", lambda e: e.tensor_mul(flat(gvb), flat(rv), flat(iv)))
            cv_g = s.do("vector", lambda e: e.tensor_mul(flat(gvb), flat(gvb),
                                                         flat(x2b)))
            s.wait("scalar", "cv", s.n("cv"))
            cs_ln = s.do("scalar", lambda e, ce=s.const(1e-38): e.activation(
                flat(rv), flat(avb), AF.Ln, bias=ce))
            s.wait("vector", "cs", cs_ln)
            s.do("vector", lambda e: e.tensor_reduce(
                rtmp[:, :].rearrange("p (f o) -> p f o", o=1),
                rv[:, :, :], axis=AX.X, op=ALU.add))
            cv_rs = s.do("vector", lambda e: e.tensor_add(rsum[:, :], rsum[:, :],
                                                          rtmp[:, :]))
            s.wait("gpsimd", "cs", cs_a)
            s.dma("gpsimd", fm(a_out)[:, :, xsl], avb[:, :, :], "qst")
            s.wait("gpsimd", "cv", cv_g)
            s.dma("gpsimd", fm(g_out)[:, :, xsl], gvb[:, :, :], "qst")
            s.dma("gpsimd", fm(x1_out)[:, :, xsl], x1v[:, :, :], "qst")
            st["qst"] = s.n("qst")
            for f in range(NF):
                s.do("vector", lambda e, f=f: e.tensor_tensor_scan(
                    hsc[:, f, :], avb[:, f, :], gvb[:, f, :],
                    carry1[:, f:f + 1], ALU.mult, ALU.add))
                s.do("vector", lambda e, f=f: e.tensor_copy(
                    carry1[:, f:f + 1], hsc[:, f, TC - 1:TC]))
        s.wait("scalar", "cv", s.n("cv"))
        s.do("scalar", lambda e: e.activation(aend[:, 0:8], rsum[:, :], AF.Exp))
        s.wait("vector", "cs", s.n("cs"))
        s.do("vector", lambda e: e.tensor_copy(aend[:, 8:16], carry1[:, :]))
        s.wait("gpsimd", "cv", s.n("cv"))
        s.wait("gpsimd", "cs", s.n("cs"))
        s.dma("gpsimd", io["exg_out"][:, :], aend[:, :], "qst")


def build_attn0a():
    nc = bass.Bass("TRN2", target_bir_lowering=False)
    x_in = nc.dram_tensor("x", [D, TPC], F32, kind="ExternalInput").ap()
    wts = {}
    for nm in ("wp1t", "wp2t", "cvt", "wat", "wxt"):
        wts[nm] = nc.dram_tensor(nm, [D, D], F32, kind="ExternalInput").ap()
    brow_d = nc.dram_tensor("brow", [1, 5 * D], F32, kind="ExternalInput").ap()
    lam_d = nc.dram_tensor("lamc", [128, 8], F32, kind="ExternalInput").ap()
    oc_d = nc.dram_tensor("ones_col", [128, 1], F32, kind="ExternalInput").ap()
    or_d = nc.dram_tensor("ones_row", [1, 128], F32, kind="ExternalInput").ap()
    x1_out = nc.dram_tensor("x1g", [D, TPC], F32, kind="ExternalOutput").ap()
    a_out = nc.dram_tensor("av", [D, TPC], F32, kind="ExternalOutput").ap()
    g_out = nc.dram_tensor("gv", [D, TPC], F32, kind="ExternalOutput").ap()
    NF = D // 128

    s = Sched(nc)
    with s.es:
        _mk_sems(s)
        emit_attn0a(s, dict(x_in=x_in, brow_d=brow_d, lam_d=lam_d, oc_d=oc_d, or_d=or_d, x1_out=x1_out, a_out=a_out, g_out=g_out, wts=wts))
        s.wait("gpsimd", "qst", s.n("qst"))
        s.finalize()
    return nc



def emit_scan0(s, io):
    a_in = io["a_in"]
    g_in = io["g_in"]
    h_out = io["h_out"]
    with s.scope():
        ab = s.sbuf("ab", [128, S_TOT])
        gb = s.sbuf("gb", [128, S_TOT])
        s.dma("sync", ab[:, :], a_in[:, :], "qx")
        q2 = s.dma("sync", gb[:, :], g_in[:, :], "qx")
        s.wait("vector", "qx", q2)
        cv_h = s.do("vector", lambda e: e.tensor_tensor_scan(
            gb[:, :], ab[:, :], gb[:, :], 0.0, ALU.mult, ALU.add))
        s.wait("gpsimd", "cv", cv_h)
        q3 = s.dma("gpsimd", h_out[:, :], gb[:, :], "qst")
        s.wait("gpsimd", "qst", q3)


def build_scan0():
    nc = bass.Bass("TRN2", target_bir_lowering=False)
    a_in = nc.dram_tensor("a", [128, S_TOT], F32, kind="ExternalInput").ap()
    g_in = nc.dram_tensor("g", [128, S_TOT], F32, kind="ExternalInput").ap()
    h_out = nc.dram_tensor("h", [128, S_TOT], F32, kind="ExternalOutput").ap()
    s = Sched(nc)
    with s.es:
        _mk_sems(s)
        emit_scan0(s, dict(a_in=a_in, g_in=g_in, h_out=h_out))
        s.wait("gpsimd", "qst", s.n("qst"))
        s.finalize()
    return nc



def emit_attn0b(s, io):
    x_in = io["x_in"]; x1_in = io["x1_in"]
    a_in = io["a_in"]; g_in = io["g_in"]
    exg_in = io["exg_in"]; msk_d = io["msk_d"]
    wp3t = io["wp3t"]; brow_d = io["brow_d"]; y_out = io["y_out"]
    NF = D // 128
    with s.scope():
        X_c = s.sbuf("X_c", [128, NF, TC])
        x1f = s.sbuf("x1f", [128, NF, TC], BF16)
        a_c = s.sbuf("a_c", [128, NF, TC])
        g_c = s.sbuf("g_c", [128, NF, TC], BF16)
        m_c = s.sbuf("m_c", [128, NF, TC], BF16)
        wb = [s.sbuf("wb0", [128, NF, 128], BF16),
              s.sbuf("wb1", [128, NF, 128], BF16)]
        brow = s.sbuf("sb_brow", [1, D])
        otc = s.sbuf("otc", [1, TC])
        exga = s.sbuf("exga", [128, NCORES, 16])
        msk = s.sbuf("sb_msk", [128, NCORES])
        carry2 = s.sbuf("carry2", [128, NF])
        sst = s.sbuf("sst", [128, NF])
        tj = s.sbuf("tj", [128, NF])
        ps_mm = [s.psum("ps_mm0", [128, TC]), s.psum("ps_mm1", [128, TC])]
        s.dma("sync", brow[:1, :], brow_d[:1, :], "qc")
        s.dma("sync", msk[:, :], msk_d[:, :], "qc")
        qc1 = s.dma("sync", exga[:, :, :],
                    exg_in.rearrange("(j p) c -> p j c", p=128), "qc")
        cv_ones = s.do("vector", lambda e: e.memset(otc[:1, :], 1.0))
        s.wait("vector", "qc", qc1)
        s.do("vector", lambda e: e.memset(sst[:, :], 0.0))
        s.do("vector", lambda e: e.memset(carry2[:, :], 0.0))
        for j in range(NCORES):
            if j > 0:
                s.do("vector", lambda e, j=j: e.tensor_mul(
                    sst[:, :], sst[:, :], exga[:, j - 1, 0:8]))
                s.do("vector", lambda e, j=j: e.tensor_add(
                    sst[:, :], sst[:, :], exga[:, j - 1, 8:16]))
            s.do("vector", lambda e, j=j: e.scalar_tensor_tensor(
                carry2[:, :], sst[:, :], msk[:, j:j + 1], carry2[:, :],
                ALU.mult, ALU.add))
        st = {"wpar": 0, "last_w": {0: 0, 1: 0}, "last_use": {0: 0, 1: 0},
              "qst": 0}
        flat = lambda t: t.ap()[:, :, :].rearrange("p a b -> p (a b)")
        for tc_i in range(NTC):
            xsl = slice(tc_i * TC, (tc_i + 1) * TC)
            s.dma("sync", X_c[:, :, :], fm(x_in)[:, :, xsl], "qx",
                  waits=[("qst", st["qst"])])
            s.dma("sync", x1f[:, :, :], fm(x1_in)[:, :, xsl], "qx",
                  waits=[("cv", s.n("cv"))])
            s.dma("sync", a_c[:, :, :], fm(a_in)[:, :, xsl], "qx")
            x_cnt = s.dma("sync", g_c[:, :, :], fm(g_in)[:, :, xsl], "qx")
            s.wait("vector", "qx", x_cnt)
            for f in range(NF):
                s.do("vector", lambda e, f=f: e.tensor_tensor_scan(
                    g_c[:, f, :], a_c[:, f, :], g_c[:, f, :],
                    carry2[:, f:f + 1], ALU.mult, ALU.add))
                s.do("vector", lambda e, f=f: e.tensor_copy(
                    carry2[:, f:f + 1], g_c[:, f, TC - 1:TC]))
            s.wait("vector", "cpe", s.n("cpe"))
            cv_m = s.do("vector", lambda e: e.tensor_mul(flat(m_c), flat(x1f),
                                                         flat(g_c)))
            for mc in range(NF):
                wp = st["wpar"]; st["wpar"] ^= 1
                w_cnt = s.dma("sync", fl(wb[wp]),
                              wp3t[mc, :, :], "qw%d" % wp,
                              waits=[("cpe", st["last_w"][wp])])
                s.wait("tensor", "qw%d" % wp, w_cnt)
                if mc == 0:
                    s.wait("tensor", "cv", cv_m)
                    if tc_i == 0:
                        s.wait("tensor", "cv", cv_ones)
                        s.wait("tensor", "qc", s.n("qc"))
                s.wait("tensor", "cv", st["last_use"][wp])
                for kt in range(NF):
                    s.do("tensor", lambda e, wp=wp, kt=kt: e.matmul(
                        ps_mm[wp][:, :], wb[wp][:, kt, :], m_c[:, kt, :],
                        start=(kt == 0), stop=False))
                cpe_mm = s.do("tensor", lambda e, wp=wp, mc=mc: e.matmul(
                    ps_mm[wp][:, :], brow[:1, mc * 128:(mc + 1) * 128],
                    otc[:1, :], start=False, stop=True), inc="cpe")
                st["last_w"][wp] = cpe_mm
                s.wait("vector", "cpe", cpe_mm)
                if mc == 0:
                    s.wait("vector", "qst", st["qst"])
                st["last_use"][wp] = s.do("vector", lambda e, wp=wp, mc=mc:
                    e.tensor_add(X_c[:, mc, :], X_c[:, mc, :], ps_mm[wp][:, :]))
            s.wait("gpsimd", "cv", s.n("cv"))
            s.dma("gpsimd", fm(y_out)[:, :, xsl], X_c[:, :, :], "qst")
            st["qst"] = s.n("qst")


def build_attn0b():
    nc = bass.Bass("TRN2", target_bir_lowering=False)
    x_in = nc.dram_tensor("x", [D, TPC], F32, kind="ExternalInput").ap()
    x1_in = nc.dram_tensor("x1g", [D, TPC], F32, kind="ExternalInput").ap()
    h_in = nc.dram_tensor("hv", [D, TPC], F32, kind="ExternalInput").ap()
    wp3t = nc.dram_tensor("wp3t", [D, D], F32, kind="ExternalInput").ap()
    brow_d = nc.dram_tensor("brow", [1, D], F32, kind="ExternalInput").ap()
    y_out = nc.dram_tensor("y", [D, TPC], F32, kind="ExternalOutput").ap()
    NF = D // 128
    s = Sched(nc)
    with s.es:
        _mk_sems(s)
        emit_attn0b(s, dict(x_in=x_in, x1_in=x1_in, h_in=h_in, wp3t=wp3t, brow_d=brow_d, y_out=y_out))
        s.wait("gpsimd", "qst", s.n("qst"))
        s.finalize()
    return nc




def emit_moe(s, io):
    NF = D // 128
    NGI = TC // 128
    x_in = io["x_in"]
    w1all = io["w1all"]
    w2all = io["w2all"]
    gw_d = io["gw_d"]
    gbrow_d = io["gbrow_d"]
    b1c_d = io["b1c_d"]
    b2c_d = io["b2c_d"]
    g1c_d = io["g1c_d"]
    be1c_d = io["be1c_d"]
    g2c_d = io["g2c_d"]
    be2c_d = io["be2c_d"]
    sel4_d = io["sel4_d"]
    id_d = io["id_d"]
    oc_d = io["oc_d"]
    or_d = io["or_d"]
    y_out = io["y_out"]
    with s.scope():
        X_cs = [s.sbuf("X_c%d" % i, [128, NF, TC]) for i in range(2)]
        xnb = s.sbuf("xnb", [128, NF, TC], BF16)
        lnsc = s.sbuf("lnsc", [128, NF, TC])
        heb = [s.sbuf("heb%d" % i, [128, NF, TC], BF16) for i in range(NEXP)]
        hrb = [s.sbuf("hrb%d" % i, [128, NF, TC], BF16) for i in range(NEXP)]
        hob = [s.sbuf("hob%d" % i, [128, NF, TC], BF16) for i in range(2)]
        sqb = s.sbuf("sqb", [128, NF, TC], BF16)
        wsl = s.sbuf("wsl", [128, NEXP, TC])
        tbuf = s.sbuf("tbuf", [128, TC])
        wb = [s.sbuf("wb%d" % i, [128, NF, 128], BF16) for i in range(3)]
        gw = s.sbuf("sb_gw", [128, NF, NEXP], BF16)
        gbrow = s.sbuf("sb_gbrow", [1, NEXP])
        b1c = s.sbuf("sb_eb1c", [128, NEXP * NF])
        b2c = s.sbuf("sb_eb2c", [128, NEXP * NF])
        g1c = s.sbuf("sb_g1c", [128, NEXP * 8])
        be1c = s.sbuf("sb_be1c", [128, NEXP * 8])
        g2c = s.sbuf("sb_g2c", [128, NEXP * 8])
        be2c = s.sbuf("sb_be2c", [128, NEXP * 8])
        sel4 = s.sbuf("sb_sel4", [NEXP, NEXP * 128])
        iden = s.sbuf("sb_ident", [128, 128])
        oc = s.sbuf("oc", [128, 1])
        ocb = s.sbuf("ocb", [128, 1], BF16)
        orow = s.sbuf("orow", [1, 128])
        gsb = s.sbuf("gsb", [128, NGI, NEXP])
        top = s.sbuf("top", [128, NGI, 8])
        wn = s.sbuf("wn", [128, NGI, NEXP])
        wr = s.sbuf("wr", [NEXP, TC])
        rows = {k: s.sbuf("row_" + k, [1, TC]) for k in "mvtrw"}
        ps = [s.psum("ps%d" % i, [128, TC]) for i in range(8)]
        ch = s.const(0.5)

        for dst, srcap in ((gw[:, :, :], fm(gw_d)), (gbrow[:1, :], gbrow_d[:1, :]),
                           (b1c[:, :], b1c_d[:, :]), (b2c[:, :], b2c_d[:, :]),
                           (g1c[:, :], g1c_d[:, :]), (be1c[:, :], be1c_d[:, :]),
                           (g2c[:, :], g2c_d[:, :]), (be2c[:, :], be2c_d[:, :]),
                           (sel4[:, :], sel4_d[:, :]), (iden[:, :], id_d[:, :]),
                           (oc[:, :], oc_d[:, :]), (orow[:1, :], or_d[:1, :])):
            s.dma("sync", dst, srcap, "qc")
        qconst = s.n("qc")
        cv_ocb = s.do("vector", lambda e: e.memset(ocb[:, :], 1.0))

        st = {"wpar": 0, "mmpar": 0, "last_w": {0: 0, 1: 0, 2: 0},
              "last_evac": {0: 0, 1: 0},
              "qst": [0, 0], "x_war": [0, 0], "hob_war": {0: 0, 1: 0}}

        def mm_stage(wt3d, toff, rhs, rhs_ready, out, bias_sb, boff, evac_war):
            for mc in range(NF):
                wp = st["wpar"]; st["wpar"] = (st["wpar"] + 1) % 3
                w_cnt = s.dma("sync", fl(wb[wp]),
                              wt3d[toff + mc, :, :],
                              "qw%d" % wp,
                              waits=[("cpe", st["last_w"][wp])])
                pp = st["mmpar"]; st["mmpar"] ^= 1
                s.wait("tensor", "qw%d" % wp, w_cnt)
                if mc == 0 and rhs_ready is not None:
                    s.wait("tensor", rhs_ready[0], rhs_ready[1])
                s.wait("tensor", "cs", st["last_evac"][pp])
                for kt in range(NF):
                    s.do("tensor", lambda e, wp=wp, pp=pp, kt=kt: e.matmul(
                        ps[4 + pp][:, :], wb[wp][:, kt, :], rhs[:, kt, :],
                        start=(kt == 0), stop=(kt == NF - 1)),
                        inc=("cpe" if kt == NF - 1 else None))
                cpe_mm = s.n("cpe")
                st["last_w"][wp] = cpe_mm
                s.wait("scalar", "cpe", cpe_mm)
                if mc == 0 and evac_war:
                    s.wait("scalar", evac_war[0], evac_war[1])
                s.do("scalar", lambda e, pp=pp, mc=mc, out=out, boff=boff:
                     e.activation(out[:, mc, :], ps[4 + pp][:, :], AF.Identity,
                                  bias=bias_sb[:, boff + mc:boff + mc + 1]))
                st["last_evac"][pp] = s.n("cs")
                s.do("scalar", lambda e, mc=mc, out=out: e.square(
                    sqb[:, mc, :], out[:, mc, :]))
            return s.n("cs")

        def stats_blk(src, cs_ready, ps01_war):
            s.wait("tensor", "cs", cs_ready)
            s.wait("tensor", "cv", ps01_war)
            for mc in range(NF):
                s.do("tensor", lambda e, mc=mc, src=src: e.matmul(
                    ps[0][:1, :], ocb[:, :1], src[:, mc, :],
                    start=(mc == 0), stop=(mc == NF - 1)),
                    inc=("cpe" if mc == NF - 1 else None))
            cpe_sum = s.n("cpe")
            for mc in range(NF):
                s.do("tensor", lambda e, mc=mc: e.matmul(
                    ps[1][:1, :], ocb[:, :1], sqb[:, mc, :],
                    start=(mc == 0), stop=(mc == NF - 1)),
                    inc=("cpe" if mc == NF - 1 else None))
            return cpe_sum, s.n("cpe")

        def ln_rows(cpe_sum, cpe_sq):
            s.wait("vector", "cpe", cpe_sum)
            s.do("vector", lambda e: e.tensor_scalar(
                rows["m"][:1, :], ps[0][:1, :], 1.0 / D, None, ALU.mult))
            s.wait("vector", "cpe", cpe_sq)
            cv_free = s.do("vector", lambda e: e.tensor_scalar(
                rows["t"][:1, :], ps[1][:1, :], 1.0 / D, None, ALU.mult))
            s.do("vector", lambda e: e.tensor_mul(
                rows["v"][:1, :], rows["m"][:1, :], rows["m"][:1, :]))
            cv_var = s.do("vector", lambda e: e.tensor_sub(
                rows["v"][:1, :], rows["t"][:1, :], rows["v"][:1, :]))
            c_eps = s.const(EPS)[:1, :]
            c_nh = s.const(-0.5)[:1, :]
            s.wait("scalar", "cv", cv_var)
            s.do("scalar", lambda e: e.activation(
                rows["t"][:1, :], rows["v"][:1, :], AF.Ln, bias=c_eps))
            cs_r = s.do("scalar", lambda e: e.activation(
                rows["r"][:1, :], rows["t"][:1, :], AF.Exp, scale=c_nh))
            s.wait("vector", "cs", cs_r)
            cv_done = s.do("vector", lambda e: e.tensor_mul(
                rows["w"][:1, :], rows["m"][:1, :], rows["r"][:1, :]))
            return cv_done, cv_free

        def bcast_rw(cv_rows, ps23_war):
            s.wait("tensor", "cv", max(cv_rows, ps23_war))
            s.do("tensor", lambda e: e.matmul(
                ps[2][:, :], orow[:1, :], rows["r"][:1, :],
                start=True, stop=True))
            return s.do("tensor", lambda e: e.matmul(
                ps[3][:, :], orow[:1, :], rows["w"][:1, :],
                start=True, stop=True), inc="cpe")

        for tc_i in range(NTC):
            par = tc_i % 2
            X_c = X_cs[par]
            xsl = slice(tc_i * TC, (tc_i + 1) * TC)
            x_cnt = s.dma("sync", X_c[:, :, :], fm(x_in)[:, :, xsl], "qx",
                          waits=[("qst", st["qst"][par]),
                                 ("cv", st["x_war"][par])])
            deps = {"x_ready": ("qx", x_cnt),
                    "sq_war": [("cpe", s.n("cpe")), ("cv", s.n("cv"))],
                    "stat_war": [("cv", s.n("cv"))] if tc_i else [],
                    "bcast_war": [("cv", s.n("cv"))] if tc_i else [],
                    "out_war": [("cpe", s.n("cpe")), ("cs", s.n("cs"))]}
            s.wait("vector", "qst", st["qst"][par])
            cv_xn = emit_ln(s, NF, X_c, lnsc, rows, oc, orow,
                            ps[0], ps[1], ps[2], ps[3], deps)
            for f in range(NF):
                s.do("vector", lambda e, f=f, X_c=X_c: e.scalar_tensor_tensor(
                    X_c[:, f, :], lnsc[:, f, :], ch, X_c[:, f, :],
                    ALU.mult, ALU.add))
            flatn = lambda t: t.ap()[:, :, :].rearrange("p a b -> p (a b)")
            cv_xn = s.do("vector", lambda e: e.tensor_copy(flatn(xnb),
                                                           flatn(lnsc)))
            if tc_i == 0:
                cv_xn = max(cv_xn, cv_ocb)
            for gi in range(NGI):
                gslice = slice(gi * 128, (gi + 1) * 128)
                s.wait("tensor", "cv", cv_xn)
                if tc_i == 0 and gi == 0:
                    s.wait("tensor", "qc", qconst)
                s.wait("tensor", "cv", s.n("cv"))
                for f in range(NF):
                    s.do("tensor", lambda e, f=f, gslice=gslice: e.matmul(
                        ps[7][:, :NEXP], xnb[:, f, gslice], gw[:, f, :],
                        start=(f == 0), stop=False))
                cpe_g = s.do("tensor", lambda e: e.matmul(
                    ps[7][:, :NEXP], orow[:1, :], gbrow[:1, :],
                    start=False, stop=True), inc="cpe")
                s.wait("vector", "cpe", cpe_g)
                s.do("vector", lambda e, gi=gi: e.tensor_copy(gsb[:, gi, :],
                                                              ps[7][:, :NEXP]))
            se = lambda e_: gsb[:, :, e_:e_ + 1]
            tcol = lambda j: top[:, :, j:j + 1]
            s.do("vector", lambda e: e.tensor_tensor(tcol(0), se(0), se(1), ALU.max))
            s.do("vector", lambda e: e.tensor_tensor(tcol(1), se(0), se(1), ALU.min))
            s.do("vector", lambda e: e.tensor_tensor(tcol(2), se(2), se(3), ALU.max))
            s.do("vector", lambda e: e.tensor_tensor(tcol(3), se(2), se(3), ALU.min))
            s.do("vector", lambda e: e.tensor_tensor(tcol(4), tcol(0), tcol(2), ALU.max))
            s.do("vector", lambda e: e.tensor_tensor(tcol(5), tcol(0), tcol(2), ALU.min))
            s.do("vector", lambda e: e.tensor_tensor(tcol(6), tcol(1), tcol(3), ALU.max))
            s.do("vector", lambda e: e.tensor_tensor(tcol(5), tcol(5), tcol(6), ALU.max))
            for e_ in range(NEXP):
                s.do("vector", lambda e, e_=e_: e.tensor_tensor(
                    wn[:, :, e_:e_ + 1], se(e_), tcol(4), ALU.subtract))
            flatg = lambda t: t.ap()[:, :, :].rearrange("p a b -> p (a b)")
            s.wait("scalar", "cv", s.n("cv"))
            cs_ed = s.do("scalar", lambda e: e.activation(flatg(wn), flatg(wn), AF.Exp))
            s.wait("vector", "cs", cs_ed)
            for e_ in range(NEXP):
                s.do("vector", lambda e, e_=e_: e.tensor_tensor(
                    tcol(6), se(e_), tcol(5), ALU.is_ge))
                s.do("vector", lambda e, e_=e_: e.tensor_tensor(
                    wn[:, :, e_:e_ + 1], wn[:, :, e_:e_ + 1], tcol(6), ALU.mult))
            s.do("vector", lambda e: e.tensor_reduce(
                top[:, :, 6:7], wn[:, :, :], axis=AX.X, op=ALU.add))
            s.do("vector", lambda e: e.reciprocal(tcol(7), tcol(6)))
            for e_ in range(NEXP):
                s.do("vector", lambda e, e_=e_: e.tensor_tensor(
                    wn[:, :, e_:e_ + 1], wn[:, :, e_:e_ + 1], tcol(7), ALU.mult))
            cv_wn = s.n("cv")
            for gi in range(NGI):
                s.wait("tensor", "cv", max(cv_wn, s.n("cv")))
                cpe_t = s.do("tensor", lambda e, gi=gi: e.transpose(
                    ps[7][:NEXP, :128], wn[:, gi, :], iden[:, :]), inc="cpe")
                s.wait("vector", "cpe", cpe_t)
                s.do("vector", lambda e, gi=gi: e.tensor_copy(
                    wr[:NEXP, gi * 128:(gi + 1) * 128], ps[7][:NEXP, :128]))
            cv_wr = s.do("vector", lambda e: e.tensor_scalar(
                wr[:NEXP, :], wr[:NEXP, :], 0.5, None, ALU.mult))
            for e_ in range(NEXP):
                s.wait("tensor", "cv", max(cv_wr, s.n("cv")))
                cpe_w = s.do("tensor", lambda e, e_=e_: e.matmul(
                    ps[6][:, :], sel4[:NEXP, e_ * 128:(e_ + 1) * 128],
                    wr[:NEXP, :], start=True, stop=True), inc="cpe")
                s.wait("vector", "cpe", cpe_w)
                s.do("vector", lambda e, e_=e_: e.tensor_copy(
                    wsl[:, e_, :], ps[6][:, :]))
            ps01_war = s.n("cv")
            ps23_war = s.n("cv")
            prev_relu = s.n("cs")
            relu_cs = {}
            lnres = {}

            def norm1(e_):
                nonlocal ps23_war, prev_relu
                cv_rows, _ = lnres[e_]
                cpe_b = bcast_rw(cv_rows, ps23_war)
                s.wait("vector", "cpe", cpe_b)
                s.wait("vector", "cs", prev_relu)
                for f in range(NF):
                    s.do("vector", lambda e, f=f, e_=e_: e.tensor_mul(
                        lnsc[:, f, :], heb[e_][:, f, :], ps[2][:, :]))
                    s.do("vector", lambda e, f=f: e.tensor_sub(
                        lnsc[:, f, :], lnsc[:, f, :], ps[3][:, :]))
                ps23_war = s.n("cv")
                s.wait("scalar", "cv", ps23_war)
                for f in range(NF):
                    s.do("scalar", lambda e, f=f, e_=e_: e.activation(
                        hrb[e_][:, f, :], lnsc[:, f, :], AF.Relu,
                        scale=g1c[:, e_ * 8 + f:e_ * 8 + f + 1],
                        bias=be1c[:, e_ * 8 + f:e_ * 8 + f + 1]))
                prev_relu = s.n("cs")
                relu_cs[e_] = prev_relu

            for e_ in range(NEXP):
                cs_e = mm_stage(w1all, e_ * NF, xnb,
                                ("cv", cv_xn) if e_ == 0 else None,
                                heb[e_], b1c, e_ * NF, None)
                if e_ > 0:
                    norm1(e_ - 1)
                cpe_sum, cpe_sq = stats_blk(heb[e_], cs_e, ps01_war)
                lnres[e_] = ln_rows(cpe_sum, cpe_sq)
                ps01_war = lnres[e_][1]
            norm1(NEXP - 1)
            hob_war = st["hob_war"]

            def norm2(e_):
                nonlocal ps23_war
                cv_rows, _ = lnres[e_]
                cpe_b = bcast_rw(cv_rows, ps23_war)
                s.wait("vector", "cpe", cpe_b)
                for f in range(NF):
                    s.do("vector", lambda e, f=f, e_=e_: e.tensor_mul(
                        lnsc[:, f, :], hob[e_ % 2][:, f, :], ps[2][:, :]))
                    s.do("vector", lambda e, f=f: e.tensor_sub(
                        lnsc[:, f, :], lnsc[:, f, :], ps[3][:, :]))
                cv_ns = s.n("cv")
                s.wait("scalar", "cv", cv_ns)
                for f in range(NF):
                    s.do("scalar", lambda e, f=f, e_=e_: e.activation(
                        lnsc[:, f, :], lnsc[:, f, :], AF.Identity,
                        scale=g2c[:, e_ * 8 + f:e_ * 8 + f + 1],
                        bias=be2c[:, e_ * 8 + f:e_ * 8 + f + 1]))
                cs_g = s.n("cs")
                s.wait("vector", "cs", cs_g)
                for f in range(NF):
                    s.do("vector", lambda e, f=f, e_=e_: e.tensor_mul(
                        tbuf[:, :], lnsc[:, f, :], wsl[:, e_, :]))
                    s.do("vector", lambda e, f=f, X_c=X_c: e.tensor_add(
                        X_c[:, f, :], X_c[:, f, :], tbuf[:, :]))
                ps23_war = cv_ns
                hob_war[e_ % 2] = cv_ns

            for e_ in range(NEXP):
                cs_e = mm_stage(w2all, e_ * NF, hrb[e_], ("cs", relu_cs[e_]),
                                hob[e_ % 2], b2c, e_ * NF,
                                ("cv", hob_war[e_ % 2]))
                cpe_sum, cpe_sq = stats_blk(hob[e_ % 2], cs_e, ps01_war)
                lnres[e_] = ln_rows(cpe_sum, cpe_sq)
                ps01_war = lnres[e_][1]
                norm2(e_)
            s.wait("gpsimd", "cv", s.n("cv"))
            s.dma("gpsimd", fm(y_out)[:, :, xsl], X_c[:, :, :], "qst")
            st["qst"][par] = s.n("qst")
            st["x_war"][par] = s.n("cv")


def build_moe():
    nc = bass.Bass("TRN2", target_bir_lowering=False)
    x_in = nc.dram_tensor("x", [D, TPC], F32, kind="ExternalInput").ap()
    w1all = nc.dram_tensor("w1all", [NEXP * D, D], F32, kind="ExternalInput").ap()
    w2all = nc.dram_tensor("w2all", [NEXP * D, D], F32, kind="ExternalInput").ap()
    gw_d = nc.dram_tensor("gw", [D, NEXP], F32, kind="ExternalInput").ap()
    gbrow_d = nc.dram_tensor("gbrow", [1, NEXP], F32, kind="ExternalInput").ap()
    b1c_d = nc.dram_tensor("eb1c", [128, NEXP * 8], F32, kind="ExternalInput").ap()
    b2c_d = nc.dram_tensor("eb2c", [128, NEXP * 8], F32, kind="ExternalInput").ap()
    g1c_d = nc.dram_tensor("g1c", [128, NEXP * 8], F32, kind="ExternalInput").ap()
    be1c_d = nc.dram_tensor("be1c", [128, NEXP * 8], F32, kind="ExternalInput").ap()
    g2c_d = nc.dram_tensor("g2c", [128, NEXP * 8], F32, kind="ExternalInput").ap()
    be2c_d = nc.dram_tensor("be2c", [128, NEXP * 8], F32, kind="ExternalInput").ap()
    sel4_d = nc.dram_tensor("sel4", [NEXP, NEXP * 128], F32, kind="ExternalInput").ap()
    id_d = nc.dram_tensor("ident", [128, 128], F32, kind="ExternalInput").ap()
    oc_d = nc.dram_tensor("ones_col", [128, 1], F32, kind="ExternalInput").ap()
    or_d = nc.dram_tensor("ones_row", [1, 128], F32, kind="ExternalInput").ap()
    y_out = nc.dram_tensor("y", [D, TPC], F32, kind="ExternalOutput").ap()
    NF = D // 128
    NGI = TC // 128

    s = Sched(nc)
    with s.es:
        _mk_sems(s)
        emit_moe(s, dict(x_in=x_in, w1all=w1all, w2all=w2all, gw_d=gw_d, gbrow_d=gbrow_d, b1c_d=b1c_d, b2c_d=b2c_d, g1c_d=g1c_d, be1c_d=be1c_d, g2c_d=g2c_d, be2c_d=be2c_d, sel4_d=sel4_d, id_d=id_d, oc_d=oc_d, or_d=or_d, y_out=y_out))
        s.wait("gpsimd", "qst", s.n("qst"))
        s.finalize()
    return nc


def emit_mamba_a(s, io):
    x_in = io["x_in"]
    wt = io["wt"]
    oc_d = io["oc_d"]; or_d = io["or_d"]
    xzp = io["xzp"]
    CH = DI // NCORES
    NF = D // 128
    NM = (2 * DI) // 128
    xzp5 = xzp.rearrange("j (h cb p t) -> j h cb p t", h=2, cb=CH // 128, p=128)
    with s.scope():
        X_c = s.sbuf("X_c", [128, NF, TPC])
        sqxn = s.sbuf("sqxn", [128, NF, TC])
        xnb = s.sbuf("xnb", [128, NF, TPC], BF16)
        ob = s.sbuf("ob", [128, NM, TC], BF16)
        wb = [s.sbuf("wb%d" % i, [128, NF, 128], BF16) for i in range(3)]
        oc = s.sbuf("oc", [128, 1])
        orow = s.sbuf("orow", [1, 128])
        rows = {k: s.sbuf("row_" + k, [1, TC]) for k in "mvtr"}
        ps_stat = s.psum("ps_stat", [1, TC])
        ps_stat2 = s.psum("ps_stat2", [1, TC])
        ps_bm = s.psum("ps_bm", [128, TC])
        ps_br = s.psum("ps_br", [128, TC])
        ps_mm = [s.psum("ps_mm0", [128, TC]), s.psum("ps_mm1", [128, TC])]
        s.dma("sync", oc[:, :], oc_d[:, :], "qc")
        s.dma("sync", orow[:1, :], or_d[:1, :], "qc")
        st = {"wpar": 0, "last_w": {0: 0, 1: 0, 2: 0},
              "last_evac": {0: 0, 1: 0}, "qst": 0}
        NCB = CH // 128
        x_cnt = s.dma("sync", X_c[:, :, :], fm(x_in)[:, :, :], "qx")
        cv_xns = []
        for tc_i in range(NTC):
            xsl = slice(tc_i * TC, (tc_i + 1) * TC)
            deps = {"x_ready": ("qx", x_cnt),
                    "sq_war": [("cpe", s.n("cpe"))],
                    "stat_war": [("cv", s.n("cv"))],
                    "bcast_war": [("cv", s.n("cv"))]}
            cv_xns.append(emit_ln(s, NF, X_c[:, :, xsl], sqxn, rows, oc, orow,
                                  ps_stat, ps_stat2, ps_bm, ps_br, deps,
                                  out=xnb[:, :, xsl]))
        for tc_i in range(NTC):
            xsl = slice(tc_i * TC, (tc_i + 1) * TC)
            cv_xn = cv_xns[tc_i]
            s.wait("scalar", "qst", st["qst"])
            for mc in range(NM):
                wp = st["wpar"]; st["wpar"] = (st["wpar"] + 1) % 3
                pp = mc % 2
                w_cnt = s.dma("sync", fl(wb[wp]),
                              wt[mc, :, :], "qw%d" % wp,
                              waits=[("cpe", st["last_w"][wp])])
                s.wait("tensor", "qw%d" % wp, w_cnt)
                if mc == 0:
                    s.wait("tensor", "cv", cv_xn)
                s.wait("tensor", "cs", st["last_evac"][pp])
                for kt in range(NF):
                    s.do("tensor", lambda e, wp=wp, pp=pp, kt=kt, xsl=xsl: e.matmul(
                        ps_mm[pp][:, :], wb[wp][:, kt, :], xnb[:, kt, xsl],
                        start=(kt == 0), stop=(kt == NF - 1)),
                        inc=("cpe" if kt == NF - 1 else None))
                cpe_mm = s.n("cpe")
                st["last_w"][wp] = cpe_mm
                s.wait("scalar", "cpe", cpe_mm)
                cs_e = s.do("scalar", lambda e, pp=pp, mc=mc:
                            e.activation(ob[:, mc, :], ps_mm[pp][:, :], AF.Copy))
                st["last_evac"][pp] = cs_e
                h_i, jj, cb = mc // 16, (mc % 16) // NCB, mc % NCB
                s.wait("gpsimd", "cs", cs_e)
                s.dma("gpsimd", xzp5[jj, h_i, cb, :, xsl], ob[:, mc, :], "qst")
            st["qst"] = s.n("qst")


def build_mamba_a():
    nc = bass.Bass("TRN2", target_bir_lowering=False)
    x_in = nc.dram_tensor("x", [D, TPC], F32, kind="ExternalInput").ap()
    wt = nc.dram_tensor("iprojt", [D, 2 * DI], F32, kind="ExternalInput").ap()
    oc_d = nc.dram_tensor("ones_col", [128, 1], F32, kind="ExternalInput").ap()
    or_d = nc.dram_tensor("ones_row", [1, 128], F32, kind="ExternalInput").ap()
    xz_out = nc.dram_tensor("xz", [2 * DI, TPC], F32, kind="ExternalOutput").ap()
    NF = D // 128
    NM = (2 * DI) // 128
    s = Sched(nc)
    with s.es:
        _mk_sems(s)
        emit_mamba_a(s, dict(x_in=x_in, wt=wt, oc_d=oc_d, or_d=or_d, xz_out=xz_out))
        s.wait("gpsimd", "qst", s.n("qst"))
        s.finalize()
    return nc



def emit_mamba_b(s, io):
    x_in = io["x_in"]
    wc_d = io["wc_d"]; cb_d = io["cb_d"]
    xccp = io["y_out"]
    CH = DI // NCORES
    NPT = CH // 128
    TCH = 4096
    JPC = TCH // TPC
    with s.scope():
        xin = s.sbuf("xin", [128, NPT, TCH + 3], BF16)
        accb = s.sbuf("accb", [128, NPT, TCH])
        ov = s.sbuf("ov", [128, NPT, TCH], BF16)
        wc = s.sbuf("sb_wc", [128, NPT * DCONV])
        cb = s.sbuf("sb_cb", [128, NPT])
        s.dma("sync", wc[:, :], wc_d[:, :], "qc")
        s.dma("sync", cb[:, :], cb_d[:, :], "qc")
        st = {"qst": 0}
        xr = x_in.rearrange("(pt p) j t -> p pt j t", p=128)
        yr = xccp.rearrange("j (pt p t) -> pt p j t", pt=NPT, p=128)
        for ci in range(S_TOT // TCH):
            j0 = ci * JPC
            s.wait("vector", "qst", st["qst"])
            s.wait("scalar", "qst", st["qst"])
            if ci == 0:
                cv_z = s.do("vector", lambda e: e.memset(xin[:, :, 0:3], 0.0))
                s.wait("sync", "cv", cv_z)
            else:
                s.dma("sync", xin[:, :, 0:3], xr[:, :, j0 - 1, TPC - 3:],
                      "qx", waits=[("cv", s.n("cv"))])
            for pt in range(NPT):
                x_cnt = s.dma("sync",
                              xin[:, pt, 3:].rearrange("p (j t) -> p j t",
                                                       j=JPC),
                              xr[:, pt, j0:j0 + JPC, :], "qx",
                              waits=([("cv", s.n("cv"))] if ci == 0 and pt == 0
                                     else []))
            s.wait("vector", "qx", x_cnt)
            s.wait("vector", "qc", s.n("qc"))
            s.wait("scalar", "qc", s.n("qc"))
            for pt in range(NPT):
                s.do("vector", lambda e, pt=pt: e.tensor_scalar(
                    accb[:, pt, :], xin[:, pt, 3:3 + TCH],
                    wc[:, pt * DCONV + 3:pt * DCONV + 4], None, ALU.mult))
                for k in (2, 1, 0):
                    s.do("vector", lambda e, pt=pt, k=k: e.scalar_tensor_tensor(
                        accb[:, pt, :], xin[:, pt, k:k + TCH],
                        wc[:, pt * DCONV + k:pt * DCONV + k + 1],
                        accb[:, pt, :], ALU.mult, ALU.add))
                cv_a = s.n("cv")
                s.wait("scalar", "cv", cv_a)
                s.do("scalar", lambda e, pt=pt: e.activation(
                    ov[:, pt, :], accb[:, pt, :], AF.Silu,
                    bias=cb[:, pt:pt + 1]))
            s.wait("gpsimd", "cs", s.n("cs"))
            for pt in range(NPT):
                s.dma("gpsimd", yr[pt][:, j0:j0 + JPC, :],
                      ov[:, pt, :].rearrange("p (j t) -> p j t", j=JPC), "qst")
            st["qst"] = s.n("qst")


def build_mamba_b():
    nc = bass.Bass("TRN2", target_bir_lowering=False)
    CH = DI // NCORES
    NPT = CH // 128
    TCH = 4096
    x_in = nc.dram_tensor("x", [CH, S_TOT], F32, kind="ExternalInput").ap()
    wc_d = nc.dram_tensor("wc", [128, NPT * DCONV], F32, kind="ExternalInput").ap()
    cb_d = nc.dram_tensor("cb", [128, NPT], F32, kind="ExternalInput").ap()
    y_out = nc.dram_tensor("y", [CH, S_TOT], F32, kind="ExternalOutput").ap()
    s = Sched(nc)
    with s.es:
        _mk_sems(s)
        emit_mamba_b(s, dict(x_in=x_in, wc_d=wc_d, cb_d=cb_d, y_out=y_out))
        s.wait("gpsimd", "qst", s.n("qst"))
        s.finalize()
    return nc



def emit_mamba_c(s, io):
    xc_in = io["xc_in"]
    xpt_d = io["xpt_d"]; dpt_d = io["dpt_d"]; dtb_d = io["dtb_d"]
    b_out = io["b_out"]; c_out = io["c_out"]; d_out = io["d_out"]
    NK = DI // 128
    with s.scope():
        xcb = s.sbuf("xcb", [128, NK, TC], BF16)
        xpt = s.sbuf("sb_xpt", [128, NK, 96], BF16)
        dpt = s.sbuf("sb_dpt", [DTR, DI], BF16)
        xdbb = s.sbuf("xdbb", [DTR, TC], BF16)
        dtb = s.sbuf("sb_dtb", [128, NK])
        xdbl = s.sbuf("xdbl", [96, TC])
        dl = s.sbuf("dl", [128, NK, TC])
        sp1 = s.sbuf("sp1", [128, TC])
        sp2 = s.sbuf("sp2", [128, TC])
        ps_x = s.psum("ps_x", [96, TC])
        ps_mm = [s.psum("ps_mm0", [128, TC]), s.psum("ps_mm1", [128, TC])]
        c1 = s.const(1.0)
        cn1 = s.const(-1.0)
        s.dma("sync", xpt[:, :, :], fm(xpt_d), "qc")
        s.dma("sync", dpt[:, :], dpt_d[:, :], "qc")
        s.dma("sync", dtb[:, :], dtb_d[:, :], "qc")
        qconst = s.n("qc")
        st = {"qst": 0, "last_read": {0: 0, 1: 0}}
        for tc_i in range(NTC):
            xsl = slice(tc_i * TC, (tc_i + 1) * TC)
            x_cnt = s.dma("sync", xcb[:, :, :], fm(xc_in)[:, :, xsl], "qx",
                          waits=[("cpe", s.n("cpe"))])
            s.wait("tensor", "qx", x_cnt)
            s.wait("tensor", "qc", qconst)
            s.wait("scalar", "qc", qconst)
            s.wait("tensor", "cv", s.n("cv"))
            for kt in range(NK):
                s.do("tensor", lambda e, kt=kt: e.matmul(
                    ps_x[:, :], xpt[:, kt, :], xcb[:, kt, :],
                    start=(kt == 0), stop=(kt == NK - 1)),
                    inc=("cpe" if kt == NK - 1 else None))
            cpe_x = s.n("cpe")
            s.wait("vector", "cpe", cpe_x)
            s.wait("vector", "qst", st["qst"])
            cv_xd = s.do("vector", lambda e: e.tensor_copy(xdbl[:, :], ps_x[:, :]))
            cv_xd = s.do("vector", lambda e: e.tensor_copy(xdbb[:DTR, :],
                                                           xdbl[:DTR, :]))
            s.wait("gpsimd", "cv", cv_xd)
            s.dma("gpsimd", b_out[:, xsl], xdbl[64:80, :], "qst")
            s.dma("gpsimd", c_out[:, xsl], xdbl[80:96, :], "qst")
            for mc in range(NK):
                pp = mc % 2
                s.wait("tensor", "cv", cv_xd)
                s.wait("tensor", "cs", st["last_read"][pp])
                s.do("tensor", lambda e, pp=pp, mc=mc: e.matmul(
                    ps_mm[pp][:, :], dpt[:, mc * 128:(mc + 1) * 128],
                    xdbb[:DTR, :], start=True, stop=True), inc="cpe")
                cpe_d = s.n("cpe")
                s.wait("scalar", "cpe", cpe_d)
                s.wait("scalar", "cv", s.n("cv"))
                s.do("scalar", lambda e, pp=pp, mc=mc: e.activation(
                    sp1[:, :], ps_mm[pp][:, :], AF.Abs, bias=dtb[:, mc:mc + 1]))
                s.do("scalar", lambda e: e.activation(
                    sp1[:, :], sp1[:, :], AF.Exp, scale=cn1))
                s.do("scalar", lambda e: e.activation(
                    sp1[:, :], sp1[:, :], AF.Ln, bias=c1))
                cs_r = s.do("scalar", lambda e, pp=pp, mc=mc: e.activation(
                    sp2[:, :], ps_mm[pp][:, :], AF.Relu, bias=dtb[:, mc:mc + 1]))
                st["last_read"][pp] = cs_r
                s.wait("vector", "cs", cs_r)
                s.do("vector", lambda e, mc=mc: e.tensor_add(
                    dl[:, mc, :], sp1[:, :], sp2[:, :]))
            s.wait("gpsimd", "cv", s.n("cv"))
            s.dma("gpsimd", fm(d_out)[:, :, xsl], dl[:, :, :], "qst")
            st["qst"] = s.n("qst")


def build_mamba_c():
    nc = bass.Bass("TRN2", target_bir_lowering=False)
    xc_in = nc.dram_tensor("xc", [DI, TPC], F32, kind="ExternalInput").ap()
    xpt_d = nc.dram_tensor("xprojt", [DI, 96], F32, kind="ExternalInput").ap()
    dpt_d = nc.dram_tensor("dprojt", [DTR, DI], F32, kind="ExternalInput").ap()
    dtb_d = nc.dram_tensor("dtbc", [128, DI // 128], F32, kind="ExternalInput").ap()
    b_out = nc.dram_tensor("bm", [NST, TPC], F32, kind="ExternalOutput").ap()
    c_out = nc.dram_tensor("cm", [NST, TPC], F32, kind="ExternalOutput").ap()
    d_out = nc.dram_tensor("delta", [DI, TPC], F32, kind="ExternalOutput").ap()
    NK = DI // 128
    s = Sched(nc)
    with s.es:
        _mk_sems(s)
        emit_mamba_c(s, dict(xc_in=xc_in, xpt_d=xpt_d, dpt_d=dpt_d, dtb_d=dtb_d, b_out=b_out, c_out=c_out, d_out=d_out))
        s.wait("gpsimd", "qst", s.n("qst"))
        s.finalize()
    return nc



def emit_mamba_d(s, io):
    dl_in = io["dl_in"]
    xc_in = io["xc_in"]
    z_in = io["z_in"]
    b_in = io["b_in"]
    c_in = io["c_in"]
    a_d = io["a_d"]; dc_d = io["dc_d"]; sel_d = io["sel_d"]
    ycp = io["y_out"]
    CH = DI // NCORES
    NPT = CH // 128
    TCH = 4096
    NSB = TCH // TC
    JPC = TCH // TPC
    with s.scope():
        dlb = s.sbuf("dlb", [128, TCH])
        xcb = s.sbuf("xcb", [128, TCH], BF16)
        zb = s.sbuf("zb", [128, TCH], BF16)
        ub = s.sbuf("ub", [128, TCH], BF16)
        dab = s.sbuf("dab", [128, TCH])
        dbb = s.sbuf("dbb", [128, TCH], BF16)
        bbp = [s.sbuf("bbp%d" % i, [128, TC], BF16) for i in range(2)]
        hb = s.sbuf("hb", [128, TCH])
        yb = s.sbuf("yb", [128, TCH])
        ybb = s.sbuf("ybb", [128, TCH], BF16)
        bsb = s.sbuf("bsb", [NST, TCH])
        csb = s.sbuf("csb", [NST, TCH])
        bsbb = s.sbuf("bsbb", [NST, TCH], BF16)
        csbb = s.sbuf("csbb", [NST, TCH], BF16)
        acol = s.sbuf("sb_acol", [128, NPT * NST])
        dcol = s.sbuf("sb_dcol", [128, NPT])
        sel = s.sbuf("sb_sel16", [NST, NST * 128], BF16)
        carry = s.sbuf("carry", [128, NPT * NST])
        ps_b = [s.psum("ps_b0", [128, TC]), s.psum("ps_b1", [128, TC])]
        s.dma("sync", acol[:, :], a_d[:, :], "qc")
        s.dma("sync", dcol[:, :], dc_d[:, :], "qc")
        s.dma("sync", sel[:, :], sel_d[:, :], "qc")
        qconst = s.n("qc")
        cv_c0 = s.do("vector", lambda e: e.memset(carry[:, :], 0.0))
        s.wait("scalar", "qc", qconst)
        cs_aexp = s.do("scalar", lambda e: e.activation(acol[:, :], acol[:, :],
                                                        AF.Exp))
        s.wait("vector", "cs", cs_aexp)
        cv_aneg = s.do("vector", lambda e: e.tensor_scalar(
            acol[:, :], acol[:, :], -1.0, None, ALU.mult))
        s.wait("scalar", "cv", cv_aneg)
        st = {"qst": 0, "pbpar": 0, "pb_read": {0: ("cv", 0), 1: ("cv", 0)},
              "bb_war": {0: 0, 1: 0}}
        dlr = dl_in.rearrange("(pt p) j t -> pt p j t", p=128)
        xcr = xc_in.rearrange("(pt p) j t -> pt p j t", p=128)
        zr = z_in.rearrange("(pt p) j t -> pt p j t", p=128)
        yr = ycp.rearrange("j (pt p t) -> pt p j t", pt=NPT, p=128)
        for pt in range(NPT):
            for ci in range(S_TOT // TCH):
                j0 = ci * JPC
                jsl = slice(j0, j0 + JPC)
                s.wait("vector", "qst", st["qst"])
                s.wait("scalar", "qst", st["qst"])
                v3 = lambda t: t.rearrange("p (j q) -> p j q", j=JPC)
                s.dma("sync", v3(dlb[:, :]), dlr[pt, :, jsl, :], "qx",
                      waits=[("cs", s.n("cs")), ("cv", s.n("cv")),
                             ("cpe", s.n("cpe"))])
                s.dma("sync", v3(xcb[:, :]), xcr[pt, :, jsl, :], "qx")
                s.dma("sync", v3(zb[:, :]), zr[pt, :, jsl, :], "qx")
                s.dma("sync", v3(bsb[:NST, :]), b_in[:, jsl, :], "qx")
                x_cnt = s.dma("sync", v3(csb[:NST, :]), c_in[:, jsl, :], "qx")
                s.wait("vector", "qx", x_cnt)
                s.wait("vector", "cpe", s.n("cpe"))
                s.do("vector", lambda e: e.tensor_copy(bsbb[:, :], bsb[:, :]))
                cv_bc = s.do("vector", lambda e: e.tensor_copy(csbb[:, :],
                                                               csb[:, :]))
                s.do("vector", lambda e: e.tensor_mul(ub[:, :], dlb[:, :],
                                                      xcb[:, :]))
                s.wait("tensor", "cv", cv_bc)
                for n in range(NST):
                    cidx = pt * NST + n
                    s.wait("scalar", "qx", x_cnt)
                    s.wait("scalar", "cv", s.n("cv"))
                    cs_da = s.do("scalar", lambda e, cidx=cidx: e.activation(
                        dab[:, :], dlb[:, :], AF.Exp,
                        scale=acol[:, cidx:cidx + 1]))
                    for sb in range(NSB):
                        pb = st["pbpar"]; st["pbpar"] ^= 1
                        ssl = slice(sb * TC, (sb + 1) * TC)
                        s.wait("tensor", *st["pb_read"][pb])
                        if n == 0 and sb == 0 and pt == 0 and ci == 0:
                            s.wait("tensor", "qc", qconst)
                        cpe_b = s.do("tensor", lambda e, pb=pb, n=n, ssl=ssl:
                            e.matmul(ps_b[pb][:, :],
                                     sel[:NST, n * 128:(n + 1) * 128],
                                     bsbb[:NST, ssl], start=True, stop=True),
                            inc="cpe")
                        s.wait("scalar", "cpe", cpe_b)
                        s.wait("scalar", "cv", st["bb_war"][pb])
                        cs_ev = s.do("scalar", lambda e, pb=pb: e.activation(
                            bbp[pb][:, :], ps_b[pb][:, :], AF.Copy))
                        st["pb_read"][pb] = ("cs", cs_ev)
                        s.wait("vector", "cs", cs_ev)
                        st["bb_war"][pb] = s.do("vector",
                            lambda e, pb=pb, ssl=ssl: e.tensor_mul(
                                dbb[:, ssl], ub[:, ssl], bbp[pb][:, :]))
                    s.wait("vector", "cs", cs_da)
                    s.do("vector", lambda e, cidx=cidx: e.tensor_tensor_scan(
                        hb[:, :], dab[:, :], dbb[:, :],
                        carry[:, cidx:cidx + 1], ALU.mult, ALU.add))
                    s.do("vector", lambda e, cidx=cidx: e.tensor_copy(
                        carry[:, cidx:cidx + 1], hb[:, TCH - 1:TCH]))
                    for sb in range(NSB):
                        pb = st["pbpar"]; st["pbpar"] ^= 1
                        ssl = slice(sb * TC, (sb + 1) * TC)
                        s.wait("tensor", *st["pb_read"][pb])
                        cpe_c = s.do("tensor", lambda e, pb=pb, n=n, ssl=ssl:
                            e.matmul(ps_b[pb][:, :],
                                     sel[:NST, n * 128:(n + 1) * 128],
                                     csbb[:NST, ssl], start=True, stop=True),
                            inc="cpe")
                        s.wait("vector", "cpe", cpe_c)
                        st["pb_read"][pb] = ("cv", s.do("vector",
                            lambda e, pb=pb, ssl=ssl: e.tensor_mul(
                                dbb[:, ssl], hb[:, ssl], ps_b[pb][:, :])))
                    if n == 0:
                        s.do("vector", lambda e: e.tensor_copy(yb[:, :],
                                                               dbb[:, :]))
                    else:
                        s.do("vector", lambda e: e.tensor_add(yb[:, :], yb[:, :],
                                                              dbb[:, :]))
                s.do("vector", lambda e, pt=pt: e.scalar_tensor_tensor(
                    yb[:, :], xcb[:, :], dcol[:, pt:pt + 1], yb[:, :],
                    ALU.mult, ALU.add))
                cs_z = s.do("scalar", lambda e: e.activation(zb[:, :], zb[:, :],
                                                             AF.Silu))
                s.wait("vector", "cs", cs_z)
                cv_y = s.do("vector", lambda e: e.tensor_mul(ybb[:, :], yb[:, :],
                                                             zb[:, :]))
                s.wait("gpsimd", "cv", cv_y)
                s.dma("gpsimd", yr[pt, :, jsl, :], v3(ybb[:, :]), "qst")
                st["qst"] = s.n("qst")


def build_mamba_d():
    nc = bass.Bass("TRN2", target_bir_lowering=False)
    CH = DI // NCORES
    NPT = CH // 128
    TCH = 4096
    NSB = TCH // TC
    dl_in = nc.dram_tensor("delta", [CH, S_TOT], F32, kind="ExternalInput").ap()
    xc_in = nc.dram_tensor("xc", [CH, S_TOT], F32, kind="ExternalInput").ap()
    z_in = nc.dram_tensor("z", [CH, S_TOT], F32, kind="ExternalInput").ap()
    b_in = nc.dram_tensor("bm", [NST, S_TOT], F32, kind="ExternalInput").ap()
    c_in = nc.dram_tensor("cm", [NST, S_TOT], F32, kind="ExternalInput").ap()
    a_d = nc.dram_tensor("acol", [128, NPT * NST], F32, kind="ExternalInput").ap()
    dc_d = nc.dram_tensor("dcol", [128, NPT], F32, kind="ExternalInput").ap()
    sel_d = nc.dram_tensor("sel16", [NST, NST * 128], F32, kind="ExternalInput").ap()
    y_out = nc.dram_tensor("y", [CH, S_TOT], F32, kind="ExternalOutput").ap()
    s = Sched(nc)
    with s.es:
        _mk_sems(s)
        emit_mamba_d(s, dict(dl_in=dl_in, xc_in=xc_in, z_in=z_in, b_in=b_in, c_in=c_in, a_d=a_d, dc_d=dc_d, sel_d=sel_d, y_out=y_out))
        s.wait("gpsimd", "qst", s.n("qst"))
        s.finalize()
    return nc



def emit_mamba_e(s, io):
    NK = DI // 128
    NF = D // 128
    x_in = io["x_in"]
    y_in = io["y_in"]
    wt = io["wt"]
    y_out = io["y_out"]
    with s.scope():
        X_cs = [s.sbuf("X_c%d" % i, [128, NF, TC]) for i in range(2)]
        ymb2 = [s.sbuf("ymbb%d" % i, [128, NK, TC], BF16) for i in range(2)]
        wb = [s.sbuf("wb0", [128, NK, 128], BF16),
              s.sbuf("wb1", [128, NK, 128], BF16)]
        ps_mm = [s.psum("ps_mm0", [128, TC]), s.psum("ps_mm1", [128, TC])]
        st = {"wpar": 0, "last_w": {0: 0, 1: 0}, "last_use": {0: 0, 1: 0},
              "qst": [0, 0], "xw": [0, 0], "ymw": [0, 0]}
        for tc_i in range(NTC):
            par = tc_i % 2
            X_c = X_cs[par]
            ymbb = ymb2[par]
            xsl = slice(tc_i * TC, (tc_i + 1) * TC)
            s.dma("sync", X_c[:, :, :], fm(x_in)[:, :, xsl], "qx",
                  waits=[("qst", st["qst"][par]), ("cv", st["xw"][par])])
            x_cnt = s.dma("sync", ymbb[:, :, :], fm(y_in)[:, :, xsl], "qx",
                          waits=[("cpe", st["ymw"][par])])
            for mc in range(NF):
                wp = st["wpar"]; st["wpar"] ^= 1
                w_cnt = s.dma("sync", fl(wb[wp]),
                              wt[mc, :, :], "qw%d" % wp,
                              waits=[("cpe", st["last_w"][wp])])
                s.wait("tensor", "qw%d" % wp, w_cnt)
                s.wait("tensor", "qx", x_cnt)
                s.wait("tensor", "cv", st["last_use"][wp])
                for kt in range(NK):
                    s.do("tensor", lambda e, wp=wp, kt=kt, ymbb=ymbb: e.matmul(
                        ps_mm[wp][:, :], wb[wp][:, kt, :], ymbb[:, kt, :],
                        start=(kt == 0), stop=(kt == NK - 1)),
                        inc=("cpe" if kt == NK - 1 else None))
                cpe_mm = s.n("cpe")
                st["last_w"][wp] = cpe_mm
                s.wait("vector", "cpe", cpe_mm)
                if mc == 0:
                    s.wait("vector", "qx", x_cnt)
                    s.wait("vector", "qst", st["qst"][par])
                st["last_use"][wp] = s.do("vector",
                    lambda e, wp=wp, mc=mc, X_c=X_c: e.tensor_add(
                        X_c[:, mc, :], X_c[:, mc, :], ps_mm[wp][:, :]))
            st["ymw"][par] = s.n("cpe")
            st["xw"][par] = s.n("cv")
            s.wait("gpsimd", "cv", s.n("cv"))
            s.dma("gpsimd", fm(y_out)[:, :, xsl], X_c[:, :, :], "qst")
            st["qst"][par] = s.n("qst")


def build_mamba_e():
    nc = bass.Bass("TRN2", target_bir_lowering=False)
    x_in = nc.dram_tensor("x", [D, TPC], F32, kind="ExternalInput").ap()
    y_in = nc.dram_tensor("ym", [DI, TPC], F32, kind="ExternalInput").ap()
    wt = nc.dram_tensor("oprojt", [DI, D], F32, kind="ExternalInput").ap()
    y_out = nc.dram_tensor("y", [D, TPC], F32, kind="ExternalOutput").ap()
    NK = DI // 128
    NF = D // 128
    s = Sched(nc)
    with s.es:
        _mk_sems(s)
        emit_mamba_e(s, dict(x_in=x_in, y_in=y_in, wt=wt, y_out=y_out))
        s.wait("gpsimd", "qst", s.n("qst"))
        s.finalize()
    return nc







def emit_prep(s, io):
    jobs = io["jobs"]
    tjobs = io["tjobs"]
    ffnj = io["ffnj"]
    oc_d = io["oc_d"]
    with s.scope():
        CW = 4096
        NB = 3
        stq = ["qs0", "qs1", "qz"]
        wf = [s.sbuf("pw%d" % i, [128, CW]) for i in range(NB)]
        wb = [s.sbuf("pb%d" % i, [128, CW], BF16) for i in range(NB)]
        oc = s.sbuf("poc", [128, 1])
        acc = s.sbuf("pacc", [128, 32])
        ps_as = [s.psum("ps_pa0", [128, 512]), s.psum("ps_pa1", [128, 512])]
        qc0 = s.dma("sync", oc[:, :], oc_d[:, :], "qc")
        par = {"p": 0, "done": {i: ("cv", 0) for i in range(NB)},
               "st": {i: 0 for i in range(NB)}}
        eng_flip = {"e": 0}
        for (src_ap, dst_ap, R, C) in jobs:
            assert C <= CW
            r0 = 0
            while r0 < R:
                rr = min(128, R - r0)
                p = par["p"]; par["p"] = (par["p"] + 1) % NB
                ds, dc = par["done"][p]
                w_cnt = s.dma("sync", wf[p][:rr, :C], src_ap[r0:r0 + rr, :],
                              "qw%d" % p, waits=[(ds, dc)])
                eng = "scalar" if eng_flip["e"] == 0 else "vector"
                eng_flip["e"] ^= 1
                s.wait(eng, "qw%d" % p, w_cnt)
                s.wait(eng, stq[p], par["st"][p])
                if eng == "scalar":
                    c = s.do("scalar", lambda e, p=p, rr=rr, C=C:
                             e.copy(wb[p][:rr, :C], wf[p][:rr, :C]))
                    par["done"][p] = ("cs", c)
                else:
                    c = s.do("vector", lambda e, p=p, rr=rr, C=C:
                             e.tensor_copy(wb[p][:rr, :C], wf[p][:rr, :C]))
                    par["done"][p] = ("cv", c)
                par["st"][p] = s.dma("gpsimd", dst_ap[r0:r0 + rr, :],
                                     wb[p][:rr, :C], stq[p],
                                     waits=[par["done"][p]])
                r0 += rr
        for (src_ap, dst3d, R, C, nkt) in tjobs:
            nmc = C // 128
            for rb in range(R // 128):
                r0 = rb * 128
                p = par["p"]; par["p"] = (par["p"] + 1) % NB
                ds, dc = par["done"][p]
                w_cnt = s.dma("sync", wf[p][:128, :C], src_ap[r0:r0 + 128, :],
                              "qw%d" % p, waits=[(ds, dc)])
                eng = "scalar" if eng_flip["e"] == 0 else "vector"
                eng_flip["e"] ^= 1
                s.wait(eng, "qw%d" % p, w_cnt)
                s.wait(eng, stq[p], par["st"][p])
                if eng == "scalar":
                    c = s.do("scalar", lambda e, p=p, C=C:
                             e.copy(wb[p][:128, :C], wf[p][:128, :C]))
                    par["done"][p] = ("cs", c)
                else:
                    c = s.do("vector", lambda e, p=p, C=C:
                             e.tensor_copy(wb[p][:128, :C], wf[p][:128, :C]))
                    par["done"][p] = ("cv", c)
                sec, kt = rb // nkt, rb % nkt
                par["st"][p] = s.dma(
                    "gpsimd",
                    dst3d[sec * nmc:(sec + 1) * nmc, :,
                          kt * 128:(kt + 1) * 128].rearrange("a q m -> q a m"),
                    wb[p][:128, :C].rearrange("q (a m) -> q a m", m=128),
                    stq[p], waits=[par["done"][p]])
        last_ac = {0: 0, 1: 0}
        for (src_ap, dst_ap, K, M, acol_dst, ascale) in ffnj:
            nkt = K // 128
            nmc = M // 128
            s.wait("scalar", "qst", s.n("qst"))
            for mc in range(nmc):
                p = par["p"]; par["p"] = (par["p"] + 1) % NB
                ds, dc = par["done"][p]
                w_cnt = s.dma("sync",
                              wf[p][:, :nkt * 128].rearrange(
                                  "q (kt m) -> q kt m", m=128),
                              src_ap[:, mc * 128:(mc + 1) * 128]
                              .rearrange("(kt q) m -> q kt m", q=128),
                              "qw%d" % p,
                              waits=[(ds, dc), ("cpe", s.n("cpe"))])
                s.wait("scalar", "qw%d" % p, w_cnt)
                s.wait("scalar", stq[p], par["st"][p])
                cs_sg = s.do("scalar", lambda e, p=p, n=nkt * 128: e.sign(
                    wb[p][:, :n], wf[p][:, :n]))
                s.wait("vector", "cs", cs_sg)
                cv_ab = s.do("vector", lambda e, p=p, n=nkt * 128: e.tensor_mul(
                    wf[p][:, :n], wf[p][:, :n], wb[p][:, :n]))
                s.wait("tensor", "cv", cv_ab)
                if mc == 0:
                    s.wait("tensor", "qc", qc0)
                s.wait("tensor", "cs", last_ac[mc % 2])
                for kt in range(nkt):
                    s.do("tensor", lambda e, p=p, kt=kt, mc=mc: e.matmul(
                        ps_as[mc % 2][:, :1],
                        wf[p][:, kt * 128:(kt + 1) * 128], oc[:, :1],
                        start=(kt == 0), stop=(kt == nkt - 1),
                        skip_group_check=True),
                        inc=("cpe" if kt == nkt - 1 else None))
                cpe_a = s.n("cpe")
                par["done"][p] = ("cpe", cpe_a)
                s.wait("scalar", "cpe", cpe_a)
                last_ac[mc % 2] = s.do("scalar",
                    lambda e, mc=mc, sc=s.const(ascale): e.activation(
                        acc[:, mc:mc + 1], ps_as[mc % 2][:, :1],
                        AF.Identity, scale=sc))
                par["st"][p] = s.dma("gpsimd", dst_ap[mc, :, :],
                                     wb[p][:, :nkt * 128],
                                     stq[p], waits=[("cs", cs_sg)])
            s.wait("gpsimd", "cs", s.n("cs"))
            s.dma("gpsimd", acol_dst, acc[:, :nmc], "qst")


def _cc(s, kind, in_ap, out_ap):
    from concourse.aot_env import allow_ici_all_to_all

    def f(e, i=in_ap, o=out_ap, k=kind):
        with allow_ici_all_to_all():
            return e.collective_compute(k, ALU.bypass,
                                        [list(range(NCORES))], [i], [o])
    return s.do("gpsimd", f, inc="cc", amt=1)


def build_fused():
    nc = bass.Bass("TRN2", target_bir_lowering=False)
    CH = DI // NCORES
    NPT = CH // 128

    def ein(name, shape):
        return nc.dram_tensor(name, shape, F32, kind="ExternalInput").ap()

    def scr(name, shape):
        return nc.dram_tensor(name, shape, F32).ap()

    def scrb(name, shape):
        return nc.dram_tensor(name, shape, BF16).ap()

    x0 = ein("x", [D, TPC])
    w1t = ein("w1t", [D, H_FFN]); w2t = ein("w2t", [H_FFN, D])
    b1c = ein("b1c", [128, H_FFN // 128]); b2c = ein("b2c", [128, D // 128])
    oc = ein("ones_col", [128, 1]); orow = ein("ones_row", [1, 128])
    wts_a = {nm: ein(nm, [D, D]) for nm in ("wp1t", "wp2t", "cvt", "wat", "wxt")}
    brow_a = ein("brow_a", [1, 5 * D]); lamc = ein("lamc", [128, 8])
    wp3t = ein("wp3t", [D, D]); brow_p3 = ein("brow_p3", [1, D])
    w1all = ein("w1all", [NEXP * D, D]); w2all = ein("w2all", [NEXP * D, D])
    gw = ein("gw", [D, NEXP]); gbrow = ein("gbrow", [1, NEXP])
    eb1c = ein("eb1c", [128, NEXP * 8]); eb2c = ein("eb2c", [128, NEXP * 8])
    g1c = ein("g1c", [128, NEXP * 8]); be1c = ein("be1c", [128, NEXP * 8])
    g2c = ein("g2c", [128, NEXP * 8]); be2c = ein("be2c", [128, NEXP * 8])
    sel4 = ein("sel4", [NEXP, NEXP * 128]); ident = ein("ident", [128, 128])
    iprojt = ein("iprojt", [D, 2 * DI])
    wc = ein("wc", [128, NPT * DCONV]); cb = ein("cb", [128, NPT])
    xprojt = ein("xprojt", [DI, 96]); dprojt = ein("dprojt", [DTR, DI])
    dtbc = ein("dtbc", [128, DI // 128])
    acol = ein("acol", [128, NPT * NST]); dcol = ein("dcol", [128, NPT])
    sel16 = nc.dram_tensor("sel16", [NST, NST * 128], BF16,
                           kind="ExternalInput").ap()
    oprojt = ein("oprojt", [DI, D])
    y_fin = nc.dram_tensor("y", [D, TPC], F32, kind="ExternalOutput").ap()

    xd = [scr("xd%d" % i, [D, TPC]) for i in range(1, 6)]
    x1g_d = scrb("x1g_d", [D, TPC])
    av_d = scr("av_d", [D, TPC]); gv_d = scrb("gv_d", [D, TPC])
    exg_l = scr("exg_l", [128, 16])
    exg_all = scr("exg_all", [NCORES * 128, 16])
    msk_in = ein("msk", [128, NCORES])
    def scrb(name, shape):
        return nc.dram_tensor(name, shape, BF16).ap()

    xzp = scrb("xzp", [NCORES, 2 * CH * TPC])
    xzc2 = scrb("xzc2", [NCORES, 2 * CH * TPC])
    xccp = scrb("xccp", [NCORES, CH * TPC])
    xct2 = scrb("xct2", [NCORES, CH * TPC])
    bm_d = scr("bm_d", [NST, TPC]); cm_d = scr("cm_d", [NST, TPC])
    bmg2 = scr("bmg2", [NCORES, NST * TPC]); cmg2 = scr("cmg2", [NCORES, NST * TPC])
    delta_d = scr("delta_d", [DI, TPC])
    dlc2 = scr("dlc2", [NCORES, CH * TPC])
    ycp = scrb("ycp", [NCORES, CH * TPC])
    yt2 = scrb("yt2", [NCORES, CH * TPC])

    bw1 = scrb("bw1", [H_FFN // 128, 128, D])
    bw2 = scrb("bw2", [D // 128, 128, H_FFN])
    a1sc = scr("a1sc", [128, H_FFN // 128]); a2sc = scr("a2sc", [128, D // 128])
    bwts_a = {nm: scrb("b" + nm, [D // 128, 128, D])
              for nm in ("wp1t", "wp2t", "cvt", "wat", "wxt")}
    bwp3 = scrb("bwp3", [D // 128, 128, D])
    bw1all = scrb("bw1all", [NEXP * (D // 128), 128, D])
    bw2all = scrb("bw2all", [NEXP * (D // 128), 128, D])
    bgw = scrb("bgw", [D, NEXP])
    biproj = scrb("biproj", [2 * DI // 128, 128, D])
    bxproj = scrb("bxproj", [DI, 96]); bdproj = scrb("bdproj", [DTR, DI])
    boproj = scrb("boproj", [D // 128, 128, DI])

    browb = scrb("browb", [1, 5 * D])
    prep_jobs = [(brow_a[:, :4096], browb[:, :4096], 1, 4096),
                 (brow_a[:, 4096:], browb[:, 4096:], 1, 1024),
                 (gw, bgw, D, NEXP),
                 (xprojt, bxproj, DI, 96), (dprojt, bdproj, DTR, DI)]
    prep_tjobs = [(wts_a[nm], bwts_a[nm], D, D, D // 128)
                  for nm in ("wp1t", "wp2t", "cvt", "wat", "wxt")]
    prep_tjobs += [(wp3t, bwp3, D, D, D // 128),
                   (w1all, bw1all, NEXP * D, D, D // 128),
                   (w2all, bw2all, NEXP * D, D, D // 128),
                   (iprojt, biproj, D, 2 * DI, D // 128),
                   (oprojt, boproj, DI, D, DI // 128)]
    prep_ffnj = [(w1t, bw1, D, H_FFN, a1sc, 1.0 / D),
                 (w2t, bw2, H_FFN, D, a2sc, 0.5 / H_FFN)]

    ffn_io = dict(bw1=bw1, bw2=bw2, a1c_d=a1sc, a2c_d=a2sc,
                  b1c_d=b1c, b2c_d=b2c, oc_d=oc, or_d=orow)
    moe_io = dict(w1all=bw1all, w2all=bw2all, gw_d=bgw, gbrow_d=gbrow,
                  b1c_d=eb1c, b2c_d=eb2c, g1c_d=g1c, be1c_d=be1c,
                  g2c_d=g2c, be2c_d=be2c, sel4_d=sel4, id_d=ident,
                  oc_d=oc, or_d=orow)

    s = Sched(nc)
    with s.es:
        _mk_sems(s)
        B = s.barrier
        emit_prep(s, dict(jobs=prep_jobs, tjobs=prep_tjobs, ffnj=prep_ffnj,
                          oc_d=oc))
        B()
        emit_ffn0(s, dict(x_in=x0, y_out=xd[0], **ffn_io))
        B()
        emit_attn0a(s, dict(x_in=xd[0], x1_out=x1g_d, a_out=av_d, g_out=gv_d,
                            brow_d=browb, lam_d=lamc, oc_d=oc, or_d=orow,
                            wts=bwts_a, exg_out=exg_l))
        B()
        _cc(s, "AllGather", exg_l[:, :], exg_all)
        B()
        emit_attn0b(s, dict(x_in=xd[0], x1_in=x1g_d, a_in=av_d, g_in=gv_d,
                            exg_in=exg_all, msk_d=msk_in,
                            wp3t=bwp3, brow_d=brow_p3, y_out=xd[1]))
        B()
        emit_ffn0(s, dict(x_in=xd[1], y_out=xd[2], **ffn_io))
        B()
        emit_moe(s, dict(x_in=xd[2], y_out=xd[3], **moe_io))
        B()
        emit_mamba_a(s, dict(x_in=xd[3], wt=biproj, oc_d=oc, or_d=orow,
                             xzp=xzp))
        B()
        _cc(s, "AllToAll", xzp, xzc2)
        B()
        xzc3 = xzc2.rearrange("j (r t) -> r j t", r=2 * CH)
        emit_mamba_b(s, dict(x_in=xzc3[0:CH], wc_d=wc, cb_d=cb, y_out=xccp))
        B()
        _cc(s, "AllToAll", xccp, xct2)
        B()
        emit_mamba_c(s, dict(xc_in=xct2.rearrange("j (c t) -> (j c) t", c=CH),
                             xpt_d=bxproj, dpt_d=bdproj, dtb_d=dtbc,
                             b_out=bm_d, c_out=cm_d, d_out=delta_d))
        B()
        _cc(s, "AllGather", bm_d[:, :], bmg2)
        _cc(s, "AllGather", cm_d[:, :], cmg2)
        _cc(s, "AllToAll",
            delta_d.rearrange("(j c) t -> j c t", j=NCORES)
                   .rearrange("j c t -> j (c t)"), dlc2)
        B()
        emit_mamba_d(s, dict(
            dl_in=dlc2.rearrange("j (c t) -> c j t", c=CH),
            xc_in=xccp.rearrange("j (c t) -> c j t", c=CH),
            z_in=xzc3[CH:2 * CH],
            b_in=bmg2.rearrange("j (n t) -> n j t", n=NST),
            c_in=cmg2.rearrange("j (n t) -> n j t", n=NST),
            a_d=acol, dc_d=dcol, sel_d=sel16, y_out=ycp))
        B()
        _cc(s, "AllToAll", ycp, yt2)
        B()
        emit_mamba_e(s, dict(x_in=xd[3],
                             y_in=yt2.rearrange("j (c t) -> (j c) t", c=CH),
                             wt=boproj, y_out=xd[4]))
        B()
        emit_moe(s, dict(x_in=xd[4], y_out=y_fin, **moe_io))
        s.wait("gpsimd", "qst", s.n("qst"))
        s.finalize()
    return nc


def _fused_in_maps(inputs, st):
    CH = DI // NCORES
    shards = _shards_fm(st["x_fm"])
    if "moe_w1all" not in st:
        W1 = np.asarray(inputs["L1_ffn_W1"]); W2 = np.asarray(inputs["L1_ffn_W2"])
        st["moe_w1all"] = np.ascontiguousarray(W1.reshape(NEXP * D, D))
        st["moe_w2all"] = np.ascontiguousarray(W2.reshape(NEXP * D, D))
        st["moe_gbrow"] = np.ascontiguousarray(
            np.asarray(inputs["L1_ffn_gate_b"])[None, :])
        for nm, key in (("g1", "moe_g1c"), ("be1", "moe_be1c"),
                        ("g2", "moe_g2c"), ("be2", "moe_be2c"),
                        ("b1", "moe_b1c"), ("b2", "moe_b2c")):
            v = np.asarray(inputs["L1_ffn_" + nm])
            st[key] = np.ascontiguousarray(
                np.concatenate([_cols(v[e]) for e in range(NEXP)], axis=1))
        st["moe_sel4"] = np.ascontiguousarray(
            st["sel4"].transpose(1, 0, 2).reshape(NEXP, NEXP * 128))
        st["moe_gw"] = np.ascontiguousarray(np.asarray(inputs["L1_ffn_gate_W"]))
    if "mamba_wc" not in st:
        cw = np.asarray(inputs["L1_attn_conv_w"])[:, 0, :]
        cbv = np.asarray(inputs["L1_attn_conv_b"])
        A_log = np.asarray(inputs["L1_attn_A_log"])
        Dv = np.asarray(inputs["L1_attn_D"])
        st["mamba_wc"] = []; st["mamba_cb"] = []
        st["mamba_acol_raw"] = []; st["mamba_dcol"] = []
        for c in range(NCORES):
            wce = cw[c * CH:(c + 1) * CH]
            st["mamba_wc"].append(np.ascontiguousarray(
                wce.reshape(2, 128, DCONV).transpose(1, 0, 2).reshape(128, -1)))
            st["mamba_cb"].append(np.ascontiguousarray(
                cbv[c * CH:(c + 1) * CH].reshape(2, 128).T))
            al = A_log[c * CH:(c + 1) * CH]
            st["mamba_acol_raw"].append(np.ascontiguousarray(
                al.reshape(2, 128, NST).transpose(1, 0, 2).reshape(128, -1)))
            st["mamba_dcol"].append(np.ascontiguousarray(
                Dv[c * CH:(c + 1) * CH].reshape(2, 128).T))
        st["mamba_sel16"] = np.ascontiguousarray(
            st["sel16"].transpose(1, 0, 2).reshape(NST, NST * 128)
            .astype(ml_dtypes.bfloat16))
    common = dict(
        w1t=st.setdefault("ffn_w1t", _t(inputs["L0_ffn_w1"])),
        w2t=st.setdefault("ffn_w2t", _t(inputs["L0_ffn_w2"])),
        b1c=st.setdefault("ffn_b1c", _cols(inputs["L0_ffn_b1"])),
        b2c=st.setdefault("ffn_b2c", _cols(inputs["L0_ffn_b2"])),
        ones_col=st["ones_col"], ones_row=st["ones_row"],
        wp1t=st.setdefault("wp1t", _t(inputs["L0_attn_Wp1"])),
        wp2t=st.setdefault("wp2t", _t(inputs["L0_attn_Wp2"])),
        cvt=st.setdefault("cvt", _t(inputs["L0_attn_conv_w"])),
        wat=st.setdefault("wat", _t(inputs["L0_attn_Wa"])),
        wxt=st.setdefault("wxt", _t(inputs["L0_attn_Wx"])),
        brow_a=st.setdefault("attn_brow", np.ascontiguousarray(np.concatenate(
            [np.asarray(inputs["L0_attn_bp1"]), np.asarray(inputs["L0_attn_bp2"]),
             np.asarray(inputs["L0_attn_conv_b"]), np.asarray(inputs["L0_attn_ba"]),
             np.asarray(inputs["L0_attn_bx"])])[None, :])),
        lamc=st.setdefault("attn_lamc", _cols(inputs["L0_attn_Lam"])),
        wp3t=st.setdefault("wp3t", _t(inputs["L0_attn_Wp3"])),
        brow_p3=st.setdefault("attn_bp3r", np.ascontiguousarray(
            np.asarray(inputs["L0_attn_bp3"])[None, :])),
        w1all=st["moe_w1all"], w2all=st["moe_w2all"], gw=st["moe_gw"],
        gbrow=st["moe_gbrow"], eb1c=st["moe_b1c"], eb2c=st["moe_b2c"],
        g1c=st["moe_g1c"], be1c=st["moe_be1c"], g2c=st["moe_g2c"],
        be2c=st["moe_be2c"], sel4=st["moe_sel4"], ident=st["ident"],
        iprojt=st.setdefault("iprojt", _t(inputs["L1_attn_in_proj_w"])),
        xprojt=st.setdefault("xprojt", _t(inputs["L1_attn_x_proj_w"])),
        dprojt=st.setdefault("dprojt", _t(inputs["L1_attn_dt_proj_w"])),
        dtbc=st.setdefault("dtbc", _cols(inputs["L1_attn_dt_proj_b"])),
        sel16=st["mamba_sel16"],
        oprojt=st.setdefault("oprojt", _t(inputs["L1_attn_out_proj_w"])),
    )
    if "msk" not in st:
        mk = []
        for c in range(NCORES):
            m = np.zeros((128, NCORES), np.float32)
            m[:, c] = 1.0
            mk.append(m)
        st["msk"] = mk
    return [dict(x=shards[c], wc=st["mamba_wc"][c], cb=st["mamba_cb"][c],
                 acol=st["mamba_acol_raw"][c], dcol=st["mamba_dcol"][c],
                 msk=st["msk"][c],
                 **common) for c in range(NCORES)]


def kernel(**inputs):
    st = _prep_common(inputs)
    nc = _prog("fused", build_fused)
    in_maps = _fused_in_maps(inputs, st)
    res = _run(nc, in_maps)
    x = _gather_tok(res, "y")
    return np.ascontiguousarray(x.T)[None, :, :]


def kernel_unfused(**inputs):
    st = _prep_common(inputs)
    x = st["x_fm"]
    x = run_ffn0(x, inputs, st)
    x = run_attn0(x, inputs, st)
    x = run_ffn0(x, inputs, st)
    x = run_moe(x, inputs, st)
    x = run_mamba(x, inputs, st)
    x = run_moe(x, inputs, st)
    return np.ascontiguousarray(x.T)[None, :, :]


def _prep_common(inputs):
    st = {}
    x = np.asarray(inputs["L0_attn_x"])
    st["x_fm"] = np.ascontiguousarray(x[0].T)
    st["ones_col"] = np.ones((128, 1), np.float32)
    st["ones_row"] = np.ones((1, 128), np.float32)
    st["ident"] = np.eye(128, dtype=np.float32)
    sel4 = np.zeros((NEXP, 4, 128), np.float32)
    for e in range(NEXP):
        sel4[e, e, :] = 1.0
    st["sel4"] = sel4
    sel16 = np.zeros((NST, 16, 128), np.float32)
    for n in range(NST):
        sel16[n, n, :] = 1.0
    st["sel16"] = sel16
    return st


def _cols(v):
    v = np.asarray(v)
    return np.ascontiguousarray(v.reshape(-1, 128).T)


def _t(w):
    return np.ascontiguousarray(np.asarray(w).T)


def _shards_fm(a_fm):
    return [np.ascontiguousarray(a_fm[:, c * TPC:(c + 1) * TPC])
            for c in range(NCORES)]


_FFN_CACHE = {}


def run_ffn0(x_fm, inputs, st):
    if "nc" not in _FFN_CACHE:
        _FFN_CACHE["nc"] = build_ffn0()
    nc = _FFN_CACHE["nc"]
    w1t = st.setdefault("ffn_w1t", _t(inputs["L0_ffn_w1"]))
    w2t = st.setdefault("ffn_w2t", _t(inputs["L0_ffn_w2"]))
    b1c = st.setdefault("ffn_b1c", _cols(inputs["L0_ffn_b1"]))
    b2c = st.setdefault("ffn_b2c", _cols(inputs["L0_ffn_b2"]))
    shards = _shards_fm(x_fm)
    in_maps = [{"x": shards[c], "w1t": w1t, "w2t": w2t, "b1c": b1c, "b2c": b2c,
                "ones_col": st["ones_col"], "ones_row": st["ones_row"]}
               for c in range(NCORES)]
    res = run_bass_kernel_spmd(nc, in_maps, core_ids=list(range(NCORES)))
    return np.concatenate([res.results[c]["y"] for c in range(NCORES)], axis=1)


_CACHE = {}


def _prog(name, builder):
    if name not in _CACHE:
        _CACHE[name] = builder()
    return _CACHE[name]


def _chan_shards(a_fm, nrows):
    return [np.ascontiguousarray(a_fm[c * nrows:(c + 1) * nrows, :])
            for c in range(NCORES)]


def _gather_tok(res, key):
    return np.concatenate([res.results[c][key] for c in range(NCORES)], axis=1)


def _gather_chan(res, key):
    return np.concatenate([res.results[c][key] for c in range(NCORES)], axis=0)


def _run(nc, in_maps):
    return run_bass_kernel_spmd(nc, in_maps, core_ids=list(range(NCORES)))


def run_attn0(x_fm, inputs, st):
    nc_a = _prog("attn0a", build_attn0a)
    wts = {"wp1t": st.setdefault("wp1t", _t(inputs["L0_attn_Wp1"])),
           "wp2t": st.setdefault("wp2t", _t(inputs["L0_attn_Wp2"])),
           "cvt": st.setdefault("cvt", _t(inputs["L0_attn_conv_w"])),
           "wat": st.setdefault("wat", _t(inputs["L0_attn_Wa"])),
           "wxt": st.setdefault("wxt", _t(inputs["L0_attn_Wx"]))}
    brow = st.setdefault("attn_brow", np.ascontiguousarray(np.concatenate(
        [np.asarray(inputs["L0_attn_bp1"]), np.asarray(inputs["L0_attn_bp2"]),
         np.asarray(inputs["L0_attn_conv_b"]), np.asarray(inputs["L0_attn_ba"]),
         np.asarray(inputs["L0_attn_bx"])])[None, :]))
    lamc = st.setdefault("attn_lamc", _cols(inputs["L0_attn_Lam"]))
    shards = _shards_fm(x_fm)
    in_maps = [dict(x=shards[c], brow=brow, lamc=lamc,
                    ones_col=st["ones_col"], ones_row=st["ones_row"], **wts)
               for c in range(NCORES)]
    res = _run(nc_a, in_maps)
    x1g = _gather_tok(res, "x1g")
    av = _gather_tok(res, "av")
    gv = _gather_tok(res, "gv")

    nc_s = _prog("scan0", build_scan0)
    a_sh = _chan_shards(av, 128)
    g_sh = _chan_shards(gv, 128)
    res = _run(nc_s, [{"a": a_sh[c], "g": g_sh[c]} for c in range(NCORES)])
    hv = _gather_chan(res, "h")

    nc_b = _prog("attn0b", build_attn0b)
    wp3t = st.setdefault("wp3t", _t(inputs["L0_attn_Wp3"]))
    bp3r = st.setdefault("attn_bp3r",
                         np.ascontiguousarray(np.asarray(inputs["L0_attn_bp3"])[None, :]))
    x1_sh = _shards_fm(x1g)
    h_sh = _shards_fm(hv)
    res = _run(nc_b, [dict(x=shards[c], x1g=x1_sh[c], hv=h_sh[c], wp3t=wp3t,
                           brow=bp3r) for c in range(NCORES)])
    return _gather_tok(res, "y")


def run_moe(x_fm, inputs, st):
    nc = _prog("moe", build_moe)
    if "moe_w1all" not in st:
        W1 = np.asarray(inputs["L1_ffn_W1"])
        W2 = np.asarray(inputs["L1_ffn_W2"])
        st["moe_w1all"] = np.ascontiguousarray(W1.reshape(NEXP * D, D))
        st["moe_w2all"] = np.ascontiguousarray(W2.reshape(NEXP * D, D))
        st["moe_gbrow"] = np.ascontiguousarray(
            np.asarray(inputs["L1_ffn_gate_b"])[None, :])
        for nm, key in (("g1", "moe_g1c"), ("be1", "moe_be1c"),
                        ("g2", "moe_g2c"), ("be2", "moe_be2c"),
                        ("b1", "moe_b1c"), ("b2", "moe_b2c")):
            v = np.asarray(inputs["L1_ffn_" + nm])
            st[key] = np.ascontiguousarray(
                np.concatenate([_cols(v[e]) for e in range(NEXP)], axis=1))
        st["moe_sel4"] = np.ascontiguousarray(
            st["sel4"].transpose(1, 0, 2).reshape(NEXP, NEXP * 128))
        st["moe_gw"] = np.ascontiguousarray(np.asarray(inputs["L1_ffn_gate_W"]))
    shards = _shards_fm(x_fm)
    in_maps = [dict(x=shards[c], w1all=st["moe_w1all"], w2all=st["moe_w2all"],
                    gw=st["moe_gw"], gbrow=st["moe_gbrow"],
                    eb1c=st["moe_b1c"], eb2c=st["moe_b2c"],
                    g1c=st["moe_g1c"], be1c=st["moe_be1c"],
                    g2c=st["moe_g2c"], be2c=st["moe_be2c"],
                    sel4=st["moe_sel4"], ident=st["ident"],
                    ones_col=st["ones_col"], ones_row=st["ones_row"])
               for c in range(NCORES)]
    return _gather_tok(_run(nc, in_maps), "y")


def run_mamba(x_fm, inputs, st):
    CH = DI // NCORES
    nc_a = _prog("mamba_a", build_mamba_a)
    iprojt = st.setdefault("iprojt", _t(inputs["L1_attn_in_proj_w"]))
    shards = _shards_fm(x_fm)
    res = _run(nc_a, [dict(x=shards[c], iprojt=iprojt, ones_col=st["ones_col"],
                           ones_row=st["ones_row"]) for c in range(NCORES)])
    xz = _gather_tok(res, "xz")
    x_half = xz[:DI]
    z_half = xz[DI:]
    nc_b = _prog("mamba_b", build_mamba_b)
    if "mamba_wc" not in st:
        cw = np.asarray(inputs["L1_attn_conv_w"])[:, 0, :]
        wc_sh = []
        cb_sh = []
        cb = np.asarray(inputs["L1_attn_conv_b"])
        for c in range(NCORES):
            wce = cw[c * CH:(c + 1) * CH]
            wc_sh.append(np.ascontiguousarray(
                wce.reshape(2, 128, DCONV).transpose(1, 0, 2).reshape(128, -1)))
            cbe = cb[c * CH:(c + 1) * CH]
            cb_sh.append(np.ascontiguousarray(cbe.reshape(2, 128).T))
        st["mamba_wc"] = wc_sh
        st["mamba_cb"] = cb_sh
    xs = _chan_shards(x_half, CH)
    res = _run(nc_b, [dict(x=xs[c], wc=st["mamba_wc"][c], cb=st["mamba_cb"][c])
                      for c in range(NCORES)])
    xc_ch = _gather_chan(res, "y")
    nc_c = _prog("mamba_c", build_mamba_c)
    xprojt = st.setdefault("xprojt", _t(inputs["L1_attn_x_proj_w"]))
    dprojt = st.setdefault("dprojt", _t(inputs["L1_attn_dt_proj_w"]))
    dtbc = st.setdefault("dtbc", _cols(inputs["L1_attn_dt_proj_b"]))
    xc_tok = _shards_fm(xc_ch)
    res = _run(nc_c, [dict(xc=xc_tok[c], xprojt=xprojt, dprojt=dprojt, dtbc=dtbc)
                      for c in range(NCORES)])
    bm = _gather_tok(res, "bm")
    cm = _gather_tok(res, "cm")
    delta = _gather_tok(res, "delta")
    nc_d = _prog("mamba_d", build_mamba_d)
    if "mamba_acol" not in st:
        A_log = np.asarray(inputs["L1_attn_A_log"])
        Dv = np.asarray(inputs["L1_attn_D"])
        ac_sh, dc_sh = [], []
        for c in range(NCORES):
            al = A_log[c * CH:(c + 1) * CH]
            ac_sh.append(np.ascontiguousarray(
                al.reshape(2, 128, NST).transpose(1, 0, 2).reshape(128, -1)))
            dc_sh.append(np.ascontiguousarray(
                Dv[c * CH:(c + 1) * CH].reshape(2, 128).T))
        st["mamba_acol_raw"] = ac_sh
        st["mamba_dcol"] = dc_sh
        st["mamba_sel16"] = np.ascontiguousarray(
            st["sel16"].transpose(1, 0, 2).reshape(NST, NST * 128))
    dl_sh = _chan_shards(delta, CH)
    z_sh = _chan_shards(z_half, CH)
    xc_sh = _chan_shards(xc_ch, CH)
    res = _run(nc_d, [dict(delta=dl_sh[c], xc=xc_sh[c], z=z_sh[c], bm=bm, cm=cm,
                           acol=st["mamba_acol_raw"][c], dcol=st["mamba_dcol"][c],
                           sel16=st["mamba_sel16"]) for c in range(NCORES)])
    y_ch = _gather_chan(res, "y")
    nc_e = _prog("mamba_e", build_mamba_e)
    oprojt = st.setdefault("oprojt", _t(inputs["L1_attn_out_proj_w"]))
    y_tok = _shards_fm(y_ch)
    res = _run(nc_e, [dict(x=shards[c], ym=y_tok[c], oprojt=oprojt)
                      for c in range(NCORES)])
    return _gather_tok(res, "y")





# version 11
# speedup vs baseline: 1.0212x; 1.0212x over previous
import contextlib
import numpy as np
import ml_dtypes
import concourse.bass as bass
import concourse.mybir as mybir
from concourse.bass_utils import run_bass_kernel_spmd

F32 = mybir.dt.float32
BF16 = mybir.dt.bfloat16
AF = mybir.ActivationFunctionType
ALU = mybir.AluOpType
AX = mybir.AxisListType

NCORES = 8
S_TOT = 16384
TPC = S_TOT // NCORES
D = 1024
H_FFN = 4096
DI = 2048
NST = 16
DTR = 64
DCONV = 4
NEXP = 4
TC = 512
NTC = TPC // TC
EPS = 1e-5
C_RG = 8.0

ENGS = ("sync", "tensor", "scalar", "vector", "gpsimd")

def _mk_sems(s):
    for nm in ("qc", "qx", "qw0", "qw1", "qw2", "qst", "cpe", "cs", "cv", "cc",
               "qz", "qs0", "qs1"):
        s.sem(nm)




class Sched:

    CPOOL_CAP = 24

    def __init__(self, nc):
        self.nc = nc
        self.ops = {e: [] for e in ENGS}
        self.es = contextlib.ExitStack()
        self.es_root = self.es
        self._sems = {}
        self.counts = {}
        self._consts = {}
        self._cpool = self.es_root.enter_context(
            self.nc.sbuf_tensor("sched_cpool", [128, self.CPOOL_CAP], F32))
        self.sem("cinit")
        self._uid = 0
        self._nbar = 0

    def scope(self):
        sched = self

        class _Scope:
            def __enter__(self_):
                self_.saved = sched.es
                sched.es = contextlib.ExitStack()
                sched.es.__enter__()
                return sched

            def __exit__(self_, *exc):
                sched.es.__exit__(*exc)
                sched.es = self_.saved
                return False
        return _Scope()

    def barrier(self):
        if "cbar" not in self._sems:
            self.sem("cbar")
        for nm in ("qc", "qx", "qw0", "qw1", "qw2", "qz"):
            if nm in self.counts:
                self.wait("sync", nm, self.counts[nm])
        for nm in ("qst", "cc", "qs0", "qs1", "qz"):
            if nm in self.counts:
                self.wait("gpsimd", nm, self.counts[nm])
        h = self._sems["cbar"]
        for eng in ENGS:
            if eng in ("tensor", "vector", "scalar", "gpsimd"):
                self.ops[eng].append(lambda e: e.drain())
            self.ops[eng].append(lambda e, h=h: e.sem_inc(h, 1))
        self._nbar += 1
        tgt = 5 * self._nbar
        self.counts["cbar"] = tgt
        for eng in ENGS:
            self.ops[eng].append(lambda e, h=h, t=tgt: e.wait_ge(h, t))

    def const(self, val):
        val = float(val)
        if val in self._consts:
            return self._consts[val]
        idx = len(self._consts)
        assert idx < self.CPOOL_CAP
        ap = self._cpool[:, idx:idx + 1]
        self._consts[val] = ap
        return ap

    def sem(self, name):
        self._sems[name] = self.es_root.enter_context(self.nc.semaphore(name))
        self.counts[name] = 0
        return name

    def sbuf(self, name, shape, dtype=F32):
        self._uid += 1
        return self.es.enter_context(
            self.nc.sbuf_tensor("%s__%d" % (name, self._uid), shape, dtype))

    def psum(self, name, shape, dtype=F32):
        self._uid += 1
        return self.es.enter_context(
            self.nc.psum_tensor("%s__%d" % (name, self._uid), shape, dtype))

    def wait(self, eng, sname, cnt=None):
        if cnt is None:
            cnt = self.counts[sname]
        if cnt <= 0:
            return
        h = self._sems[sname]
        self.ops[eng].append(lambda e, h=h, c=cnt: e.wait_ge(h, c))

    SELF_SEM = {"vector": "cv", "scalar": "cs"}

    def do(self, eng, fn, inc=None, amt=1):
        own = self.SELF_SEM.get(eng)
        if own is not None:
            if inc is None:
                inc = own
            if inc == own and self.counts[own] > 0:
                self.wait(eng, own, self.counts[own])
        if inc is None:
            self.ops[eng].append(fn)
            return None
        h = self._sems[inc]
        self.ops[eng].append(lambda e, fn=fn, h=h, a=amt: fn(e).then_inc(h, a))
        self.counts[inc] += amt
        return self.counts[inc]

    def dma(self, eng, out, in_, sem, waits=()):
        for w in waits:
            if isinstance(w, tuple):
                self.wait(eng, w[0], w[1])
            else:
                self.wait(eng, w)
        return self.do(eng, lambda e, o=out, i=in_: e.dma_start(out=o, in_=i),
                       inc=sem, amt=16)

    def n(self, sname):
        return self.counts[sname]

    def finalize(self):
        if self._consts:
            h = self._sems["cinit"]
            pre = []
            items = list(self._consts.items())
            for i, (val, ap) in enumerate(items):
                if i == len(items) - 1:
                    pre.append(lambda e, a=ap, v=val, h=h:
                               e.memset(a, v).then_inc(h, 1))
                else:
                    pre.append(lambda e, a=ap, v=val: e.memset(a, v))
            self.ops["vector"] = pre + self.ops["vector"]
            for eng in ENGS:
                if eng == "vector" or not self.ops[eng]:
                    continue
                self.ops[eng] = [lambda e, h=h: e.wait_ge(h, 1)] + self.ops[eng]
        block = self.es_root.enter_context(self.nc.Block())
        for eng in ENGS:
            if not self.ops[eng]:
                continue
            section = getattr(block, eng)
            ops = self.ops[eng]

            def body(e, ops=ops):
                for f in ops:
                    f(e)
            section(body)


def fm(ap2d):
    return ap2d.rearrange("(kt p) t -> p kt t", p=128)


def fl(t):
    return t.ap()[:, :, :].rearrange("q a b -> q (a b)")


def emit_ln(s, nf, X_c, sqxn, rows, ones_col, ones_row,
            ps_stat, ps_stat2, ps_bm, ps_br, deps, out=None):
    nc = s.nc
    if out is None:
        out = sqxn
    (x_sem, x_cnt) = deps["x_ready"]
    s.wait("scalar", x_sem, x_cnt)
    for w in deps.get("sq_war", ()):
        s.wait("scalar", *w)
    for f in range(nf):
        cs_sq = s.do("scalar", lambda e, f=f: e.square(sqxn[:, f, :], X_c[:, f, :]),
                     inc="cs")
    s.wait("tensor", x_sem, x_cnt)
    s.wait("tensor", "qc", s.n("qc") if "qc" in s.counts else 0)
    for w in deps.get("stat_war", ()):
        s.wait("tensor", *w)
    for f in range(nf):
        s.do("tensor", lambda e, f=f: e.matmul(
            ps_stat[:1, :], ones_col[:, :1], X_c[:, f, :],
            start=(f == 0), stop=(f == nf - 1)),
            inc=("cpe" if f == nf - 1 else None))
    cpe_sum = s.n("cpe")
    s.wait("tensor", "cs", cs_sq)
    for f in range(nf):
        s.do("tensor", lambda e, f=f: e.matmul(
            ps_stat2[:1, :], ones_col[:, :1], sqxn[:, f, :],
            start=(f == 0), stop=(f == nf - 1)),
            inc=("cpe" if f == nf - 1 else None))
    cpe_sumsq = s.n("cpe")
    mrow, vrow, trow, rrow = rows["m"], rows["v"], rows["t"], rows["r"]
    s.wait("vector", "cpe", cpe_sum)
    cv0 = s.do("vector", lambda e: e.tensor_scalar(
        mrow[:1, :], ps_stat[:1, :], 1.0 / (nf * 128), None, ALU.mult), inc="cv")
    s.wait("vector", "cpe", cpe_sumsq)
    s.do("vector", lambda e: e.tensor_scalar(
        trow[:1, :], ps_stat2[:1, :], 1.0 / (nf * 128), None, ALU.mult), inc="cv")
    s.do("vector", lambda e: e.tensor_mul(vrow[:1, :], mrow[:1, :], mrow[:1, :]), inc="cv")
    cv_var = s.do("vector", lambda e: e.tensor_sub(vrow[:1, :], trow[:1, :], vrow[:1, :]),
                  inc="cv")
    s.wait("scalar", "cv", cv_var)
    c_eps = s.const(EPS)[:1, :]
    c_nh = s.const(-0.5)[:1, :]
    s.do("scalar", lambda e: e.activation(trow[:1, :], vrow[:1, :], AF.Ln, bias=c_eps),
         inc="cs")
    cs_rstd = s.do("scalar", lambda e: e.activation(rrow[:1, :], trow[:1, :], AF.Exp,
                                                    scale=c_nh), inc="cs")
    s.wait("tensor", "cv", cv0)
    for w in deps.get("bcast_war", ()):
        s.wait("tensor", *w)
    s.do("tensor", lambda e: e.matmul(ps_bm[:, :], ones_row[:1, :], mrow[:1, :],
                                      start=True, stop=True))
    s.wait("tensor", "cs", cs_rstd)
    cpe_b = s.do("tensor", lambda e: e.matmul(ps_br[:, :], ones_row[:1, :], rrow[:1, :],
                                              start=True, stop=True), inc="cpe")
    s.wait("vector", "cpe", cpe_b)
    for w in deps.get("out_war", ()):
        s.wait("vector", *w)
    for f in range(nf):
        s.do("vector", lambda e, f=f: e.tensor_sub(out[:, f, :], X_c[:, f, :],
                                                   ps_bm[:, :]), inc="cv")
        s.do("vector", lambda e, f=f: e.tensor_mul(out[:, f, :], out[:, f, :],
                                                   ps_br[:, :]), inc="cv")
    return s.n("cv")


def emit_ffn0(s, io):
    x_in = io["x_in"]; y_out = io["y_out"]
    bw1 = io["bw1"]; bw2 = io["bw2"]
    a1c_d = io["a1c_d"]; a2c_d = io["a2c_d"]
    b1c_d = io["b1c_d"]; b2c_d = io["b2c_d"]
    oc_d = io["oc_d"]; or_d = io["or_d"]
    NF = D // 128
    NM1 = H_FFN // 128
    NM2 = D // 128
    with s.scope():
        X = s.sbuf("X", [128, NF, TPC])
        sq = s.sbuf("sq", [128, NF, TC])
        xnb = s.sbuf("xnb", [128, NF, TPC], BF16)
        h = s.sbuf("h", [128, NM1, TC], BF16)
        w1b = [s.sbuf("w1b0", [128, NF, 128], BF16),
               s.sbuf("w1b1", [128, NF, 128], BF16)]
        w2b = [s.sbuf("w2b0", [128, NM1, 128], BF16),
               s.sbuf("w2b1", [128, NM1, 128], BF16)]
        tb = [s.sbuf("tb0", [128, TC]), s.sbuf("tb1", [128, TC])]
        b1c = s.sbuf("sb_b1c", [128, NM1])
        b2h = s.sbuf("sb_b2h", [128, NM2])
        a1c = s.sbuf("sb_a1c", [128, NM1])
        a2c = s.sbuf("sb_a2c", [128, NM2])
        oc = s.sbuf("oc", [128, 1])
        orow = s.sbuf("orow", [1, 128])
        rows = {k: s.sbuf("row_" + k, [1, TC]) for k in "mvtr"}
        ps_stat = s.psum("ps_stat", [1, TC])
        ps_stat2 = s.psum("ps_stat2", [1, TC])
        ps_bm = s.psum("ps_bm", [128, TC])
        ps_br = s.psum("ps_br", [128, TC])
        ps_mm = [s.psum("ps_mm0", [128, TC]), s.psum("ps_mm1", [128, TC])]

        s.dma("sync", b1c[:, :], b1c_d[:, :], "qc")
        s.dma("sync", b2h[:, :], b2c_d[:, :], "qc")
        s.dma("sync", a1c[:, :], a1c_d[:, :], "qc")
        s.dma("sync", a2c[:, :], a2c_d[:, :], "qc")
        s.dma("sync", oc[:, :], oc_d[:, :], "qc")
        s.dma("sync", orow[:1, :], or_d[:1, :], "qc")
        qld_const = s.n("qc")
        s.wait("vector", "qc", qld_const)
        cv_b2 = s.do("vector", lambda e: e.tensor_scalar(b2h[:, :], b2h[:, :], 0.5,
                                                         None, ALU.mult))
        x_cnt = s.dma("sync", X[:, :, :], fm(x_in)[:, :, :], "qx")
        cv_xns = []
        for tc_i in range(NTC):
            xsl = slice(tc_i * TC, (tc_i + 1) * TC)
            deps = {"x_ready": ("qx", x_cnt),
                    "sq_war": [("cpe", s.n("cpe"))],
                    "stat_war": [("cv", s.n("cv"))],
                    "bcast_war": [("cv", s.n("cv"))]}
            cv_xns.append(emit_ln(s, NF, X[:, :, xsl], sq, rows, oc, orow,
                                  ps_stat, ps_stat2, ps_bm, ps_br, deps,
                                  out=xnb[:, :, xsl]))
        last_w1 = {0: 0, 1: 0}
        last_w2 = {0: 0, 1: 0}
        last_evac1 = {0: 0, 1: 0}
        last_evac2 = {0: 0, 1: 0}
        last_h_read = 0
        for tc_i in range(NTC):
            xsl = slice(tc_i * TC, (tc_i + 1) * TC)
            for mc in range(NM1):
                p = mc % 2
                w_cnt = s.dma("sync", fl(w1b[p]),
                              bw1[mc, :, :], "qw%d" % p,
                              waits=[("cpe", max(last_w1[p], last_w2[p]))])
                s.wait("tensor", "qw%d" % p, w_cnt)
                if mc == 0:
                    s.wait("tensor", "cv", cv_xns[tc_i])
                s.wait("tensor", "cs", last_evac1[p])
                s.wait("tensor", "cv", last_evac2[p])
                for kt in range(NF):
                    s.do("tensor", lambda e, p=p, kt=kt, xsl=xsl: e.matmul(
                        ps_mm[p][:, :], w1b[p][:, kt, :], xnb[:, kt, xsl],
                        start=(kt == 0), stop=(kt == NF - 1)),
                        inc=("cpe" if kt == NF - 1 else None))
                cpe_h = s.n("cpe")
                last_w1[p] = cpe_h
                s.wait("scalar", "cpe", cpe_h)
                if tc_i > 0 and mc == 0:
                    s.wait("scalar", "cpe", last_h_read)
                last_evac1[p] = s.do("scalar", lambda e, p=p, mc=mc: e.activation(
                    h[:, mc, :], ps_mm[p][:, :], AF.Gelu,
                    bias=b1c[:, mc:mc + 1], scale=a1c[:, mc:mc + 1]))
            cs_h_all = s.n("cs")
            for mc2 in range(NM2):
                p = mc2 % 2
                w_cnt = s.dma("sync", fl(w2b[p]),
                              bw2[mc2, :, :], "qw%d" % p,
                              waits=[("cpe", max(last_w2[p], last_w1[p]))])
                s.wait("tensor", "qw%d" % p, w_cnt)
                s.wait("tensor", "cs", cs_h_all)
                s.wait("tensor", "cv", last_evac2[p])
                for kt in range(NM1):
                    s.do("tensor", lambda e, p=p, kt=kt: e.matmul(
                        ps_mm[p][:, :], w2b[p][:, kt, :], h[:, kt, :],
                        start=(kt == 0), stop=(kt == NM1 - 1)),
                        inc=("cpe" if kt == NM1 - 1 else None))
                cpe_m2 = s.n("cpe")
                last_w2[p] = cpe_m2
                if mc2 == NM2 - 1:
                    last_h_read = cpe_m2
                s.wait("vector", "cpe", cpe_m2)
                if tc_i == 0 and mc2 == 0:
                    s.wait("vector", "cv", cv_b2)
                s.do("vector", lambda e, p=p, mc2=mc2: e.tensor_scalar(
                    tb[p][:, :], ps_mm[p][:, :], a2c[:, mc2:mc2 + 1],
                    b2h[:, mc2:mc2 + 1], ALU.mult, ALU.add))
                last_evac2[p] = s.do("vector",
                    lambda e, p=p, mc2=mc2, xsl=xsl: e.tensor_add(
                        X[:, mc2, xsl], X[:, mc2, xsl], tb[p][:, :]))
            s.wait("gpsimd", "cv", s.n("cv"))
            s.dma("gpsimd", fm(y_out)[:, :, xsl], X[:, :, xsl], "qst")


def build_ffn0():
    nc = bass.Bass("TRN2", target_bir_lowering=False)
    x_in = nc.dram_tensor("x", [D, TPC], F32, kind="ExternalInput").ap()
    w1t = nc.dram_tensor("w1t", [D, H_FFN], F32, kind="ExternalInput").ap()
    w2t = nc.dram_tensor("w2t", [H_FFN, D], F32, kind="ExternalInput").ap()
    b1c_d = nc.dram_tensor("b1c", [128, H_FFN // 128], F32, kind="ExternalInput").ap()
    b2c_d = nc.dram_tensor("b2c", [128, D // 128], F32, kind="ExternalInput").ap()
    oc_d = nc.dram_tensor("ones_col", [128, 1], F32, kind="ExternalInput").ap()
    or_d = nc.dram_tensor("ones_row", [1, 128], F32, kind="ExternalInput").ap()
    y_out = nc.dram_tensor("y", [D, TPC], F32, kind="ExternalOutput").ap()

    NF = D // 128
    NM1 = H_FFN // 128
    NM2 = D // 128

    s = Sched(nc)
    with s.es:
        _mk_sems(s)
        emit_ffn0(s, dict(x_in=x_in, w1t=w1t, w2t=w2t, b1c_d=b1c_d, b2c_d=b2c_d, oc_d=oc_d, or_d=or_d, y_out=y_out))
        s.wait("gpsimd", "qst", s.n("qst"))
        s.finalize()
    return nc




def emit_attn0a(s, io):
    x_in = io["x_in"]
    brow_d = io["brow_d"]; lam_d = io["lam_d"]
    oc_d = io["oc_d"]; or_d = io["or_d"]
    x1_out = io["x1_out"]; a_out = io["a_out"]; g_out = io["g_out"]
    wts = io["wts"]
    NF = D // 128
    with s.scope():
        X = s.sbuf("X", [128, NF, TPC])
        xnb = s.sbuf("xnb", [128, NF, TPC], BF16)
        x1v = s.sbuf("x1v", [128, NF, TC], BF16)
        x2a = s.sbuf("x2a", [128, NF, TC], BF16)
        x2b = s.sbuf("x2b", [128, NF, TC], BF16)
        rv = s.sbuf("rv", [128, NF, TC])
        iv = s.sbuf("iv", [128, NF, TC], BF16)
        avb = s.sbuf("avb", [128, NF, TC])
        gvb = s.sbuf("gvb", [128, NF, TC], BF16)
        hsc = s.sbuf("hsc", [128, NF, TC], BF16)
        wb = [s.sbuf("wb%d" % i, [128, NF, 128], BF16) for i in range(3)]
        brow = s.sbuf("sb_brow", [1, 5 * D], BF16)
        lamc = s.sbuf("sb_lamc", [128, 8])
        lt = s.sbuf("lt", [128, 8])
        lr = s.sbuf("lr", [128, 8])
        lam2 = s.sbuf("lam2", [128, 8])
        carry1 = s.sbuf("carry1", [128, 8])
        rsum = s.sbuf("rsum", [128, 8])
        rtmp = s.sbuf("rtmp", [128, 8])
        aend = s.sbuf("aend", [128, 16])
        oc = s.sbuf("oc", [128, 1])
        orow = s.sbuf("orow", [1, 128])
        otc = s.sbuf("otc", [1, TC], BF16)
        rows = {k: s.sbuf("row_" + k, [1, TC]) for k in "mvtr"}
        ps_stat = s.psum("ps_stat", [1, TC])
        ps_stat2 = s.psum("ps_stat2", [1, TC])
        ps_bm = s.psum("ps_bm", [128, TC])
        ps_br = s.psum("ps_br", [128, TC])
        ps_mm = [s.psum("ps_mm0", [128, TC]), s.psum("ps_mm1", [128, TC])]

        c1 = s.const(1.0)
        cn1 = s.const(-1.0)
        ch = s.const(0.5)
        s.dma("sync", brow[:1, :], brow_d[:1, :], "qc")
        s.dma("sync", lamc[:, :], lam_d[:, :], "qc")
        s.dma("sync", oc[:, :], oc_d[:, :], "qc")
        s.dma("sync", orow[:1, :], or_d[:1, :], "qc")
        qconst = s.n("qc")
        cv_ones = s.do("vector", lambda e: e.memset(otc[:1, :], 1.0))
        s.wait("scalar", "qc", qconst)
        s.do("scalar", lambda e: e.activation(lt[:, :], lamc[:, :], AF.Abs))
        s.do("scalar", lambda e: e.activation(lt[:, :], lt[:, :], AF.Exp, scale=cn1))
        cs_l = s.do("scalar", lambda e: e.activation(lt[:, :], lt[:, :], AF.Ln, bias=c1))
        cs_r = s.do("scalar", lambda e: e.activation(lr[:, :], lamc[:, :], AF.Relu,
                                                     scale=cn1))
        s.wait("vector", "cs", cs_r)
        s.do("vector", lambda e: e.tensor_add(lt[:, :], lt[:, :], lr[:, :]))
        cv_lam = s.do("vector", lambda e: e.tensor_scalar(
            lamc[:, :], lt[:, :], -C_RG, None, ALU.mult))
        cv_lam = s.do("vector", lambda e: e.tensor_scalar(
            lam2[:, :], lamc[:, :], 2.0, None, ALU.mult))
        s.do("vector", lambda e: e.memset(carry1[:, :], 0.0))
        s.do("vector", lambda e: e.memset(rsum[:, :], 0.0))
        x_cnt = s.dma("sync", X[:, :, :], fm(x_in)[:, :, :], "qx")
        cv_xns = []
        for tc_i in range(NTC):
            xsl = slice(tc_i * TC, (tc_i + 1) * TC)
            deps = {"x_ready": ("qx", x_cnt),
                    "sq_war": [("cpe", s.n("cpe"))],
                    "stat_war": [("cv", s.n("cv"))],
                    "bcast_war": [("cv", s.n("cv"))]}
            cv_xns.append(emit_ln(s, NF, X[:, :, xsl], rv, rows, oc, orow,
                                  ps_stat, ps_stat2, ps_bm, ps_br, deps,
                                  out=xnb[:, :, xsl]))
        st = {"wpar": 0, "mmpar": 0, "last_w": {0: 0, 1: 0, 2: 0},
              "last_evac": {0: 0, 1: 0}, "qst": 0}

        def lin(wt, boff, rhs, rhs_ready, out, func, scale=None):
            evacs = []
            for mc in range(NF):
                wp = st["wpar"]; st["wpar"] = (st["wpar"] + 1) % 3
                w_cnt = s.dma("sync", fl(wb[wp]),
                              wt[mc, :, :], "qw%d" % wp,
                              waits=[("cpe", st["last_w"][wp])])
                pp = st["mmpar"]; st["mmpar"] ^= 1
                s.wait("tensor", "qw%d" % wp, w_cnt)
                if mc == 0 and rhs_ready is not None:
                    s.wait("tensor", rhs_ready[0], rhs_ready[1])
                    s.wait("tensor", "qc", s.n("qc"))
                s.wait("tensor", "cs", st["last_evac"][pp])
                for kt in range(NF):
                    s.do("tensor", lambda e, wp=wp, pp=pp, kt=kt, rhs=rhs: e.matmul(
                        ps_mm[pp][:, :], wb[wp][:, kt, :], rhs[:, kt, :],
                        start=(kt == 0), stop=False))
                cpe_mm = s.do("tensor", lambda e, pp=pp, boff=boff, mc=mc: e.matmul(
                    ps_mm[pp][:, :], brow[:1, boff + mc * 128:boff + (mc + 1) * 128],
                    otc[:1, :], start=False, stop=True), inc="cpe")
                st["last_w"][wp] = cpe_mm
                s.wait("scalar", "cpe", cpe_mm)
                if scale is None:
                    cs_e = s.do("scalar", lambda e, pp=pp, mc=mc, out=out: e.activation(
                        out[:, mc, :], ps_mm[pp][:, :], func))
                else:
                    cs_e = s.do("scalar", lambda e, pp=pp, mc=mc, out=out: e.activation(
                        out[:, mc, :], ps_mm[pp][:, :], func, scale=scale))
                st["last_evac"][pp] = cs_e
                evacs.append(cs_e)
            return evacs[-1]

        flat = lambda t: t.ap()[:, :, :].rearrange("p a b -> p (a b)")
        for tc_i in range(NTC):
            xsl = slice(tc_i * TC, (tc_i + 1) * TC)
            xnc = xnb[:, :, xsl]
            cv_rdy = max(cv_xns[tc_i], cv_ones or 0, cv_lam or 0)
            s.wait("scalar", "qst", st["qst"])
            s.wait("scalar", "cv", s.n("cv"))
            cs_x2a = lin(wts["wp2t"], 1 * D, xnc, ("cv", cv_rdy), x2a, AF.Copy)
            lin(wts["wp1t"], 0 * D, xnc, ("cv", cv_rdy), x1v, AF.Gelu)
            cs_x2b = lin(wts["cvt"], 2 * D, x2a, ("cs", cs_x2a), x2b, AF.Relu)
            cs_rv = lin(wts["wat"], 3 * D, x2b, ("cs", cs_x2b), rv, AF.Sigmoid)
            for mc in range(NF):
                s.do("scalar", lambda e, mc=mc: e.activation(
                    avb[:, mc, :], rv[:, mc, :], AF.Exp,
                    scale=lamc[:, mc:mc + 1]))
            cs_a = s.n("cs")
            for mc in range(NF):
                s.do("scalar", lambda e, mc=mc: e.activation(
                    rv[:, mc, :], rv[:, mc, :], AF.Exp,
                    scale=lam2[:, mc:mc + 1]))
            cs_g = s.do("scalar", lambda e: e.activation(flat(rv), flat(rv), AF.Sqrt,
                                                         scale=cn1, bias=c1))
            cs_i = lin(wts["wxt"], 4 * D, x2b, ("cs", cs_x2b), iv, AF.Sigmoid)
            s.wait("vector", "cs", max(cs_g, cs_i))
            s.do("vector", lambda e: e.tensor_mul(flat(gvb), flat(rv), flat(iv)))
            cv_g = s.do("vector", lambda e: e.tensor_mul(flat(gvb), flat(gvb),
                                                         flat(x2b)))
            s.wait("scalar", "cv", s.n("cv"))
            cs_ln = s.do("scalar", lambda e, ce=s.const(1e-38): e.activation(
                flat(rv), flat(avb), AF.Ln, bias=ce))
            s.wait("vector", "cs", cs_ln)
            s.do("vector", lambda e: e.tensor_reduce(
                rtmp[:, :].rearrange("p (f o) -> p f o", o=1),
                rv[:, :, :], axis=AX.X, op=ALU.add))
            cv_rs = s.do("vector", lambda e: e.tensor_add(rsum[:, :], rsum[:, :],
                                                          rtmp[:, :]))
            s.wait("gpsimd", "cs", cs_a)
            s.dma("gpsimd", fm(a_out)[:, :, xsl], avb[:, :, :], "qst")
            s.wait("gpsimd", "cv", cv_g)
            s.dma("gpsimd", fm(g_out)[:, :, xsl], gvb[:, :, :], "qst")
            s.dma("gpsimd", fm(x1_out)[:, :, xsl], x1v[:, :, :], "qst")
            st["qst"] = s.n("qst")
            for f in range(NF):
                s.do("vector", lambda e, f=f: e.tensor_tensor_scan(
                    hsc[:, f, :], avb[:, f, :], gvb[:, f, :],
                    carry1[:, f:f + 1], ALU.mult, ALU.add))
                s.do("vector", lambda e, f=f: e.tensor_copy(
                    carry1[:, f:f + 1], hsc[:, f, TC - 1:TC]))
        s.wait("scalar", "cv", s.n("cv"))
        s.do("scalar", lambda e: e.activation(aend[:, 0:8], rsum[:, :], AF.Exp))
        s.wait("vector", "cs", s.n("cs"))
        s.do("vector", lambda e: e.tensor_copy(aend[:, 8:16], carry1[:, :]))
        s.wait("gpsimd", "cv", s.n("cv"))
        s.wait("gpsimd", "cs", s.n("cs"))
        s.dma("gpsimd", io["exg_out"][:, :], aend[:, :], "qst")


def build_attn0a():
    nc = bass.Bass("TRN2", target_bir_lowering=False)
    x_in = nc.dram_tensor("x", [D, TPC], F32, kind="ExternalInput").ap()
    wts = {}
    for nm in ("wp1t", "wp2t", "cvt", "wat", "wxt"):
        wts[nm] = nc.dram_tensor(nm, [D, D], F32, kind="ExternalInput").ap()
    brow_d = nc.dram_tensor("brow", [1, 5 * D], F32, kind="ExternalInput").ap()
    lam_d = nc.dram_tensor("lamc", [128, 8], F32, kind="ExternalInput").ap()
    oc_d = nc.dram_tensor("ones_col", [128, 1], F32, kind="ExternalInput").ap()
    or_d = nc.dram_tensor("ones_row", [1, 128], F32, kind="ExternalInput").ap()
    x1_out = nc.dram_tensor("x1g", [D, TPC], F32, kind="ExternalOutput").ap()
    a_out = nc.dram_tensor("av", [D, TPC], F32, kind="ExternalOutput").ap()
    g_out = nc.dram_tensor("gv", [D, TPC], F32, kind="ExternalOutput").ap()
    NF = D // 128

    s = Sched(nc)
    with s.es:
        _mk_sems(s)
        emit_attn0a(s, dict(x_in=x_in, brow_d=brow_d, lam_d=lam_d, oc_d=oc_d, or_d=or_d, x1_out=x1_out, a_out=a_out, g_out=g_out, wts=wts))
        s.wait("gpsimd", "qst", s.n("qst"))
        s.finalize()
    return nc



def emit_scan0(s, io):
    a_in = io["a_in"]
    g_in = io["g_in"]
    h_out = io["h_out"]
    with s.scope():
        ab = s.sbuf("ab", [128, S_TOT])
        gb = s.sbuf("gb", [128, S_TOT])
        s.dma("sync", ab[:, :], a_in[:, :], "qx")
        q2 = s.dma("sync", gb[:, :], g_in[:, :], "qx")
        s.wait("vector", "qx", q2)
        cv_h = s.do("vector", lambda e: e.tensor_tensor_scan(
            gb[:, :], ab[:, :], gb[:, :], 0.0, ALU.mult, ALU.add))
        s.wait("gpsimd", "cv", cv_h)
        q3 = s.dma("gpsimd", h_out[:, :], gb[:, :], "qst")
        s.wait("gpsimd", "qst", q3)


def build_scan0():
    nc = bass.Bass("TRN2", target_bir_lowering=False)
    a_in = nc.dram_tensor("a", [128, S_TOT], F32, kind="ExternalInput").ap()
    g_in = nc.dram_tensor("g", [128, S_TOT], F32, kind="ExternalInput").ap()
    h_out = nc.dram_tensor("h", [128, S_TOT], F32, kind="ExternalOutput").ap()
    s = Sched(nc)
    with s.es:
        _mk_sems(s)
        emit_scan0(s, dict(a_in=a_in, g_in=g_in, h_out=h_out))
        s.wait("gpsimd", "qst", s.n("qst"))
        s.finalize()
    return nc



def emit_attn0b(s, io):
    x_in = io["x_in"]; x1_in = io["x1_in"]
    a_in = io["a_in"]; g_in = io["g_in"]
    exg_in = io["exg_in"]; msk_d = io["msk_d"]
    wp3t = io["wp3t"]; brow_d = io["brow_d"]; y_out = io["y_out"]
    NF = D // 128
    with s.scope():
        X_c = s.sbuf("X_c", [128, NF, TC])
        x1f = s.sbuf("x1f", [128, NF, TC], BF16)
        a_c = s.sbuf("a_c", [128, NF, TC])
        g_c = s.sbuf("g_c", [128, NF, TC], BF16)
        m_c = s.sbuf("m_c", [128, NF, TC], BF16)
        wb = [s.sbuf("wb0", [128, NF, 128], BF16),
              s.sbuf("wb1", [128, NF, 128], BF16)]
        brow = s.sbuf("sb_brow", [1, D])
        otc = s.sbuf("otc", [1, TC])
        exga = s.sbuf("exga", [128, NCORES, 16])
        msk = s.sbuf("sb_msk", [128, NCORES])
        carry2 = s.sbuf("carry2", [128, NF])
        sst = s.sbuf("sst", [128, NF])
        tj = s.sbuf("tj", [128, NF])
        ps_mm = [s.psum("ps_mm0", [128, TC]), s.psum("ps_mm1", [128, TC])]
        s.dma("sync", brow[:1, :], brow_d[:1, :], "qc")
        s.dma("sync", msk[:, :], msk_d[:, :], "qc")
        qc1 = s.dma("sync", exga[:, :, :],
                    exg_in.rearrange("(j p) c -> p j c", p=128), "qc")
        cv_ones = s.do("vector", lambda e: e.memset(otc[:1, :], 1.0))
        s.wait("vector", "qc", qc1)
        s.do("vector", lambda e: e.memset(sst[:, :], 0.0))
        s.do("vector", lambda e: e.memset(carry2[:, :], 0.0))
        for j in range(NCORES):
            if j > 0:
                s.do("vector", lambda e, j=j: e.tensor_mul(
                    sst[:, :], sst[:, :], exga[:, j - 1, 0:8]))
                s.do("vector", lambda e, j=j: e.tensor_add(
                    sst[:, :], sst[:, :], exga[:, j - 1, 8:16]))
            s.do("vector", lambda e, j=j: e.scalar_tensor_tensor(
                carry2[:, :], sst[:, :], msk[:, j:j + 1], carry2[:, :],
                ALU.mult, ALU.add))
        st = {"wpar": 0, "last_w": {0: 0, 1: 0}, "last_use": {0: 0, 1: 0},
              "qst": 0}
        flat = lambda t: t.ap()[:, :, :].rearrange("p a b -> p (a b)")
        for tc_i in range(NTC):
            xsl = slice(tc_i * TC, (tc_i + 1) * TC)
            s.dma("sync", X_c[:, :, :], fm(x_in)[:, :, xsl], "qx",
                  waits=[("qst", st["qst"])])
            s.dma("sync", x1f[:, :, :], fm(x1_in)[:, :, xsl], "qx",
                  waits=[("cv", s.n("cv"))])
            s.dma("sync", a_c[:, :, :], fm(a_in)[:, :, xsl], "qx")
            x_cnt = s.dma("sync", g_c[:, :, :], fm(g_in)[:, :, xsl], "qx")
            s.wait("vector", "qx", x_cnt)
            for f in range(NF):
                s.do("vector", lambda e, f=f: e.tensor_tensor_scan(
                    g_c[:, f, :], a_c[:, f, :], g_c[:, f, :],
                    carry2[:, f:f + 1], ALU.mult, ALU.add))
                s.do("vector", lambda e, f=f: e.tensor_copy(
                    carry2[:, f:f + 1], g_c[:, f, TC - 1:TC]))
            s.wait("vector", "cpe", s.n("cpe"))
            cv_m = s.do("vector", lambda e: e.tensor_mul(flat(m_c), flat(x1f),
                                                         flat(g_c)))
            for mc in range(NF):
                wp = st["wpar"]; st["wpar"] ^= 1
                w_cnt = s.dma("sync", fl(wb[wp]),
                              wp3t[mc, :, :], "qw%d" % wp,
                              waits=[("cpe", st["last_w"][wp])])
                s.wait("tensor", "qw%d" % wp, w_cnt)
                if mc == 0:
                    s.wait("tensor", "cv", cv_m)
                    if tc_i == 0:
                        s.wait("tensor", "cv", cv_ones)
                        s.wait("tensor", "qc", s.n("qc"))
                s.wait("tensor", "cv", st["last_use"][wp])
                for kt in range(NF):
                    s.do("tensor", lambda e, wp=wp, kt=kt: e.matmul(
                        ps_mm[wp][:, :], wb[wp][:, kt, :], m_c[:, kt, :],
                        start=(kt == 0), stop=False))
                cpe_mm = s.do("tensor", lambda e, wp=wp, mc=mc: e.matmul(
                    ps_mm[wp][:, :], brow[:1, mc * 128:(mc + 1) * 128],
                    otc[:1, :], start=False, stop=True), inc="cpe")
                st["last_w"][wp] = cpe_mm
                s.wait("vector", "cpe", cpe_mm)
                if mc == 0:
                    s.wait("vector", "qst", st["qst"])
                st["last_use"][wp] = s.do("vector", lambda e, wp=wp, mc=mc:
                    e.tensor_add(X_c[:, mc, :], X_c[:, mc, :], ps_mm[wp][:, :]))
            s.wait("gpsimd", "cv", s.n("cv"))
            s.dma("gpsimd", fm(y_out)[:, :, xsl], X_c[:, :, :], "qst")
            st["qst"] = s.n("qst")


def build_attn0b():
    nc = bass.Bass("TRN2", target_bir_lowering=False)
    x_in = nc.dram_tensor("x", [D, TPC], F32, kind="ExternalInput").ap()
    x1_in = nc.dram_tensor("x1g", [D, TPC], F32, kind="ExternalInput").ap()
    h_in = nc.dram_tensor("hv", [D, TPC], F32, kind="ExternalInput").ap()
    wp3t = nc.dram_tensor("wp3t", [D, D], F32, kind="ExternalInput").ap()
    brow_d = nc.dram_tensor("brow", [1, D], F32, kind="ExternalInput").ap()
    y_out = nc.dram_tensor("y", [D, TPC], F32, kind="ExternalOutput").ap()
    NF = D // 128
    s = Sched(nc)
    with s.es:
        _mk_sems(s)
        emit_attn0b(s, dict(x_in=x_in, x1_in=x1_in, h_in=h_in, wp3t=wp3t, brow_d=brow_d, y_out=y_out))
        s.wait("gpsimd", "qst", s.n("qst"))
        s.finalize()
    return nc




def emit_moe(s, io):
    NF = D // 128
    NGI = TC // 128
    x_in = io["x_in"]
    w1all = io["w1all"]
    w2all = io["w2all"]
    gw_d = io["gw_d"]
    gbrow_d = io["gbrow_d"]
    b1c_d = io["b1c_d"]
    b2c_d = io["b2c_d"]
    g1c_d = io["g1c_d"]
    be1c_d = io["be1c_d"]
    g2c_d = io["g2c_d"]
    be2c_d = io["be2c_d"]
    sel4_d = io["sel4_d"]
    id_d = io["id_d"]
    oc_d = io["oc_d"]
    or_d = io["or_d"]
    y_out = io["y_out"]
    with s.scope():
        X_cs = [s.sbuf("X_c%d" % i, [128, NF, TC]) for i in range(2)]
        xnb = s.sbuf("xnb", [128, NF, TC], BF16)
        lnsc = s.sbuf("lnsc", [128, NF, TC])
        heb = [s.sbuf("heb%d" % i, [128, NF, TC], BF16) for i in range(NEXP)]
        hrb = [s.sbuf("hrb%d" % i, [128, NF, TC], BF16) for i in range(NEXP)]
        hob = [s.sbuf("hob%d" % i, [128, NF, TC], BF16) for i in range(2)]
        sqb = s.sbuf("sqb", [128, NF, TC], BF16)
        wsl = s.sbuf("wsl", [128, NEXP, TC])
        tbuf = s.sbuf("tbuf", [128, TC])
        wb = [s.sbuf("wb%d" % i, [128, NF, 128], BF16) for i in range(3)]
        gw = s.sbuf("sb_gw", [128, NF, NEXP], BF16)
        gbrow = s.sbuf("sb_gbrow", [1, NEXP])
        b1c = s.sbuf("sb_eb1c", [128, NEXP * NF])
        b2c = s.sbuf("sb_eb2c", [128, NEXP * NF])
        g1c = s.sbuf("sb_g1c", [128, NEXP * 8])
        be1c = s.sbuf("sb_be1c", [128, NEXP * 8])
        g2c = s.sbuf("sb_g2c", [128, NEXP * 8])
        be2c = s.sbuf("sb_be2c", [128, NEXP * 8])
        sel4 = s.sbuf("sb_sel4", [NEXP, NEXP * 128])
        iden = s.sbuf("sb_ident", [128, 128])
        oc = s.sbuf("oc", [128, 1])
        ocb = s.sbuf("ocb", [128, 1], BF16)
        orow = s.sbuf("orow", [1, 128])
        gsb = s.sbuf("gsb", [128, NGI, NEXP])
        top = s.sbuf("top", [128, NGI, 8])
        wn = s.sbuf("wn", [128, NGI, NEXP])
        wr = s.sbuf("wr", [NEXP, TC])
        rows = {k: s.sbuf("row_" + k, [1, TC]) for k in "mvtrw"}
        ps = [s.psum("ps%d" % i, [128, TC]) for i in range(8)]
        ch = s.const(0.5)

        for dst, srcap in ((gw[:, :, :], fm(gw_d)), (gbrow[:1, :], gbrow_d[:1, :]),
                           (b1c[:, :], b1c_d[:, :]), (b2c[:, :], b2c_d[:, :]),
                           (g1c[:, :], g1c_d[:, :]), (be1c[:, :], be1c_d[:, :]),
                           (g2c[:, :], g2c_d[:, :]), (be2c[:, :], be2c_d[:, :]),
                           (sel4[:, :], sel4_d[:, :]), (iden[:, :], id_d[:, :]),
                           (oc[:, :], oc_d[:, :]), (orow[:1, :], or_d[:1, :])):
            s.dma("sync", dst, srcap, "qc")
        qconst = s.n("qc")
        cv_ocb = s.do("vector", lambda e: e.memset(ocb[:, :], 1.0))

        st = {"wpar": 0, "mmpar": 0, "last_w": {0: 0, 1: 0, 2: 0},
              "last_evac": {0: 0, 1: 0},
              "qst": [0, 0], "x_war": [0, 0], "hob_war": {0: 0, 1: 0}}

        def mm_stage(wt3d, toff, rhs, rhs_ready, out, bias_sb, boff, evac_war):
            for mc in range(NF):
                wp = st["wpar"]; st["wpar"] = (st["wpar"] + 1) % 3
                w_cnt = s.dma("sync", fl(wb[wp]),
                              wt3d[toff + mc, :, :],
                              "qw%d" % wp,
                              waits=[("cpe", st["last_w"][wp])])
                pp = st["mmpar"]; st["mmpar"] ^= 1
                s.wait("tensor", "qw%d" % wp, w_cnt)
                if mc == 0 and rhs_ready is not None:
                    s.wait("tensor", rhs_ready[0], rhs_ready[1])
                s.wait("tensor", "cs", st["last_evac"][pp])
                for kt in range(NF):
                    s.do("tensor", lambda e, wp=wp, pp=pp, kt=kt: e.matmul(
                        ps[4 + pp][:, :], wb[wp][:, kt, :], rhs[:, kt, :],
                        start=(kt == 0), stop=(kt == NF - 1)),
                        inc=("cpe" if kt == NF - 1 else None))
                cpe_mm = s.n("cpe")
                st["last_w"][wp] = cpe_mm
                s.wait("scalar", "cpe", cpe_mm)
                if mc == 0 and evac_war:
                    s.wait("scalar", evac_war[0], evac_war[1])
                s.do("scalar", lambda e, pp=pp, mc=mc, out=out, boff=boff:
                     e.activation(out[:, mc, :], ps[4 + pp][:, :], AF.Identity,
                                  bias=bias_sb[:, boff + mc:boff + mc + 1]))
                st["last_evac"][pp] = s.n("cs")
                s.do("scalar", lambda e, mc=mc, out=out: e.square(
                    sqb[:, mc, :], out[:, mc, :]))
            return s.n("cs")

        def stats_blk(src, cs_ready, ps01_war):
            s.wait("tensor", "cs", cs_ready)
            s.wait("tensor", "cv", ps01_war)
            for mc in range(NF):
                s.do("tensor", lambda e, mc=mc, src=src: e.matmul(
                    ps[0][:1, :], ocb[:, :1], src[:, mc, :],
                    start=(mc == 0), stop=(mc == NF - 1)),
                    inc=("cpe" if mc == NF - 1 else None))
            cpe_sum = s.n("cpe")
            for mc in range(NF):
                s.do("tensor", lambda e, mc=mc: e.matmul(
                    ps[1][:1, :], ocb[:, :1], sqb[:, mc, :],
                    start=(mc == 0), stop=(mc == NF - 1)),
                    inc=("cpe" if mc == NF - 1 else None))
            return cpe_sum, s.n("cpe")

        def ln_rows(cpe_sum, cpe_sq):
            s.wait("vector", "cpe", cpe_sum)
            s.do("vector", lambda e: e.tensor_scalar(
                rows["m"][:1, :], ps[0][:1, :], 1.0 / D, None, ALU.mult))
            s.wait("vector", "cpe", cpe_sq)
            cv_free = s.do("vector", lambda e: e.tensor_scalar(
                rows["t"][:1, :], ps[1][:1, :], 1.0 / D, None, ALU.mult))
            s.do("vector", lambda e: e.tensor_mul(
                rows["v"][:1, :], rows["m"][:1, :], rows["m"][:1, :]))
            cv_var = s.do("vector", lambda e: e.tensor_sub(
                rows["v"][:1, :], rows["t"][:1, :], rows["v"][:1, :]))
            c_eps = s.const(EPS)[:1, :]
            c_nh = s.const(-0.5)[:1, :]
            s.wait("scalar", "cv", cv_var)
            s.do("scalar", lambda e: e.activation(
                rows["t"][:1, :], rows["v"][:1, :], AF.Ln, bias=c_eps))
            cs_r = s.do("scalar", lambda e: e.activation(
                rows["r"][:1, :], rows["t"][:1, :], AF.Exp, scale=c_nh))
            s.wait("vector", "cs", cs_r)
            cv_done = s.do("vector", lambda e: e.tensor_mul(
                rows["w"][:1, :], rows["m"][:1, :], rows["r"][:1, :]))
            return cv_done, cv_free

        def bcast_rw(cv_rows, ps23_war):
            s.wait("tensor", "cv", max(cv_rows, ps23_war))
            s.do("tensor", lambda e: e.matmul(
                ps[2][:, :], orow[:1, :], rows["r"][:1, :],
                start=True, stop=True))
            return s.do("tensor", lambda e: e.matmul(
                ps[3][:, :], orow[:1, :], rows["w"][:1, :],
                start=True, stop=True), inc="cpe")

        for tc_i in range(NTC):
            par = tc_i % 2
            X_c = X_cs[par]
            xsl = slice(tc_i * TC, (tc_i + 1) * TC)
            x_cnt = s.dma("sync", X_c[:, :, :], fm(x_in)[:, :, xsl], "qx",
                          waits=[("qst", st["qst"][par]),
                                 ("cv", st["x_war"][par])])
            deps = {"x_ready": ("qx", x_cnt),
                    "sq_war": [("cpe", s.n("cpe")), ("cv", s.n("cv"))],
                    "stat_war": [("cv", s.n("cv"))] if tc_i else [],
                    "bcast_war": [("cv", s.n("cv"))] if tc_i else [],
                    "out_war": [("cpe", s.n("cpe")), ("cs", s.n("cs"))]}
            s.wait("vector", "qst", st["qst"][par])
            cv_xn = emit_ln(s, NF, X_c, lnsc, rows, oc, orow,
                            ps[0], ps[1], ps[2], ps[3], deps)
            for f in range(NF):
                s.do("vector", lambda e, f=f, X_c=X_c: e.scalar_tensor_tensor(
                    X_c[:, f, :], lnsc[:, f, :], ch, X_c[:, f, :],
                    ALU.mult, ALU.add))
            flatn = lambda t: t.ap()[:, :, :].rearrange("p a b -> p (a b)")
            cv_xn = s.do("vector", lambda e: e.tensor_copy(flatn(xnb),
                                                           flatn(lnsc)))
            if tc_i == 0:
                cv_xn = max(cv_xn, cv_ocb)
            for gi in range(NGI):
                gslice = slice(gi * 128, (gi + 1) * 128)
                s.wait("tensor", "cv", cv_xn)
                if tc_i == 0 and gi == 0:
                    s.wait("tensor", "qc", qconst)
                s.wait("tensor", "cv", s.n("cv"))
                for f in range(NF):
                    s.do("tensor", lambda e, f=f, gslice=gslice: e.matmul(
                        ps[7][:, :NEXP], xnb[:, f, gslice], gw[:, f, :],
                        start=(f == 0), stop=False))
                cpe_g = s.do("tensor", lambda e: e.matmul(
                    ps[7][:, :NEXP], orow[:1, :], gbrow[:1, :],
                    start=False, stop=True), inc="cpe")
                s.wait("vector", "cpe", cpe_g)
                s.do("vector", lambda e, gi=gi: e.tensor_copy(gsb[:, gi, :],
                                                              ps[7][:, :NEXP]))
            se = lambda e_: gsb[:, :, e_:e_ + 1]
            tcol = lambda j: top[:, :, j:j + 1]
            s.do("vector", lambda e: e.tensor_tensor(tcol(0), se(0), se(1), ALU.max))
            s.do("vector", lambda e: e.tensor_tensor(tcol(1), se(0), se(1), ALU.min))
            s.do("vector", lambda e: e.tensor_tensor(tcol(2), se(2), se(3), ALU.max))
            s.do("vector", lambda e: e.tensor_tensor(tcol(3), se(2), se(3), ALU.min))
            s.do("vector", lambda e: e.tensor_tensor(tcol(4), tcol(0), tcol(2), ALU.max))
            s.do("vector", lambda e: e.tensor_tensor(tcol(5), tcol(0), tcol(2), ALU.min))
            s.do("vector", lambda e: e.tensor_tensor(tcol(6), tcol(1), tcol(3), ALU.max))
            s.do("vector", lambda e: e.tensor_tensor(tcol(5), tcol(5), tcol(6), ALU.max))
            for e_ in range(NEXP):
                s.do("vector", lambda e, e_=e_: e.tensor_tensor(
                    wn[:, :, e_:e_ + 1], se(e_), tcol(4), ALU.subtract))
            flatg = lambda t: t.ap()[:, :, :].rearrange("p a b -> p (a b)")
            s.wait("scalar", "cv", s.n("cv"))
            cs_ed = s.do("scalar", lambda e: e.activation(flatg(wn), flatg(wn), AF.Exp))
            s.wait("vector", "cs", cs_ed)
            for e_ in range(NEXP):
                s.do("vector", lambda e, e_=e_: e.tensor_tensor(
                    tcol(6), se(e_), tcol(5), ALU.is_ge))
                s.do("vector", lambda e, e_=e_: e.tensor_tensor(
                    wn[:, :, e_:e_ + 1], wn[:, :, e_:e_ + 1], tcol(6), ALU.mult))
            s.do("vector", lambda e: e.tensor_reduce(
                top[:, :, 6:7], wn[:, :, :], axis=AX.X, op=ALU.add))
            s.do("vector", lambda e: e.reciprocal(tcol(7), tcol(6)))
            for e_ in range(NEXP):
                s.do("vector", lambda e, e_=e_: e.tensor_tensor(
                    wn[:, :, e_:e_ + 1], wn[:, :, e_:e_ + 1], tcol(7), ALU.mult))
            cv_wn = s.n("cv")
            for gi in range(NGI):
                s.wait("tensor", "cv", max(cv_wn, s.n("cv")))
                cpe_t = s.do("tensor", lambda e, gi=gi: e.transpose(
                    ps[7][:NEXP, :128], wn[:, gi, :], iden[:, :]), inc="cpe")
                s.wait("vector", "cpe", cpe_t)
                s.do("vector", lambda e, gi=gi: e.tensor_copy(
                    wr[:NEXP, gi * 128:(gi + 1) * 128], ps[7][:NEXP, :128]))
            cv_wr = s.do("vector", lambda e: e.tensor_scalar(
                wr[:NEXP, :], wr[:NEXP, :], 0.5, None, ALU.mult))
            for e_ in range(NEXP):
                s.wait("tensor", "cv", max(cv_wr, s.n("cv")))
                cpe_w = s.do("tensor", lambda e, e_=e_: e.matmul(
                    ps[6][:, :], sel4[:NEXP, e_ * 128:(e_ + 1) * 128],
                    wr[:NEXP, :], start=True, stop=True), inc="cpe")
                s.wait("vector", "cpe", cpe_w)
                s.do("vector", lambda e, e_=e_: e.tensor_copy(
                    wsl[:, e_, :], ps[6][:, :]))
            ps01_war = s.n("cv")
            ps23_war = s.n("cv")
            prev_relu = s.n("cs")
            relu_cs = {}
            lnres = {}

            def norm1(e_):
                nonlocal ps23_war, prev_relu
                cv_rows, _ = lnres[e_]
                cpe_b = bcast_rw(cv_rows, ps23_war)
                s.wait("vector", "cpe", cpe_b)
                s.wait("vector", "cs", prev_relu)
                for f in range(NF):
                    s.do("vector", lambda e, f=f, e_=e_: e.tensor_mul(
                        lnsc[:, f, :], heb[e_][:, f, :], ps[2][:, :]))
                    s.do("vector", lambda e, f=f: e.tensor_sub(
                        lnsc[:, f, :], lnsc[:, f, :], ps[3][:, :]))
                ps23_war = s.n("cv")
                s.wait("scalar", "cv", ps23_war)
                for f in range(NF):
                    s.do("scalar", lambda e, f=f, e_=e_: e.activation(
                        hrb[e_][:, f, :], lnsc[:, f, :], AF.Relu,
                        scale=g1c[:, e_ * 8 + f:e_ * 8 + f + 1],
                        bias=be1c[:, e_ * 8 + f:e_ * 8 + f + 1]))
                prev_relu = s.n("cs")
                relu_cs[e_] = prev_relu

            for e_ in range(NEXP):
                cs_e = mm_stage(w1all, e_ * NF, xnb,
                                ("cv", cv_xn) if e_ == 0 else None,
                                heb[e_], b1c, e_ * NF, None)
                if e_ > 0:
                    norm1(e_ - 1)
                cpe_sum, cpe_sq = stats_blk(heb[e_], cs_e, ps01_war)
                lnres[e_] = ln_rows(cpe_sum, cpe_sq)
                ps01_war = lnres[e_][1]
            norm1(NEXP - 1)
            hob_war = st["hob_war"]

            def norm2(e_):
                nonlocal ps23_war
                cv_rows, _ = lnres[e_]
                cpe_b = bcast_rw(cv_rows, ps23_war)
                s.wait("vector", "cpe", cpe_b)
                for f in range(NF):
                    s.do("vector", lambda e, f=f, e_=e_: e.tensor_mul(
                        lnsc[:, f, :], hob[e_ % 2][:, f, :], ps[2][:, :]))
                    s.do("vector", lambda e, f=f: e.tensor_sub(
                        lnsc[:, f, :], lnsc[:, f, :], ps[3][:, :]))
                cv_ns = s.n("cv")
                for f in range(NF):
                    s.do("vector", lambda e, f=f, e_=e_: e.tensor_scalar(
                        lnsc[:, f, :], lnsc[:, f, :],
                        g2c[:, e_ * 8 + f:e_ * 8 + f + 1],
                        be2c[:, e_ * 8 + f:e_ * 8 + f + 1], ALU.mult, ALU.add))
                    s.do("vector", lambda e, f=f, e_=e_: e.tensor_mul(
                        tbuf[:, :], lnsc[:, f, :], wsl[:, e_, :]))
                    s.do("vector", lambda e, f=f, X_c=X_c: e.tensor_add(
                        X_c[:, f, :], X_c[:, f, :], tbuf[:, :]))
                ps23_war = cv_ns
                hob_war[e_ % 2] = cv_ns

            for e_ in range(NEXP):
                cs_e = mm_stage(w2all, e_ * NF, hrb[e_], ("cs", relu_cs[e_]),
                                hob[e_ % 2], b2c, e_ * NF,
                                ("cv", hob_war[e_ % 2]))
                cpe_sum, cpe_sq = stats_blk(hob[e_ % 2], cs_e, ps01_war)
                lnres[e_] = ln_rows(cpe_sum, cpe_sq)
                ps01_war = lnres[e_][1]
                norm2(e_)
            s.wait("gpsimd", "cv", s.n("cv"))
            s.dma("gpsimd", fm(y_out)[:, :, xsl], X_c[:, :, :], "qst")
            st["qst"][par] = s.n("qst")
            st["x_war"][par] = s.n("cv")


def build_moe():
    nc = bass.Bass("TRN2", target_bir_lowering=False)
    x_in = nc.dram_tensor("x", [D, TPC], F32, kind="ExternalInput").ap()
    w1all = nc.dram_tensor("w1all", [NEXP * D, D], F32, kind="ExternalInput").ap()
    w2all = nc.dram_tensor("w2all", [NEXP * D, D], F32, kind="ExternalInput").ap()
    gw_d = nc.dram_tensor("gw", [D, NEXP], F32, kind="ExternalInput").ap()
    gbrow_d = nc.dram_tensor("gbrow", [1, NEXP], F32, kind="ExternalInput").ap()
    b1c_d = nc.dram_tensor("eb1c", [128, NEXP * 8], F32, kind="ExternalInput").ap()
    b2c_d = nc.dram_tensor("eb2c", [128, NEXP * 8], F32, kind="ExternalInput").ap()
    g1c_d = nc.dram_tensor("g1c", [128, NEXP * 8], F32, kind="ExternalInput").ap()
    be1c_d = nc.dram_tensor("be1c", [128, NEXP * 8], F32, kind="ExternalInput").ap()
    g2c_d = nc.dram_tensor("g2c", [128, NEXP * 8], F32, kind="ExternalInput").ap()
    be2c_d = nc.dram_tensor("be2c", [128, NEXP * 8], F32, kind="ExternalInput").ap()
    sel4_d = nc.dram_tensor("sel4", [NEXP, NEXP * 128], F32, kind="ExternalInput").ap()
    id_d = nc.dram_tensor("ident", [128, 128], F32, kind="ExternalInput").ap()
    oc_d = nc.dram_tensor("ones_col", [128, 1], F32, kind="ExternalInput").ap()
    or_d = nc.dram_tensor("ones_row", [1, 128], F32, kind="ExternalInput").ap()
    y_out = nc.dram_tensor("y", [D, TPC], F32, kind="ExternalOutput").ap()
    NF = D // 128
    NGI = TC // 128

    s = Sched(nc)
    with s.es:
        _mk_sems(s)
        emit_moe(s, dict(x_in=x_in, w1all=w1all, w2all=w2all, gw_d=gw_d, gbrow_d=gbrow_d, b1c_d=b1c_d, b2c_d=b2c_d, g1c_d=g1c_d, be1c_d=be1c_d, g2c_d=g2c_d, be2c_d=be2c_d, sel4_d=sel4_d, id_d=id_d, oc_d=oc_d, or_d=or_d, y_out=y_out))
        s.wait("gpsimd", "qst", s.n("qst"))
        s.finalize()
    return nc


def emit_mamba_a(s, io):
    x_in = io["x_in"]
    wt = io["wt"]
    oc_d = io["oc_d"]; or_d = io["or_d"]
    xzp = io["xzp"]
    CH = DI // NCORES
    NF = D // 128
    NM = (2 * DI) // 128
    xzp5 = xzp.rearrange("j (h cb p t) -> j h cb p t", h=2, cb=CH // 128, p=128)
    with s.scope():
        X_c = s.sbuf("X_c", [128, NF, TPC])
        sqxn = s.sbuf("sqxn", [128, NF, TC])
        xnb = s.sbuf("xnb", [128, NF, TPC], BF16)
        ob = s.sbuf("ob", [128, NM, TC], BF16)
        wb = [s.sbuf("wb%d" % i, [128, NF, 128], BF16) for i in range(3)]
        oc = s.sbuf("oc", [128, 1])
        orow = s.sbuf("orow", [1, 128])
        rows = {k: s.sbuf("row_" + k, [1, TC]) for k in "mvtr"}
        ps_stat = s.psum("ps_stat", [1, TC])
        ps_stat2 = s.psum("ps_stat2", [1, TC])
        ps_bm = s.psum("ps_bm", [128, TC])
        ps_br = s.psum("ps_br", [128, TC])
        ps_mm = [s.psum("ps_mm0", [128, TC]), s.psum("ps_mm1", [128, TC])]
        s.dma("sync", oc[:, :], oc_d[:, :], "qc")
        s.dma("sync", orow[:1, :], or_d[:1, :], "qc")
        st = {"wpar": 0, "last_w": {0: 0, 1: 0, 2: 0},
              "last_evac": {0: 0, 1: 0}, "qst": 0}
        NCB = CH // 128
        x_cnt = s.dma("sync", X_c[:, :, :], fm(x_in)[:, :, :], "qx")
        cv_xns = []
        for tc_i in range(NTC):
            xsl = slice(tc_i * TC, (tc_i + 1) * TC)
            deps = {"x_ready": ("qx", x_cnt),
                    "sq_war": [("cpe", s.n("cpe"))],
                    "stat_war": [("cv", s.n("cv"))],
                    "bcast_war": [("cv", s.n("cv"))]}
            cv_xns.append(emit_ln(s, NF, X_c[:, :, xsl], sqxn, rows, oc, orow,
                                  ps_stat, ps_stat2, ps_bm, ps_br, deps,
                                  out=xnb[:, :, xsl]))
        for tc_i in range(NTC):
            xsl = slice(tc_i * TC, (tc_i + 1) * TC)
            cv_xn = cv_xns[tc_i]
            s.wait("scalar", "qst", st["qst"])
            for mc in range(NM):
                wp = st["wpar"]; st["wpar"] = (st["wpar"] + 1) % 3
                pp = mc % 2
                w_cnt = s.dma("sync", fl(wb[wp]),
                              wt[mc, :, :], "qw%d" % wp,
                              waits=[("cpe", st["last_w"][wp])])
                s.wait("tensor", "qw%d" % wp, w_cnt)
                if mc == 0:
                    s.wait("tensor", "cv", cv_xn)
                s.wait("tensor", "cs", st["last_evac"][pp])
                for kt in range(NF):
                    s.do("tensor", lambda e, wp=wp, pp=pp, kt=kt, xsl=xsl: e.matmul(
                        ps_mm[pp][:, :], wb[wp][:, kt, :], xnb[:, kt, xsl],
                        start=(kt == 0), stop=(kt == NF - 1)),
                        inc=("cpe" if kt == NF - 1 else None))
                cpe_mm = s.n("cpe")
                st["last_w"][wp] = cpe_mm
                s.wait("scalar", "cpe", cpe_mm)
                cs_e = s.do("scalar", lambda e, pp=pp, mc=mc:
                            e.activation(ob[:, mc, :], ps_mm[pp][:, :], AF.Copy))
                st["last_evac"][pp] = cs_e
                h_i, jj, cb = mc // 16, (mc % 16) // NCB, mc % NCB
                s.wait("gpsimd", "cs", cs_e)
                s.dma("gpsimd", xzp5[jj, h_i, cb, :, xsl], ob[:, mc, :], "qst")
            st["qst"] = s.n("qst")


def build_mamba_a():
    nc = bass.Bass("TRN2", target_bir_lowering=False)
    x_in = nc.dram_tensor("x", [D, TPC], F32, kind="ExternalInput").ap()
    wt = nc.dram_tensor("iprojt", [D, 2 * DI], F32, kind="ExternalInput").ap()
    oc_d = nc.dram_tensor("ones_col", [128, 1], F32, kind="ExternalInput").ap()
    or_d = nc.dram_tensor("ones_row", [1, 128], F32, kind="ExternalInput").ap()
    xz_out = nc.dram_tensor("xz", [2 * DI, TPC], F32, kind="ExternalOutput").ap()
    NF = D // 128
    NM = (2 * DI) // 128
    s = Sched(nc)
    with s.es:
        _mk_sems(s)
        emit_mamba_a(s, dict(x_in=x_in, wt=wt, oc_d=oc_d, or_d=or_d, xz_out=xz_out))
        s.wait("gpsimd", "qst", s.n("qst"))
        s.finalize()
    return nc



def emit_mamba_b(s, io):
    x_in = io["x_in"]
    wc_d = io["wc_d"]; cb_d = io["cb_d"]
    xccp = io["y_out"]
    CH = DI // NCORES
    NPT = CH // 128
    TCH = 4096
    JPC = TCH // TPC
    with s.scope():
        xin = s.sbuf("xin", [128, NPT, TCH + 3], BF16)
        accb = s.sbuf("accb", [128, NPT, TCH])
        ov = s.sbuf("ov", [128, NPT, TCH], BF16)
        wc = s.sbuf("sb_wc", [128, NPT * DCONV])
        cb = s.sbuf("sb_cb", [128, NPT])
        s.dma("sync", wc[:, :], wc_d[:, :], "qc")
        s.dma("sync", cb[:, :], cb_d[:, :], "qc")
        st = {"qst": 0}
        xr = x_in.rearrange("(pt p) j t -> p pt j t", p=128)
        yr = xccp.rearrange("j (pt p t) -> pt p j t", pt=NPT, p=128)
        for ci in range(S_TOT // TCH):
            j0 = ci * JPC
            s.wait("vector", "qst", st["qst"])
            s.wait("scalar", "qst", st["qst"])
            if ci == 0:
                cv_z = s.do("vector", lambda e: e.memset(xin[:, :, 0:3], 0.0))
                s.wait("sync", "cv", cv_z)
            else:
                s.dma("sync", xin[:, :, 0:3], xr[:, :, j0 - 1, TPC - 3:],
                      "qx", waits=[("cv", s.n("cv"))])
            for pt in range(NPT):
                x_cnt = s.dma("sync",
                              xin[:, pt, 3:].rearrange("p (j t) -> p j t",
                                                       j=JPC),
                              xr[:, pt, j0:j0 + JPC, :], "qx",
                              waits=([("cv", s.n("cv"))] if ci == 0 and pt == 0
                                     else []))
            s.wait("vector", "qx", x_cnt)
            s.wait("vector", "qc", s.n("qc"))
            s.wait("scalar", "qc", s.n("qc"))
            for pt in range(NPT):
                s.do("vector", lambda e, pt=pt: e.tensor_scalar(
                    accb[:, pt, :], xin[:, pt, 3:3 + TCH],
                    wc[:, pt * DCONV + 3:pt * DCONV + 4], None, ALU.mult))
                for k in (2, 1, 0):
                    s.do("vector", lambda e, pt=pt, k=k: e.scalar_tensor_tensor(
                        accb[:, pt, :], xin[:, pt, k:k + TCH],
                        wc[:, pt * DCONV + k:pt * DCONV + k + 1],
                        accb[:, pt, :], ALU.mult, ALU.add))
                cv_a = s.n("cv")
                s.wait("scalar", "cv", cv_a)
                s.do("scalar", lambda e, pt=pt: e.activation(
                    ov[:, pt, :], accb[:, pt, :], AF.Silu,
                    bias=cb[:, pt:pt + 1]))
            s.wait("gpsimd", "cs", s.n("cs"))
            for pt in range(NPT):
                s.dma("gpsimd", yr[pt][:, j0:j0 + JPC, :],
                      ov[:, pt, :].rearrange("p (j t) -> p j t", j=JPC), "qst")
            st["qst"] = s.n("qst")


def build_mamba_b():
    nc = bass.Bass("TRN2", target_bir_lowering=False)
    CH = DI // NCORES
    NPT = CH // 128
    TCH = 4096
    x_in = nc.dram_tensor("x", [CH, S_TOT], F32, kind="ExternalInput").ap()
    wc_d = nc.dram_tensor("wc", [128, NPT * DCONV], F32, kind="ExternalInput").ap()
    cb_d = nc.dram_tensor("cb", [128, NPT], F32, kind="ExternalInput").ap()
    y_out = nc.dram_tensor("y", [CH, S_TOT], F32, kind="ExternalOutput").ap()
    s = Sched(nc)
    with s.es:
        _mk_sems(s)
        emit_mamba_b(s, dict(x_in=x_in, wc_d=wc_d, cb_d=cb_d, y_out=y_out))
        s.wait("gpsimd", "qst", s.n("qst"))
        s.finalize()
    return nc



def emit_mamba_c(s, io):
    xc_in = io["xc_in"]
    xpt_d = io["xpt_d"]; dpt_d = io["dpt_d"]; dtb_d = io["dtb_d"]
    b_out = io["b_out"]; c_out = io["c_out"]; d_out = io["d_out"]
    NK = DI // 128
    with s.scope():
        xcb = s.sbuf("xcb", [128, NK, TC], BF16)
        xpt = s.sbuf("sb_xpt", [128, NK, 96], BF16)
        dpt = s.sbuf("sb_dpt", [DTR, DI], BF16)
        xdbb = s.sbuf("xdbb", [DTR, TC], BF16)
        dtb = s.sbuf("sb_dtb", [128, NK])
        xdbl = s.sbuf("xdbl", [96, TC])
        dl = s.sbuf("dl", [128, NK, TC])
        sp1 = s.sbuf("sp1", [128, TC])
        sp2 = s.sbuf("sp2", [128, TC])
        ps_x = s.psum("ps_x", [96, TC])
        ps_mm = [s.psum("ps_mm0", [128, TC]), s.psum("ps_mm1", [128, TC])]
        c1 = s.const(1.0)
        cn1 = s.const(-1.0)
        s.dma("sync", xpt[:, :, :], fm(xpt_d), "qc")
        s.dma("sync", dpt[:, :], dpt_d[:, :], "qc")
        s.dma("sync", dtb[:, :], dtb_d[:, :], "qc")
        qconst = s.n("qc")
        st = {"qst": 0, "last_read": {0: 0, 1: 0}}
        for tc_i in range(NTC):
            xsl = slice(tc_i * TC, (tc_i + 1) * TC)
            x_cnt = s.dma("sync", xcb[:, :, :], fm(xc_in)[:, :, xsl], "qx",
                          waits=[("cpe", s.n("cpe"))])
            s.wait("tensor", "qx", x_cnt)
            s.wait("tensor", "qc", qconst)
            s.wait("scalar", "qc", qconst)
            s.wait("tensor", "cv", s.n("cv"))
            for kt in range(NK):
                s.do("tensor", lambda e, kt=kt: e.matmul(
                    ps_x[:, :], xpt[:, kt, :], xcb[:, kt, :],
                    start=(kt == 0), stop=(kt == NK - 1)),
                    inc=("cpe" if kt == NK - 1 else None))
            cpe_x = s.n("cpe")
            s.wait("vector", "cpe", cpe_x)
            s.wait("vector", "qst", st["qst"])
            cv_xd = s.do("vector", lambda e: e.tensor_copy(xdbl[:, :], ps_x[:, :]))
            cv_xd = s.do("vector", lambda e: e.tensor_copy(xdbb[:DTR, :],
                                                           xdbl[:DTR, :]))
            s.wait("gpsimd", "cv", cv_xd)
            s.dma("gpsimd", b_out[:, xsl], xdbl[64:80, :], "qst")
            s.dma("gpsimd", c_out[:, xsl], xdbl[80:96, :], "qst")
            for mc in range(NK):
                pp = mc % 2
                s.wait("tensor", "cv", cv_xd)
                s.wait("tensor", "cs", st["last_read"][pp])
                s.do("tensor", lambda e, pp=pp, mc=mc: e.matmul(
                    ps_mm[pp][:, :], dpt[:, mc * 128:(mc + 1) * 128],
                    xdbb[:DTR, :], start=True, stop=True), inc="cpe")
                cpe_d = s.n("cpe")
                s.wait("scalar", "cpe", cpe_d)
                s.wait("scalar", "cv", s.n("cv"))
                s.do("scalar", lambda e, pp=pp, mc=mc: e.activation(
                    sp1[:, :], ps_mm[pp][:, :], AF.Abs, bias=dtb[:, mc:mc + 1]))
                s.do("scalar", lambda e: e.activation(
                    sp1[:, :], sp1[:, :], AF.Exp, scale=cn1))
                s.do("scalar", lambda e: e.activation(
                    sp1[:, :], sp1[:, :], AF.Ln, bias=c1))
                cs_r = s.do("scalar", lambda e, pp=pp, mc=mc: e.activation(
                    sp2[:, :], ps_mm[pp][:, :], AF.Relu, bias=dtb[:, mc:mc + 1]))
                st["last_read"][pp] = cs_r
                s.wait("vector", "cs", cs_r)
                s.do("vector", lambda e, mc=mc: e.tensor_add(
                    dl[:, mc, :], sp1[:, :], sp2[:, :]))
            s.wait("gpsimd", "cv", s.n("cv"))
            s.dma("gpsimd", fm(d_out)[:, :, xsl], dl[:, :, :], "qst")
            st["qst"] = s.n("qst")


def build_mamba_c():
    nc = bass.Bass("TRN2", target_bir_lowering=False)
    xc_in = nc.dram_tensor("xc", [DI, TPC], F32, kind="ExternalInput").ap()
    xpt_d = nc.dram_tensor("xprojt", [DI, 96], F32, kind="ExternalInput").ap()
    dpt_d = nc.dram_tensor("dprojt", [DTR, DI], F32, kind="ExternalInput").ap()
    dtb_d = nc.dram_tensor("dtbc", [128, DI // 128], F32, kind="ExternalInput").ap()
    b_out = nc.dram_tensor("bm", [NST, TPC], F32, kind="ExternalOutput").ap()
    c_out = nc.dram_tensor("cm", [NST, TPC], F32, kind="ExternalOutput").ap()
    d_out = nc.dram_tensor("delta", [DI, TPC], F32, kind="ExternalOutput").ap()
    NK = DI // 128
    s = Sched(nc)
    with s.es:
        _mk_sems(s)
        emit_mamba_c(s, dict(xc_in=xc_in, xpt_d=xpt_d, dpt_d=dpt_d, dtb_d=dtb_d, b_out=b_out, c_out=c_out, d_out=d_out))
        s.wait("gpsimd", "qst", s.n("qst"))
        s.finalize()
    return nc



def emit_mamba_d(s, io):
    dl_in = io["dl_in"]
    xc_in = io["xc_in"]
    z_in = io["z_in"]
    b_in = io["b_in"]
    c_in = io["c_in"]
    a_d = io["a_d"]; dc_d = io["dc_d"]; sel_d = io["sel_d"]
    ycp = io["y_out"]
    CH = DI // NCORES
    NPT = CH // 128
    TCH = 4096
    NSB = TCH // TC
    JPC = TCH // TPC
    with s.scope():
        dlb = s.sbuf("dlb", [128, TCH])
        xcb = s.sbuf("xcb", [128, TCH], BF16)
        zb = s.sbuf("zb", [128, TCH], BF16)
        ub = s.sbuf("ub", [128, TCH], BF16)
        dab = s.sbuf("dab", [128, TCH])
        dbb = s.sbuf("dbb", [128, TCH], BF16)
        bbp = [s.sbuf("bbp%d" % i, [128, TC], BF16) for i in range(2)]
        hb = s.sbuf("hb", [128, TCH])
        yb = s.sbuf("yb", [128, TCH])
        ybb = s.sbuf("ybb", [128, TCH], BF16)
        bsb = s.sbuf("bsb", [NST, TCH])
        csb = s.sbuf("csb", [NST, TCH])
        bsbb = s.sbuf("bsbb", [NST, TCH], BF16)
        csbb = s.sbuf("csbb", [NST, TCH], BF16)
        acol = s.sbuf("sb_acol", [128, NPT * NST])
        dcol = s.sbuf("sb_dcol", [128, NPT])
        sel = s.sbuf("sb_sel16", [NST, NST * 128], BF16)
        carry = s.sbuf("carry", [128, NPT * NST])
        ps_b = [s.psum("ps_b0", [128, TC]), s.psum("ps_b1", [128, TC])]
        s.dma("sync", acol[:, :], a_d[:, :], "qc")
        s.dma("sync", dcol[:, :], dc_d[:, :], "qc")
        s.dma("sync", sel[:, :], sel_d[:, :], "qc")
        qconst = s.n("qc")
        cv_c0 = s.do("vector", lambda e: e.memset(carry[:, :], 0.0))
        s.wait("scalar", "qc", qconst)
        cs_aexp = s.do("scalar", lambda e: e.activation(acol[:, :], acol[:, :],
                                                        AF.Exp))
        s.wait("vector", "cs", cs_aexp)
        cv_aneg = s.do("vector", lambda e: e.tensor_scalar(
            acol[:, :], acol[:, :], -1.0, None, ALU.mult))
        s.wait("scalar", "cv", cv_aneg)
        st = {"qst": 0, "pbpar": 0, "pb_read": {0: ("cv", 0), 1: ("cv", 0)},
              "bb_war": {0: 0, 1: 0}}
        dlr = dl_in.rearrange("(pt p) j t -> pt p j t", p=128)
        xcr = xc_in.rearrange("(pt p) j t -> pt p j t", p=128)
        zr = z_in.rearrange("(pt p) j t -> pt p j t", p=128)
        yr = ycp.rearrange("j (pt p t) -> pt p j t", pt=NPT, p=128)
        for pt in range(NPT):
            for ci in range(S_TOT // TCH):
                j0 = ci * JPC
                jsl = slice(j0, j0 + JPC)
                s.wait("vector", "qst", st["qst"])
                s.wait("scalar", "qst", st["qst"])
                v3 = lambda t: t.rearrange("p (j q) -> p j q", j=JPC)
                s.dma("sync", v3(dlb[:, :]), dlr[pt, :, jsl, :], "qx",
                      waits=[("cs", s.n("cs")), ("cv", s.n("cv")),
                             ("cpe", s.n("cpe"))])
                s.dma("sync", v3(xcb[:, :]), xcr[pt, :, jsl, :], "qx")
                s.dma("sync", v3(zb[:, :]), zr[pt, :, jsl, :], "qx")
                s.dma("sync", v3(bsb[:NST, :]), b_in[:, jsl, :], "qx")
                x_cnt = s.dma("sync", v3(csb[:NST, :]), c_in[:, jsl, :], "qx")
                s.wait("vector", "qx", x_cnt)
                s.wait("vector", "cpe", s.n("cpe"))
                s.do("vector", lambda e: e.tensor_copy(bsbb[:, :], bsb[:, :]))
                cv_bc = s.do("vector", lambda e: e.tensor_copy(csbb[:, :],
                                                               csb[:, :]))
                s.do("vector", lambda e: e.tensor_mul(ub[:, :], dlb[:, :],
                                                      xcb[:, :]))
                s.wait("tensor", "cv", cv_bc)
                for n in range(NST):
                    cidx = pt * NST + n
                    s.wait("scalar", "qx", x_cnt)
                    s.wait("scalar", "cv", s.n("cv"))
                    cs_da = s.do("scalar", lambda e, cidx=cidx: e.activation(
                        dab[:, :], dlb[:, :], AF.Exp,
                        scale=acol[:, cidx:cidx + 1]))
                    for sb in range(NSB):
                        pb = st["pbpar"]; st["pbpar"] ^= 1
                        ssl = slice(sb * TC, (sb + 1) * TC)
                        s.wait("tensor", *st["pb_read"][pb])
                        if n == 0 and sb == 0 and pt == 0 and ci == 0:
                            s.wait("tensor", "qc", qconst)
                        cpe_b = s.do("tensor", lambda e, pb=pb, n=n, ssl=ssl:
                            e.matmul(ps_b[pb][:, :],
                                     sel[:NST, n * 128:(n + 1) * 128],
                                     bsbb[:NST, ssl], start=True, stop=True),
                            inc="cpe")
                        s.wait("scalar", "cpe", cpe_b)
                        s.wait("scalar", "cv", st["bb_war"][pb])
                        cs_ev = s.do("scalar", lambda e, pb=pb: e.activation(
                            bbp[pb][:, :], ps_b[pb][:, :], AF.Copy))
                        st["pb_read"][pb] = ("cs", cs_ev)
                        s.wait("vector", "cs", cs_ev)
                        st["bb_war"][pb] = s.do("vector",
                            lambda e, pb=pb, ssl=ssl: e.tensor_mul(
                                dbb[:, ssl], ub[:, ssl], bbp[pb][:, :]))
                    s.wait("vector", "cs", cs_da)
                    s.do("vector", lambda e, cidx=cidx: e.tensor_tensor_scan(
                        hb[:, :], dab[:, :], dbb[:, :],
                        carry[:, cidx:cidx + 1], ALU.mult, ALU.add))
                    s.do("vector", lambda e, cidx=cidx: e.tensor_copy(
                        carry[:, cidx:cidx + 1], hb[:, TCH - 1:TCH]))
                    for sb in range(NSB):
                        pb = st["pbpar"]; st["pbpar"] ^= 1
                        ssl = slice(sb * TC, (sb + 1) * TC)
                        s.wait("tensor", *st["pb_read"][pb])
                        cpe_c = s.do("tensor", lambda e, pb=pb, n=n, ssl=ssl:
                            e.matmul(ps_b[pb][:, :],
                                     sel[:NST, n * 128:(n + 1) * 128],
                                     csbb[:NST, ssl], start=True, stop=True),
                            inc="cpe")
                        s.wait("vector", "cpe", cpe_c)
                        st["pb_read"][pb] = ("cv", s.do("vector",
                            lambda e, pb=pb, ssl=ssl: e.tensor_mul(
                                dbb[:, ssl], hb[:, ssl], ps_b[pb][:, :])))
                    if n == 0:
                        s.do("vector", lambda e: e.tensor_copy(yb[:, :],
                                                               dbb[:, :]))
                    else:
                        s.do("vector", lambda e: e.tensor_add(yb[:, :], yb[:, :],
                                                              dbb[:, :]))
                s.do("vector", lambda e, pt=pt: e.scalar_tensor_tensor(
                    yb[:, :], xcb[:, :], dcol[:, pt:pt + 1], yb[:, :],
                    ALU.mult, ALU.add))
                cs_z = s.do("scalar", lambda e: e.activation(zb[:, :], zb[:, :],
                                                             AF.Silu))
                s.wait("vector", "cs", cs_z)
                cv_y = s.do("vector", lambda e: e.tensor_mul(ybb[:, :], yb[:, :],
                                                             zb[:, :]))
                s.wait("gpsimd", "cv", cv_y)
                s.dma("gpsimd", yr[pt, :, jsl, :], v3(ybb[:, :]), "qst")
                st["qst"] = s.n("qst")


def build_mamba_d():
    nc = bass.Bass("TRN2", target_bir_lowering=False)
    CH = DI // NCORES
    NPT = CH // 128
    TCH = 4096
    NSB = TCH // TC
    dl_in = nc.dram_tensor("delta", [CH, S_TOT], F32, kind="ExternalInput").ap()
    xc_in = nc.dram_tensor("xc", [CH, S_TOT], F32, kind="ExternalInput").ap()
    z_in = nc.dram_tensor("z", [CH, S_TOT], F32, kind="ExternalInput").ap()
    b_in = nc.dram_tensor("bm", [NST, S_TOT], F32, kind="ExternalInput").ap()
    c_in = nc.dram_tensor("cm", [NST, S_TOT], F32, kind="ExternalInput").ap()
    a_d = nc.dram_tensor("acol", [128, NPT * NST], F32, kind="ExternalInput").ap()
    dc_d = nc.dram_tensor("dcol", [128, NPT], F32, kind="ExternalInput").ap()
    sel_d = nc.dram_tensor("sel16", [NST, NST * 128], F32, kind="ExternalInput").ap()
    y_out = nc.dram_tensor("y", [CH, S_TOT], F32, kind="ExternalOutput").ap()
    s = Sched(nc)
    with s.es:
        _mk_sems(s)
        emit_mamba_d(s, dict(dl_in=dl_in, xc_in=xc_in, z_in=z_in, b_in=b_in, c_in=c_in, a_d=a_d, dc_d=dc_d, sel_d=sel_d, y_out=y_out))
        s.wait("gpsimd", "qst", s.n("qst"))
        s.finalize()
    return nc



def emit_mamba_e(s, io):
    NK = DI // 128
    NF = D // 128
    x_in = io["x_in"]
    y_in = io["y_in"]
    wt = io["wt"]
    y_out = io["y_out"]
    with s.scope():
        X_cs = [s.sbuf("X_c%d" % i, [128, NF, TC]) for i in range(2)]
        ymb2 = [s.sbuf("ymbb%d" % i, [128, NK, TC], BF16) for i in range(2)]
        wb = [s.sbuf("wb0", [128, NK, 128], BF16),
              s.sbuf("wb1", [128, NK, 128], BF16)]
        ps_mm = [s.psum("ps_mm0", [128, TC]), s.psum("ps_mm1", [128, TC])]
        st = {"wpar": 0, "last_w": {0: 0, 1: 0}, "last_use": {0: 0, 1: 0},
              "qst": [0, 0], "xw": [0, 0], "ymw": [0, 0]}
        for tc_i in range(NTC):
            par = tc_i % 2
            X_c = X_cs[par]
            ymbb = ymb2[par]
            xsl = slice(tc_i * TC, (tc_i + 1) * TC)
            s.dma("sync", X_c[:, :, :], fm(x_in)[:, :, xsl], "qx",
                  waits=[("qst", st["qst"][par]), ("cv", st["xw"][par])])
            x_cnt = s.dma("sync", ymbb[:, :, :], fm(y_in)[:, :, xsl], "qx",
                          waits=[("cpe", st["ymw"][par])])
            for mc in range(NF):
                wp = st["wpar"]; st["wpar"] ^= 1
                w_cnt = s.dma("sync", fl(wb[wp]),
                              wt[mc, :, :], "qw%d" % wp,
                              waits=[("cpe", st["last_w"][wp])])
                s.wait("tensor", "qw%d" % wp, w_cnt)
                s.wait("tensor", "qx", x_cnt)
                s.wait("tensor", "cv", st["last_use"][wp])
                for kt in range(NK):
                    s.do("tensor", lambda e, wp=wp, kt=kt, ymbb=ymbb: e.matmul(
                        ps_mm[wp][:, :], wb[wp][:, kt, :], ymbb[:, kt, :],
                        start=(kt == 0), stop=(kt == NK - 1)),
                        inc=("cpe" if kt == NK - 1 else None))
                cpe_mm = s.n("cpe")
                st["last_w"][wp] = cpe_mm
                s.wait("vector", "cpe", cpe_mm)
                if mc == 0:
                    s.wait("vector", "qx", x_cnt)
                    s.wait("vector", "qst", st["qst"][par])
                st["last_use"][wp] = s.do("vector",
                    lambda e, wp=wp, mc=mc, X_c=X_c: e.tensor_add(
                        X_c[:, mc, :], X_c[:, mc, :], ps_mm[wp][:, :]))
            st["ymw"][par] = s.n("cpe")
            st["xw"][par] = s.n("cv")
            s.wait("gpsimd", "cv", s.n("cv"))
            s.dma("gpsimd", fm(y_out)[:, :, xsl], X_c[:, :, :], "qst")
            st["qst"][par] = s.n("qst")


def build_mamba_e():
    nc = bass.Bass("TRN2", target_bir_lowering=False)
    x_in = nc.dram_tensor("x", [D, TPC], F32, kind="ExternalInput").ap()
    y_in = nc.dram_tensor("ym", [DI, TPC], F32, kind="ExternalInput").ap()
    wt = nc.dram_tensor("oprojt", [DI, D], F32, kind="ExternalInput").ap()
    y_out = nc.dram_tensor("y", [D, TPC], F32, kind="ExternalOutput").ap()
    NK = DI // 128
    NF = D // 128
    s = Sched(nc)
    with s.es:
        _mk_sems(s)
        emit_mamba_e(s, dict(x_in=x_in, y_in=y_in, wt=wt, y_out=y_out))
        s.wait("gpsimd", "qst", s.n("qst"))
        s.finalize()
    return nc







def emit_prep(s, io):
    jobs = io["jobs"]
    tjobs = io["tjobs"]
    ffnj = io["ffnj"]
    oc_d = io["oc_d"]
    with s.scope():
        CW = 4096
        NB = 3
        stq = ["qs0", "qs1", "qz"]
        wf = [s.sbuf("pw%d" % i, [128, CW]) for i in range(NB)]
        wb = [s.sbuf("pb%d" % i, [128, CW], BF16) for i in range(NB)]
        oc = s.sbuf("poc", [128, 1])
        acc = s.sbuf("pacc", [128, 32])
        ps_as = [s.psum("ps_pa0", [128, 512]), s.psum("ps_pa1", [128, 512])]
        qc0 = s.dma("sync", oc[:, :], oc_d[:, :], "qc")
        par = {"p": 0, "done": {i: ("cv", 0) for i in range(NB)},
               "st": {i: 0 for i in range(NB)}}
        eng_flip = {"e": 0}
        for (src_ap, dst_ap, R, C) in jobs:
            assert C <= CW
            r0 = 0
            while r0 < R:
                rr = min(128, R - r0)
                p = par["p"]; par["p"] = (par["p"] + 1) % NB
                ds, dc = par["done"][p]
                w_cnt = s.dma("sync", wf[p][:rr, :C], src_ap[r0:r0 + rr, :],
                              "qw%d" % p, waits=[(ds, dc)])
                eng = "scalar" if eng_flip["e"] == 0 else "vector"
                eng_flip["e"] ^= 1
                s.wait(eng, "qw%d" % p, w_cnt)
                s.wait(eng, stq[p], par["st"][p])
                if eng == "scalar":
                    c = s.do("scalar", lambda e, p=p, rr=rr, C=C:
                             e.copy(wb[p][:rr, :C], wf[p][:rr, :C]))
                    par["done"][p] = ("cs", c)
                else:
                    c = s.do("vector", lambda e, p=p, rr=rr, C=C:
                             e.tensor_copy(wb[p][:rr, :C], wf[p][:rr, :C]))
                    par["done"][p] = ("cv", c)
                par["st"][p] = s.dma("gpsimd", dst_ap[r0:r0 + rr, :],
                                     wb[p][:rr, :C], stq[p],
                                     waits=[par["done"][p]])
                r0 += rr
        for (src_ap, dst3d, R, C, nkt) in tjobs:
            nmc = C // 128
            for rb in range(R // 128):
                r0 = rb * 128
                p = par["p"]; par["p"] = (par["p"] + 1) % NB
                ds, dc = par["done"][p]
                w_cnt = s.dma("sync", wf[p][:128, :C], src_ap[r0:r0 + 128, :],
                              "qw%d" % p, waits=[(ds, dc)])
                eng = "scalar" if eng_flip["e"] == 0 else "vector"
                eng_flip["e"] ^= 1
                s.wait(eng, "qw%d" % p, w_cnt)
                s.wait(eng, stq[p], par["st"][p])
                if eng == "scalar":
                    c = s.do("scalar", lambda e, p=p, C=C:
                             e.copy(wb[p][:128, :C], wf[p][:128, :C]))
                    par["done"][p] = ("cs", c)
                else:
                    c = s.do("vector", lambda e, p=p, C=C:
                             e.tensor_copy(wb[p][:128, :C], wf[p][:128, :C]))
                    par["done"][p] = ("cv", c)
                sec, kt = rb // nkt, rb % nkt
                par["st"][p] = s.dma(
                    "gpsimd",
                    dst3d[sec * nmc:(sec + 1) * nmc, :,
                          kt * 128:(kt + 1) * 128].rearrange("a q m -> q a m"),
                    wb[p][:128, :C].rearrange("q (a m) -> q a m", m=128),
                    stq[p], waits=[par["done"][p]])
        last_ac = {0: 0, 1: 0}
        for (src_ap, dst_ap, K, M, acol_dst, ascale) in ffnj:
            nkt = K // 128
            nmc = M // 128
            s.wait("scalar", "qst", s.n("qst"))
            for mc in range(nmc):
                p = par["p"]; par["p"] = (par["p"] + 1) % NB
                ds, dc = par["done"][p]
                w_cnt = s.dma("sync",
                              wf[p][:, :nkt * 128].rearrange(
                                  "q (kt m) -> q kt m", m=128),
                              src_ap[:, mc * 128:(mc + 1) * 128]
                              .rearrange("(kt q) m -> q kt m", q=128),
                              "qw%d" % p,
                              waits=[(ds, dc), ("cpe", s.n("cpe"))])
                s.wait("scalar", "qw%d" % p, w_cnt)
                s.wait("scalar", stq[p], par["st"][p])
                cs_sg = s.do("scalar", lambda e, p=p, n=nkt * 128: e.sign(
                    wb[p][:, :n], wf[p][:, :n]))
                s.wait("vector", "cs", cs_sg)
                cv_ab = s.do("vector", lambda e, p=p, n=nkt * 128: e.tensor_mul(
                    wf[p][:, :n], wf[p][:, :n], wb[p][:, :n]))
                s.wait("tensor", "cv", cv_ab)
                if mc == 0:
                    s.wait("tensor", "qc", qc0)
                s.wait("tensor", "cs", last_ac[mc % 2])
                for kt in range(nkt):
                    s.do("tensor", lambda e, p=p, kt=kt, mc=mc: e.matmul(
                        ps_as[mc % 2][:, :1],
                        wf[p][:, kt * 128:(kt + 1) * 128], oc[:, :1],
                        start=(kt == 0), stop=(kt == nkt - 1),
                        skip_group_check=True),
                        inc=("cpe" if kt == nkt - 1 else None))
                cpe_a = s.n("cpe")
                par["done"][p] = ("cpe", cpe_a)
                s.wait("scalar", "cpe", cpe_a)
                last_ac[mc % 2] = s.do("scalar",
                    lambda e, mc=mc, sc=s.const(ascale): e.activation(
                        acc[:, mc:mc + 1], ps_as[mc % 2][:, :1],
                        AF.Identity, scale=sc))
                par["st"][p] = s.dma("gpsimd", dst_ap[mc, :, :],
                                     wb[p][:, :nkt * 128],
                                     stq[p], waits=[("cs", cs_sg)])
            s.wait("gpsimd", "cs", s.n("cs"))
            s.dma("gpsimd", acol_dst, acc[:, :nmc], "qst")


def _cc(s, kind, in_ap, out_ap):
    from concourse.aot_env import allow_ici_all_to_all

    def f(e, i=in_ap, o=out_ap, k=kind):
        with allow_ici_all_to_all():
            return e.collective_compute(k, ALU.bypass,
                                        [list(range(NCORES))], [i], [o])
    return s.do("gpsimd", f, inc="cc", amt=1)


def build_fused():
    nc = bass.Bass("TRN2", target_bir_lowering=False)
    CH = DI // NCORES
    NPT = CH // 128

    def ein(name, shape):
        return nc.dram_tensor(name, shape, F32, kind="ExternalInput").ap()

    def scr(name, shape):
        return nc.dram_tensor(name, shape, F32).ap()

    def scrb(name, shape):
        return nc.dram_tensor(name, shape, BF16).ap()

    x0 = ein("x", [D, TPC])
    w1t = ein("w1t", [D, H_FFN]); w2t = ein("w2t", [H_FFN, D])
    b1c = ein("b1c", [128, H_FFN // 128]); b2c = ein("b2c", [128, D // 128])
    oc = ein("ones_col", [128, 1]); orow = ein("ones_row", [1, 128])
    wts_a = {nm: ein(nm, [D, D]) for nm in ("wp1t", "wp2t", "cvt", "wat", "wxt")}
    brow_a = ein("brow_a", [1, 5 * D]); lamc = ein("lamc", [128, 8])
    wp3t = ein("wp3t", [D, D]); brow_p3 = ein("brow_p3", [1, D])
    w1all = ein("w1all", [NEXP * D, D]); w2all = ein("w2all", [NEXP * D, D])
    gw = ein("gw", [D, NEXP]); gbrow = ein("gbrow", [1, NEXP])
    eb1c = ein("eb1c", [128, NEXP * 8]); eb2c = ein("eb2c", [128, NEXP * 8])
    g1c = ein("g1c", [128, NEXP * 8]); be1c = ein("be1c", [128, NEXP * 8])
    g2c = ein("g2c", [128, NEXP * 8]); be2c = ein("be2c", [128, NEXP * 8])
    sel4 = ein("sel4", [NEXP, NEXP * 128]); ident = ein("ident", [128, 128])
    iprojt = ein("iprojt", [D, 2 * DI])
    wc = ein("wc", [128, NPT * DCONV]); cb = ein("cb", [128, NPT])
    xprojt = ein("xprojt", [DI, 96]); dprojt = ein("dprojt", [DTR, DI])
    dtbc = ein("dtbc", [128, DI // 128])
    acol = ein("acol", [128, NPT * NST]); dcol = ein("dcol", [128, NPT])
    sel16 = nc.dram_tensor("sel16", [NST, NST * 128], BF16,
                           kind="ExternalInput").ap()
    oprojt = ein("oprojt", [DI, D])
    y_fin = nc.dram_tensor("y", [D, TPC], F32, kind="ExternalOutput").ap()

    xd = [scr("xd%d" % i, [D, TPC]) for i in range(1, 6)]
    x1g_d = scrb("x1g_d", [D, TPC])
    av_d = scr("av_d", [D, TPC]); gv_d = scrb("gv_d", [D, TPC])
    exg_l = scr("exg_l", [128, 16])
    exg_all = scr("exg_all", [NCORES * 128, 16])
    msk_in = ein("msk", [128, NCORES])
    def scrb(name, shape):
        return nc.dram_tensor(name, shape, BF16).ap()

    xzp = scrb("xzp", [NCORES, 2 * CH * TPC])
    xzc2 = scrb("xzc2", [NCORES, 2 * CH * TPC])
    xccp = scrb("xccp", [NCORES, CH * TPC])
    xct2 = scrb("xct2", [NCORES, CH * TPC])
    bm_d = scr("bm_d", [NST, TPC]); cm_d = scr("cm_d", [NST, TPC])
    bmg2 = scr("bmg2", [NCORES, NST * TPC]); cmg2 = scr("cmg2", [NCORES, NST * TPC])
    delta_d = scr("delta_d", [DI, TPC])
    dlc2 = scr("dlc2", [NCORES, CH * TPC])
    ycp = scrb("ycp", [NCORES, CH * TPC])
    yt2 = scrb("yt2", [NCORES, CH * TPC])

    bw1 = scrb("bw1", [H_FFN // 128, 128, D])
    bw2 = scrb("bw2", [D // 128, 128, H_FFN])
    a1sc = scr("a1sc", [128, H_FFN // 128]); a2sc = scr("a2sc", [128, D // 128])
    bwts_a = {nm: scrb("b" + nm, [D // 128, 128, D])
              for nm in ("wp1t", "wp2t", "cvt", "wat", "wxt")}
    bwp3 = scrb("bwp3", [D // 128, 128, D])
    bw1all = scrb("bw1all", [NEXP * (D // 128), 128, D])
    bw2all = scrb("bw2all", [NEXP * (D // 128), 128, D])
    bgw = scrb("bgw", [D, NEXP])
    biproj = scrb("biproj", [2 * DI // 128, 128, D])
    bxproj = scrb("bxproj", [DI, 96]); bdproj = scrb("bdproj", [DTR, DI])
    boproj = scrb("boproj", [D // 128, 128, DI])

    browb = scrb("browb", [1, 5 * D])
    prep_jobs = [(brow_a[:, :4096], browb[:, :4096], 1, 4096),
                 (brow_a[:, 4096:], browb[:, 4096:], 1, 1024),
                 (gw, bgw, D, NEXP),
                 (xprojt, bxproj, DI, 96), (dprojt, bdproj, DTR, DI)]
    prep_tjobs = [(wts_a[nm], bwts_a[nm], D, D, D // 128)
                  for nm in ("wp1t", "wp2t", "cvt", "wat", "wxt")]
    prep_tjobs += [(wp3t, bwp3, D, D, D // 128),
                   (w1all, bw1all, NEXP * D, D, D // 128),
                   (w2all, bw2all, NEXP * D, D, D // 128),
                   (iprojt, biproj, D, 2 * DI, D // 128),
                   (oprojt, boproj, DI, D, DI // 128)]
    prep_ffnj = [(w1t, bw1, D, H_FFN, a1sc, 1.0 / D),
                 (w2t, bw2, H_FFN, D, a2sc, 0.5 / H_FFN)]

    ffn_io = dict(bw1=bw1, bw2=bw2, a1c_d=a1sc, a2c_d=a2sc,
                  b1c_d=b1c, b2c_d=b2c, oc_d=oc, or_d=orow)
    moe_io = dict(w1all=bw1all, w2all=bw2all, gw_d=bgw, gbrow_d=gbrow,
                  b1c_d=eb1c, b2c_d=eb2c, g1c_d=g1c, be1c_d=be1c,
                  g2c_d=g2c, be2c_d=be2c, sel4_d=sel4, id_d=ident,
                  oc_d=oc, or_d=orow)

    s = Sched(nc)
    with s.es:
        _mk_sems(s)
        B = s.barrier
        emit_prep(s, dict(jobs=prep_jobs, tjobs=prep_tjobs, ffnj=prep_ffnj,
                          oc_d=oc))
        B()
        emit_ffn0(s, dict(x_in=x0, y_out=xd[0], **ffn_io))
        B()
        emit_attn0a(s, dict(x_in=xd[0], x1_out=x1g_d, a_out=av_d, g_out=gv_d,
                            brow_d=browb, lam_d=lamc, oc_d=oc, or_d=orow,
                            wts=bwts_a, exg_out=exg_l))
        B()
        _cc(s, "AllGather", exg_l[:, :], exg_all)
        B()
        emit_attn0b(s, dict(x_in=xd[0], x1_in=x1g_d, a_in=av_d, g_in=gv_d,
                            exg_in=exg_all, msk_d=msk_in,
                            wp3t=bwp3, brow_d=brow_p3, y_out=xd[1]))
        B()
        emit_ffn0(s, dict(x_in=xd[1], y_out=xd[2], **ffn_io))
        B()
        emit_moe(s, dict(x_in=xd[2], y_out=xd[3], **moe_io))
        B()
        emit_mamba_a(s, dict(x_in=xd[3], wt=biproj, oc_d=oc, or_d=orow,
                             xzp=xzp))
        B()
        _cc(s, "AllToAll", xzp, xzc2)
        B()
        xzc3 = xzc2.rearrange("j (r t) -> r j t", r=2 * CH)
        emit_mamba_b(s, dict(x_in=xzc3[0:CH], wc_d=wc, cb_d=cb, y_out=xccp))
        B()
        _cc(s, "AllToAll", xccp, xct2)
        B()
        emit_mamba_c(s, dict(xc_in=xct2.rearrange("j (c t) -> (j c) t", c=CH),
                             xpt_d=bxproj, dpt_d=bdproj, dtb_d=dtbc,
                             b_out=bm_d, c_out=cm_d, d_out=delta_d))
        B()
        _cc(s, "AllGather", bm_d[:, :], bmg2)
        _cc(s, "AllGather", cm_d[:, :], cmg2)
        _cc(s, "AllToAll",
            delta_d.rearrange("(j c) t -> j c t", j=NCORES)
                   .rearrange("j c t -> j (c t)"), dlc2)
        B()
        emit_mamba_d(s, dict(
            dl_in=dlc2.rearrange("j (c t) -> c j t", c=CH),
            xc_in=xccp.rearrange("j (c t) -> c j t", c=CH),
            z_in=xzc3[CH:2 * CH],
            b_in=bmg2.rearrange("j (n t) -> n j t", n=NST),
            c_in=cmg2.rearrange("j (n t) -> n j t", n=NST),
            a_d=acol, dc_d=dcol, sel_d=sel16, y_out=ycp))
        B()
        _cc(s, "AllToAll", ycp, yt2)
        B()
        emit_mamba_e(s, dict(x_in=xd[3],
                             y_in=yt2.rearrange("j (c t) -> (j c) t", c=CH),
                             wt=boproj, y_out=xd[4]))
        B()
        emit_moe(s, dict(x_in=xd[4], y_out=y_fin, **moe_io))
        s.wait("gpsimd", "qst", s.n("qst"))
        s.finalize()
    return nc


def _fused_in_maps(inputs, st):
    CH = DI // NCORES
    shards = _shards_fm(st["x_fm"])
    if "moe_w1all" not in st:
        W1 = np.asarray(inputs["L1_ffn_W1"]); W2 = np.asarray(inputs["L1_ffn_W2"])
        st["moe_w1all"] = np.ascontiguousarray(W1.reshape(NEXP * D, D))
        st["moe_w2all"] = np.ascontiguousarray(W2.reshape(NEXP * D, D))
        st["moe_gbrow"] = np.ascontiguousarray(
            np.asarray(inputs["L1_ffn_gate_b"])[None, :])
        for nm, key in (("g1", "moe_g1c"), ("be1", "moe_be1c"),
                        ("g2", "moe_g2c"), ("be2", "moe_be2c"),
                        ("b1", "moe_b1c"), ("b2", "moe_b2c")):
            v = np.asarray(inputs["L1_ffn_" + nm])
            st[key] = np.ascontiguousarray(
                np.concatenate([_cols(v[e]) for e in range(NEXP)], axis=1))
        st["moe_sel4"] = np.ascontiguousarray(
            st["sel4"].transpose(1, 0, 2).reshape(NEXP, NEXP * 128))
        st["moe_gw"] = np.ascontiguousarray(np.asarray(inputs["L1_ffn_gate_W"]))
    if "mamba_wc" not in st:
        cw = np.asarray(inputs["L1_attn_conv_w"])[:, 0, :]
        cbv = np.asarray(inputs["L1_attn_conv_b"])
        A_log = np.asarray(inputs["L1_attn_A_log"])
        Dv = np.asarray(inputs["L1_attn_D"])
        st["mamba_wc"] = []; st["mamba_cb"] = []
        st["mamba_acol_raw"] = []; st["mamba_dcol"] = []
        for c in range(NCORES):
            wce = cw[c * CH:(c + 1) * CH]
            st["mamba_wc"].append(np.ascontiguousarray(
                wce.reshape(2, 128, DCONV).transpose(1, 0, 2).reshape(128, -1)))
            st["mamba_cb"].append(np.ascontiguousarray(
                cbv[c * CH:(c + 1) * CH].reshape(2, 128).T))
            al = A_log[c * CH:(c + 1) * CH]
            st["mamba_acol_raw"].append(np.ascontiguousarray(
                al.reshape(2, 128, NST).transpose(1, 0, 2).reshape(128, -1)))
            st["mamba_dcol"].append(np.ascontiguousarray(
                Dv[c * CH:(c + 1) * CH].reshape(2, 128).T))
        st["mamba_sel16"] = np.ascontiguousarray(
            st["sel16"].transpose(1, 0, 2).reshape(NST, NST * 128)
            .astype(ml_dtypes.bfloat16))
    common = dict(
        w1t=st.setdefault("ffn_w1t", _t(inputs["L0_ffn_w1"])),
        w2t=st.setdefault("ffn_w2t", _t(inputs["L0_ffn_w2"])),
        b1c=st.setdefault("ffn_b1c", _cols(inputs["L0_ffn_b1"])),
        b2c=st.setdefault("ffn_b2c", _cols(inputs["L0_ffn_b2"])),
        ones_col=st["ones_col"], ones_row=st["ones_row"],
        wp1t=st.setdefault("wp1t", _t(inputs["L0_attn_Wp1"])),
        wp2t=st.setdefault("wp2t", _t(inputs["L0_attn_Wp2"])),
        cvt=st.setdefault("cvt", _t(inputs["L0_attn_conv_w"])),
        wat=st.setdefault("wat", _t(inputs["L0_attn_Wa"])),
        wxt=st.setdefault("wxt", _t(inputs["L0_attn_Wx"])),
        brow_a=st.setdefault("attn_brow", np.ascontiguousarray(np.concatenate(
            [np.asarray(inputs["L0_attn_bp1"]), np.asarray(inputs["L0_attn_bp2"]),
             np.asarray(inputs["L0_attn_conv_b"]), np.asarray(inputs["L0_attn_ba"]),
             np.asarray(inputs["L0_attn_bx"])])[None, :])),
        lamc=st.setdefault("attn_lamc", _cols(inputs["L0_attn_Lam"])),
        wp3t=st.setdefault("wp3t", _t(inputs["L0_attn_Wp3"])),
        brow_p3=st.setdefault("attn_bp3r", np.ascontiguousarray(
            np.asarray(inputs["L0_attn_bp3"])[None, :])),
        w1all=st["moe_w1all"], w2all=st["moe_w2all"], gw=st["moe_gw"],
        gbrow=st["moe_gbrow"], eb1c=st["moe_b1c"], eb2c=st["moe_b2c"],
        g1c=st["moe_g1c"], be1c=st["moe_be1c"], g2c=st["moe_g2c"],
        be2c=st["moe_be2c"], sel4=st["moe_sel4"], ident=st["ident"],
        iprojt=st.setdefault("iprojt", _t(inputs["L1_attn_in_proj_w"])),
        xprojt=st.setdefault("xprojt", _t(inputs["L1_attn_x_proj_w"])),
        dprojt=st.setdefault("dprojt", _t(inputs["L1_attn_dt_proj_w"])),
        dtbc=st.setdefault("dtbc", _cols(inputs["L1_attn_dt_proj_b"])),
        sel16=st["mamba_sel16"],
        oprojt=st.setdefault("oprojt", _t(inputs["L1_attn_out_proj_w"])),
    )
    if "msk" not in st:
        mk = []
        for c in range(NCORES):
            m = np.zeros((128, NCORES), np.float32)
            m[:, c] = 1.0
            mk.append(m)
        st["msk"] = mk
    return [dict(x=shards[c], wc=st["mamba_wc"][c], cb=st["mamba_cb"][c],
                 acol=st["mamba_acol_raw"][c], dcol=st["mamba_dcol"][c],
                 msk=st["msk"][c],
                 **common) for c in range(NCORES)]


def kernel(**inputs):
    st = _prep_common(inputs)
    nc = _prog("fused", build_fused)
    in_maps = _fused_in_maps(inputs, st)
    res = _run(nc, in_maps)
    x = _gather_tok(res, "y")
    return np.ascontiguousarray(x.T)[None, :, :]


def kernel_unfused(**inputs):
    st = _prep_common(inputs)
    x = st["x_fm"]
    x = run_ffn0(x, inputs, st)
    x = run_attn0(x, inputs, st)
    x = run_ffn0(x, inputs, st)
    x = run_moe(x, inputs, st)
    x = run_mamba(x, inputs, st)
    x = run_moe(x, inputs, st)
    return np.ascontiguousarray(x.T)[None, :, :]


def _prep_common(inputs):
    st = {}
    x = np.asarray(inputs["L0_attn_x"])
    st["x_fm"] = np.ascontiguousarray(x[0].T)
    st["ones_col"] = np.ones((128, 1), np.float32)
    st["ones_row"] = np.ones((1, 128), np.float32)
    st["ident"] = np.eye(128, dtype=np.float32)
    sel4 = np.zeros((NEXP, 4, 128), np.float32)
    for e in range(NEXP):
        sel4[e, e, :] = 1.0
    st["sel4"] = sel4
    sel16 = np.zeros((NST, 16, 128), np.float32)
    for n in range(NST):
        sel16[n, n, :] = 1.0
    st["sel16"] = sel16
    return st


def _cols(v):
    v = np.asarray(v)
    return np.ascontiguousarray(v.reshape(-1, 128).T)


def _t(w):
    return np.ascontiguousarray(np.asarray(w).T)


def _shards_fm(a_fm):
    return [np.ascontiguousarray(a_fm[:, c * TPC:(c + 1) * TPC])
            for c in range(NCORES)]


_FFN_CACHE = {}


def run_ffn0(x_fm, inputs, st):
    if "nc" not in _FFN_CACHE:
        _FFN_CACHE["nc"] = build_ffn0()
    nc = _FFN_CACHE["nc"]
    w1t = st.setdefault("ffn_w1t", _t(inputs["L0_ffn_w1"]))
    w2t = st.setdefault("ffn_w2t", _t(inputs["L0_ffn_w2"]))
    b1c = st.setdefault("ffn_b1c", _cols(inputs["L0_ffn_b1"]))
    b2c = st.setdefault("ffn_b2c", _cols(inputs["L0_ffn_b2"]))
    shards = _shards_fm(x_fm)
    in_maps = [{"x": shards[c], "w1t": w1t, "w2t": w2t, "b1c": b1c, "b2c": b2c,
                "ones_col": st["ones_col"], "ones_row": st["ones_row"]}
               for c in range(NCORES)]
    res = run_bass_kernel_spmd(nc, in_maps, core_ids=list(range(NCORES)))
    return np.concatenate([res.results[c]["y"] for c in range(NCORES)], axis=1)


_CACHE = {}


def _prog(name, builder):
    if name not in _CACHE:
        _CACHE[name] = builder()
    return _CACHE[name]


def _chan_shards(a_fm, nrows):
    return [np.ascontiguousarray(a_fm[c * nrows:(c + 1) * nrows, :])
            for c in range(NCORES)]


def _gather_tok(res, key):
    return np.concatenate([res.results[c][key] for c in range(NCORES)], axis=1)


def _gather_chan(res, key):
    return np.concatenate([res.results[c][key] for c in range(NCORES)], axis=0)


def _run(nc, in_maps):
    return run_bass_kernel_spmd(nc, in_maps, core_ids=list(range(NCORES)))


def run_attn0(x_fm, inputs, st):
    nc_a = _prog("attn0a", build_attn0a)
    wts = {"wp1t": st.setdefault("wp1t", _t(inputs["L0_attn_Wp1"])),
           "wp2t": st.setdefault("wp2t", _t(inputs["L0_attn_Wp2"])),
           "cvt": st.setdefault("cvt", _t(inputs["L0_attn_conv_w"])),
           "wat": st.setdefault("wat", _t(inputs["L0_attn_Wa"])),
           "wxt": st.setdefault("wxt", _t(inputs["L0_attn_Wx"]))}
    brow = st.setdefault("attn_brow", np.ascontiguousarray(np.concatenate(
        [np.asarray(inputs["L0_attn_bp1"]), np.asarray(inputs["L0_attn_bp2"]),
         np.asarray(inputs["L0_attn_conv_b"]), np.asarray(inputs["L0_attn_ba"]),
         np.asarray(inputs["L0_attn_bx"])])[None, :]))
    lamc = st.setdefault("attn_lamc", _cols(inputs["L0_attn_Lam"]))
    shards = _shards_fm(x_fm)
    in_maps = [dict(x=shards[c], brow=brow, lamc=lamc,
                    ones_col=st["ones_col"], ones_row=st["ones_row"], **wts)
               for c in range(NCORES)]
    res = _run(nc_a, in_maps)
    x1g = _gather_tok(res, "x1g")
    av = _gather_tok(res, "av")
    gv = _gather_tok(res, "gv")

    nc_s = _prog("scan0", build_scan0)
    a_sh = _chan_shards(av, 128)
    g_sh = _chan_shards(gv, 128)
    res = _run(nc_s, [{"a": a_sh[c], "g": g_sh[c]} for c in range(NCORES)])
    hv = _gather_chan(res, "h")

    nc_b = _prog("attn0b", build_attn0b)
    wp3t = st.setdefault("wp3t", _t(inputs["L0_attn_Wp3"]))
    bp3r = st.setdefault("attn_bp3r",
                         np.ascontiguousarray(np.asarray(inputs["L0_attn_bp3"])[None, :]))
    x1_sh = _shards_fm(x1g)
    h_sh = _shards_fm(hv)
    res = _run(nc_b, [dict(x=shards[c], x1g=x1_sh[c], hv=h_sh[c], wp3t=wp3t,
                           brow=bp3r) for c in range(NCORES)])
    return _gather_tok(res, "y")


def run_moe(x_fm, inputs, st):
    nc = _prog("moe", build_moe)
    if "moe_w1all" not in st:
        W1 = np.asarray(inputs["L1_ffn_W1"])
        W2 = np.asarray(inputs["L1_ffn_W2"])
        st["moe_w1all"] = np.ascontiguousarray(W1.reshape(NEXP * D, D))
        st["moe_w2all"] = np.ascontiguousarray(W2.reshape(NEXP * D, D))
        st["moe_gbrow"] = np.ascontiguousarray(
            np.asarray(inputs["L1_ffn_gate_b"])[None, :])
        for nm, key in (("g1", "moe_g1c"), ("be1", "moe_be1c"),
                        ("g2", "moe_g2c"), ("be2", "moe_be2c"),
                        ("b1", "moe_b1c"), ("b2", "moe_b2c")):
            v = np.asarray(inputs["L1_ffn_" + nm])
            st[key] = np.ascontiguousarray(
                np.concatenate([_cols(v[e]) for e in range(NEXP)], axis=1))
        st["moe_sel4"] = np.ascontiguousarray(
            st["sel4"].transpose(1, 0, 2).reshape(NEXP, NEXP * 128))
        st["moe_gw"] = np.ascontiguousarray(np.asarray(inputs["L1_ffn_gate_W"]))
    shards = _shards_fm(x_fm)
    in_maps = [dict(x=shards[c], w1all=st["moe_w1all"], w2all=st["moe_w2all"],
                    gw=st["moe_gw"], gbrow=st["moe_gbrow"],
                    eb1c=st["moe_b1c"], eb2c=st["moe_b2c"],
                    g1c=st["moe_g1c"], be1c=st["moe_be1c"],
                    g2c=st["moe_g2c"], be2c=st["moe_be2c"],
                    sel4=st["moe_sel4"], ident=st["ident"],
                    ones_col=st["ones_col"], ones_row=st["ones_row"])
               for c in range(NCORES)]
    return _gather_tok(_run(nc, in_maps), "y")


def run_mamba(x_fm, inputs, st):
    CH = DI // NCORES
    nc_a = _prog("mamba_a", build_mamba_a)
    iprojt = st.setdefault("iprojt", _t(inputs["L1_attn_in_proj_w"]))
    shards = _shards_fm(x_fm)
    res = _run(nc_a, [dict(x=shards[c], iprojt=iprojt, ones_col=st["ones_col"],
                           ones_row=st["ones_row"]) for c in range(NCORES)])
    xz = _gather_tok(res, "xz")
    x_half = xz[:DI]
    z_half = xz[DI:]
    nc_b = _prog("mamba_b", build_mamba_b)
    if "mamba_wc" not in st:
        cw = np.asarray(inputs["L1_attn_conv_w"])[:, 0, :]
        wc_sh = []
        cb_sh = []
        cb = np.asarray(inputs["L1_attn_conv_b"])
        for c in range(NCORES):
            wce = cw[c * CH:(c + 1) * CH]
            wc_sh.append(np.ascontiguousarray(
                wce.reshape(2, 128, DCONV).transpose(1, 0, 2).reshape(128, -1)))
            cbe = cb[c * CH:(c + 1) * CH]
            cb_sh.append(np.ascontiguousarray(cbe.reshape(2, 128).T))
        st["mamba_wc"] = wc_sh
        st["mamba_cb"] = cb_sh
    xs = _chan_shards(x_half, CH)
    res = _run(nc_b, [dict(x=xs[c], wc=st["mamba_wc"][c], cb=st["mamba_cb"][c])
                      for c in range(NCORES)])
    xc_ch = _gather_chan(res, "y")
    nc_c = _prog("mamba_c", build_mamba_c)
    xprojt = st.setdefault("xprojt", _t(inputs["L1_attn_x_proj_w"]))
    dprojt = st.setdefault("dprojt", _t(inputs["L1_attn_dt_proj_w"]))
    dtbc = st.setdefault("dtbc", _cols(inputs["L1_attn_dt_proj_b"]))
    xc_tok = _shards_fm(xc_ch)
    res = _run(nc_c, [dict(xc=xc_tok[c], xprojt=xprojt, dprojt=dprojt, dtbc=dtbc)
                      for c in range(NCORES)])
    bm = _gather_tok(res, "bm")
    cm = _gather_tok(res, "cm")
    delta = _gather_tok(res, "delta")
    nc_d = _prog("mamba_d", build_mamba_d)
    if "mamba_acol" not in st:
        A_log = np.asarray(inputs["L1_attn_A_log"])
        Dv = np.asarray(inputs["L1_attn_D"])
        ac_sh, dc_sh = [], []
        for c in range(NCORES):
            al = A_log[c * CH:(c + 1) * CH]
            ac_sh.append(np.ascontiguousarray(
                al.reshape(2, 128, NST).transpose(1, 0, 2).reshape(128, -1)))
            dc_sh.append(np.ascontiguousarray(
                Dv[c * CH:(c + 1) * CH].reshape(2, 128).T))
        st["mamba_acol_raw"] = ac_sh
        st["mamba_dcol"] = dc_sh
        st["mamba_sel16"] = np.ascontiguousarray(
            st["sel16"].transpose(1, 0, 2).reshape(NST, NST * 128))
    dl_sh = _chan_shards(delta, CH)
    z_sh = _chan_shards(z_half, CH)
    xc_sh = _chan_shards(xc_ch, CH)
    res = _run(nc_d, [dict(delta=dl_sh[c], xc=xc_sh[c], z=z_sh[c], bm=bm, cm=cm,
                           acol=st["mamba_acol_raw"][c], dcol=st["mamba_dcol"][c],
                           sel16=st["mamba_sel16"]) for c in range(NCORES)])
    y_ch = _gather_chan(res, "y")
    nc_e = _prog("mamba_e", build_mamba_e)
    oprojt = st.setdefault("oprojt", _t(inputs["L1_attn_out_proj_w"]))
    y_tok = _shards_fm(y_ch)
    res = _run(nc_e, [dict(x=shards[c], ym=y_tok[c], oprojt=oprojt)
                      for c in range(NCORES)])
    return _gather_tok(res, "y")



